# version 01
# baseline (speedup 1.0000x reference)
import math
from contextlib import ExitStack

import numpy as np
import ml_dtypes

import concourse.bass as bass
import concourse.mybir as mybir
from concourse.bass_utils import run_bass_kernel_spmd
from concourse.aot_env import allow_ici_all_to_all

F32 = mybir.dt.float32
BF16 = mybir.dt.bfloat16
AF = mybir.ActivationFunctionType
ALU = mybir.AluOpType
AXX = mybir.AxisListType.X

D = 4096
NT = 1024
NP = 8
SCLQ = 1.0 / math.sqrt(128.0)
EPS = 1e-5
_TAPS = False

ENGS = ("sync", "gpsimd", "scalar", "vector", "tensor")


def _rho(B):
    return B if B < 4 else 7 - B


def _co(B):
    return 0 if B < 4 else 512


class Sched:
    def __init__(self):
        self.ops = {e: [] for e in ENGS}
        self.cnt = {}
        self._pend = {e: [] for e in ENGS}

    def tok(self, ch):
        return self.cnt.get(ch, 0)

    def emit(self, eng, fn, waits=(), inc=None, amt=1, fence=False):
        if fence:
            self.fence(eng)
        waits = list(waits) + self._pend[eng]
        self._pend[eng] = []
        if inc is not None:
            self.cnt[inc] = self.cnt.get(inc, 0) + amt
        self.ops[eng].append((tuple((c, v) for c, v in waits if v), fn, inc, amt))
        return self.cnt.get(inc, 0)

    def fence(self, eng):
        if not self.ops[eng]:
            return
        w, fn, inc, amt = self.ops[eng][-1]
        if fn is None:
            return
        if inc is None:
            ch = "sf_" + eng
            self.cnt[ch] = self.cnt.get(ch, 0) + 1
            self.ops[eng][-1] = (w, fn, ch, 1)
            self._pend[eng].append((ch, self.cnt[ch]))
        else:
            self._pend[eng].append((inc, self.cnt[inc]))

    def wait_only(self, eng, waits):
        self.ops[eng].append((tuple(waits), None, None, 0))

    def replay(self, block, sems):
        def mk(ename):
            def body(eng):
                seen = {}
                for waits, fn, inc, amt in self.ops[ename]:
                    for ch, val in waits:
                        if val > seen.get(ch, 0):
                            eng.wait_ge(sems[ch], val)
                            seen[ch] = val
                    if fn is not None:
                        ins = fn(eng)
                        if inc is not None:
                            ins.then_inc(sems[inc], amt)
            return body

        block.sync(mk("sync"))
        block.gpsimd(mk("gpsimd"))
        block.scalar(mk("scalar"))
        block.vector(mk("vector"))
        block.tensor(mk("tensor"))


def build_nc(taps=False):
    nc = bass.Bass("TRN2", target_bir_lowering=False)

    xs = nc.dram_tensor("xs", [NT, D], F32, kind="ExternalInput").ap()
    wqT = nc.dram_tensor("wqT", [D, D], F32, kind="ExternalInput").ap()
    wkT = nc.dram_tensor("wkT", [D, D], F32, kind="ExternalInput").ap()
    wvT = nc.dram_tensor("wvT", [D, D], F32, kind="ExternalInput").ap()
    woT = nc.dram_tensor("woT", [D, D], F32, kind="ExternalInput").ap()
    bqc = nc.dram_tensor("bqc", [128, 32], F32, kind="ExternalInput").ap()
    bkc = nc.dram_tensor("bkc", [128, 32], F32, kind="ExternalInput").ap()
    bvr = nc.dram_tensor("bvr", [1, D], F32, kind="ExternalInput").ap()
    bor = nc.dram_tensor("bor", [1, D], F32, kind="ExternalInput").ap()
    rhsw = nc.dram_tensor("rhsw", [128, 32 * 68], F32, kind="ExternalInput").ap()
    bTm = nc.dram_tensor("bTm", [64, D], F32, kind="ExternalInput").ap()
    out = nc.dram_tensor("out", [NT, D], F32, kind="ExternalOutput").ap()

    ident = nc.inline_tensor(np.eye(128, dtype=ml_dtypes.bfloat16), "ident")
    ones_col = nc.inline_tensor(np.ones((128, 1), dtype=ml_dtypes.bfloat16), "ones_col")
    ones_row = nc.inline_tensor(np.ones((1, 128), dtype=ml_dtypes.bfloat16), "ones_row")
    ones_row_f = nc.inline_tensor(np.ones((1, 128), dtype=np.float32), "ones_row_f")
    mknp = np.zeros((128, 4 * 512), dtype=ml_dtypes.bfloat16)
    pp = np.arange(128)[:, None]
    jj_ = np.arange(512)[None, :]
    for u in range(4):
        mknp[:, 512 * u:512 * (u + 1)] = ((128 * u + pp) <= jj_).astype(ml_dtypes.bfloat16)
    maskt = nc.inline_tensor(mknp, "maskt")

    k_send = nc.dram_tensor("k_send", [8 * 512, NT], BF16, kind="Internal")
    q_send = nc.dram_tensor("q_send", [8 * 512, NT], BF16, kind="Internal")
    v_send = nc.dram_tensor("v_send", [8 * 1024, 512], BF16, kind="Internal")
    kgD = nc.dram_tensor("kgD", [8 * 512, NT], BF16, kind="Internal")
    qgD = nc.dram_tensor("qgD", [8 * 512, NT], BF16, kind="Internal")
    vgD = nc.dram_tensor("vgD", [8 * 1024, 512], BF16, kind="Internal")
    ao_send = nc.dram_tensor("ao_send", [8 * 512, NT], BF16, kind="Internal")
    aogD = nc.dram_tensor("aogD", [8 * 512, NT], BF16, kind="Internal")
    x1d = nc.dram_tensor("x1d", [NT, D], F32, kind="Internal")
    tap = {}
    if taps:
        for n, sh, dt in [("ksend", (8 * 512, NT), BF16), ("qsend", (8 * 512, NT), BF16),
                          ("vsend", (8 * 1024, 512), BF16), ("aosend", (8 * 512, NT), BF16),
                          ("aog", (8 * 512, NT), BF16), ("x1", (NT, D), F32),
                          ("kg", (8 * 512, NT), BF16), ("qg", (8 * 512, NT), BF16),
                          ("vg", (8 * 1024, 512), BF16)]:
            tap[n] = nc.dram_tensor("tap_" + n, list(sh), dt, kind="ExternalOutput")

    S = Sched()
    E = S.emit
    T = S.tok

    with ExitStack() as ctx:
        SB = lambda n, sh, dt: ctx.enter_context(nc.sbuf_tensor(n, sh, dt))
        big = SB("big", [128, 32 * NT], BF16)
        wb = [SB(f"wb{i}", [128, 8192], BF16) for i in range(2)]
        xa = [SB(f"xa{i}", [128, D], F32) for i in range(2)]
        xn = [SB(f"xn{i}", [128, D], BF16) for i in range(2)]
        dump = SB("dump", [128, D], BF16)
        kst = [SB(f"kst{i}", [128, 512], BF16) for i in range(4)]
        k2t = [SB(f"k2t{i}", [128, 512], BF16) for i in range(2)]
        attT = [SB(f"attT{i}", [128, 512], BF16) for i in range(4)]
        rbc = [SB(f"rbc{i}", [128, 512], F32) for i in range(2)]
        aost = [SB(f"aost{i}", [128, 512], BF16) for i in range(3)]
        xres = [SB(f"xres{i}", [128, 512], F32) for i in range(2)]
        x1st = [SB(f"x1st{i}", [128, 512], F32) for i in range(2)]
        ost = [SB(f"ost{i}", [128, 512], F32) for i in range(2)]
        sml = SB("sml", [128, 128], BF16)
        mlt = SB("mlt", [128, 68], BF16)
        mg = SB("mg", [128, 64], BF16)
        mgT = SB("mgT", [64, 128], BF16)
        gl = SB("gl", [128, 28], F32)
        s_id = SB("s_id", [128, 128], BF16)
        s_msk = SB("s_msk", [128, 4 * 512], BF16)
        s_oc = SB("s_oc", [128, 1], BF16)
        s_or = SB("s_or", [1, 128], BF16)
        s_orf = SB("s_orf", [1, 128], F32)
        s_bq = SB("s_bq", [128, 32], F32)
        s_bqs = SB("s_bqs", [128, 32], F32)
        s_bk = SB("s_bk", [128, 32], F32)
        rhswb = SB("rhswb", [128, 32 * 68], BF16)
        urow = SB("urow", [128, 2048], F32)
        mqct = SB("mqct", [128, 8], BF16)
        mqt = SB("mqt", [128, 1], BF16)
        nmqt = SB("nmqt", [1, 1024], BF16)
        stat = SB("stat", [128, 16], F32)
        epsc = SB("epsc", [128, 1], F32)
        pall = ctx.enter_context(nc.psum_tensor("pall", [128, 4096], F32))
        sems = {}

        def PB(b):
            return pall[:, 512 * b:512 * (b + 1)]

        def PBb(b):
            return pall[:, 512 * b:512 * (b + 1)].bitcast(BF16)

        bvb = dump[0:1, :]
        bob = dump[0:1, :]
        r_t = urow[0:1, 0:512]
        r_rd = [urow[0:1, 512:1024], urow[0:1, 1024:1536]]
        r_ksq = urow[0:1, 1536:1568]
        r_qsq = urow[0:1, 1568:1600]
        r_m2 = urow[0:1, 1600:1608]
        r_mro = urow[0:1, 1608:1616]
        r_nM = urow[0:1, 1616:1624]
        r_mx1 = urow[0:1, 1624:1632]

        for dst, src in [(s_id[:], ident.ap()), (s_msk[:], maskt.ap()),
                         (s_oc[:], ones_col.ap()), (s_or[:], ones_row.ap()),
                         (s_orf[:], ones_row_f.ap()), (s_bq[:], bqc), (s_bk[:], bkc)]:
            E("sync", lambda e, d=dst, s=src: e.dma_start(out=d, in_=s), inc="cst", amt=16)
        for dst, src in [(rhswb[:], rhsw)]:
            E("gpsimd", lambda e, d=dst, s=src: e.dma_start(out=d, in_=s), inc="cst", amt=16)
        cst_all = T("cst")
        E("vector", lambda e: e.tensor_scalar(out=s_bqs[:], in0=s_bq[:], scalar1=SCLQ,
                                              scalar2=None, op0=ALU.mult),
          waits=[("cst", cst_all)], inc="prep")
        E("vector", lambda e: e.memset(urow[0:1, 1536:1600], -3.0e38), inc="prep")
        prep_all = E("vector", lambda e: e.memset(epsc[:], EPS), inc="prep")

        def ln_tile(xa_t, xn_t, par, wfirst):
            o = 8 * par
            c_s1, c_s2, c_mean, c_var, c_ms, c_rstd, c_nmr, c_m2 = [
                stat[:, o + i:o + i + 1] for i in range(8)]
            E("vector", lambda e: e.tensor_scalar(out=dump[:], in0=xa_t[:], scalar1=1.0,
                                                  scalar2=0.0, op0=ALU.mult, op1=ALU.add,
                                                  accum_out=c_s1),
              waits=wfirst)
            E("vector", lambda e: e.scalar_tensor_tensor(out=dump[:], in0=xa_t[:], scalar=1.0,
                                                         in1=xa_t[:], op0=ALU.mult, op1=ALU.mult,
                                                         accum_out=c_s2))
            E("vector", lambda e: e.tensor_scalar(out=c_mean, in0=c_s1, scalar1=1.0 / D,
                                                  scalar2=None, op0=ALU.mult), fence=True)
            E("vector", lambda e: e.scalar_tensor_tensor(out=c_var, in0=c_s1, scalar=c_mean,
                                                         in1=c_s2, op0=ALU.mult,
                                                         op1=ALU.subtract), fence=True)
            v_var = E("vector", lambda e: e.tensor_scalar(out=c_var, in0=c_var,
                                                          scalar1=-1.0 / D, scalar2=None,
                                                          op0=ALU.mult), inc="lnv", fence=True)
            E("scalar", lambda e: e.activation(out=c_rstd, in_=c_var, func=AF.Ln,
                                               bias=epsc[:], scale=1.0),
              waits=[("lnv", v_var), ("prep", prep_all)])
            a_r = E("scalar", lambda e: e.activation(out=c_rstd, in_=c_rstd, func=AF.Exp,
                                                     bias=0.0, scale=-0.5), inc="lnr",
                    fence=True)
            v_n = E("vector", lambda e: e.tensor_scalar(out=c_nmr, in0=c_mean, scalar1=c_rstd,
                                                        scalar2=-1.0, op0=ALU.mult, op1=ALU.mult),
                    waits=[("lnr", a_r)], inc="lnn")
            a_x = E("scalar", lambda e: e.activation(out=xn_t[:], in_=xa_t[:], func=AF.Identity,
                                                     bias=c_nmr, scale=c_rstd),
                    waits=[("lnn", v_n)], inc="lnx")
            return a_x

        tev_bank = {0: 0, 1: 0}
        lnx_tok = {}
        for t8 in range(8):
            wx = [("lnx", lnx_tok[t8 % 2])] if t8 >= 2 else []
            xl = E("sync", lambda e, t=t8: e.dma_start(out=xa[t % 2][:],
                                                       in_=xs[128 * t:128 * (t + 1), :]),
                   waits=wx, inc="xld", amt=16)
            a_x = ln_tile(xa[t8 % 2], xn[t8 % 2], t8 % 2,
                          [("xld", xl), ("cst", cst_all)])
            lnx_tok[t8 % 2] = a_x
            for g in range(4):
                bank = (t8 * 4 + g) % 2
                w = [("lnx", a_x)]
                if tev_bank[bank]:
                    w.append(("tev", tev_bank[bank]))
                last = None
                for j2 in range(8):
                    j = 8 * g + j2
                    last = E("tensor", lambda e, b=bank, j2=j2, j=j, t=t8: e.transpose(
                        out=PBb(b)[:, 128 * j2:128 * (j2 + 1)],
                        in_=xn[t % 2][:, 128 * j:128 * (j + 1)], identity=s_id[:]),
                        waits=(w if j2 == 0 else []), inc=("ttr" if j2 == 7 else None))
                dst = big[:].rearrange("p (j c) -> p j c", c=NT)[:, 8 * g:8 * (g + 1),
                                                                128 * t8:128 * (t8 + 1)]
                src = PBb(bank)[:].rearrange("p (j c) -> p j c", c=128)
                tev_bank[bank] = E("vector", lambda e, d=dst, s=src: e.tensor_copy(out=d, in_=s),
                                   waits=[("ttr", last)], inc="tev")
        tev_all = T("tev")
        brl_all = E("gpsimd", lambda e: e.dma_start(out=bvb, in_=bvr),
                    waits=[("lnv", T("lnv"))], inc="brl", amt=16)

        slotg = {}

        def proj_phase(wT, pref, bias_t, scale, send, entry_guard):
            strip_free = {0: None, 1: None}
            for hg in range(8):
                ev_before = T(pref + "ev")
                wtk = {}
                for jh in range(2):
                    buf = jh
                    src = wT[2048 * jh:2048 * (jh + 1), 512 * hg:512 * (hg + 1)].rearrange(
                        "(j p) c -> p j c", p=128)
                    dstw = wb[buf][:].rearrange("p (j c) -> p j c", c=512)
                    w = [("tev", tev_all)]
                    if strip_free[buf] is not None:
                        w.append((pref + "mm", strip_free[buf]))
                    if pref == "q":
                        w.append(("kmm", T("kmm")))
                    wtk[jh] = E("gpsimd", lambda e, s=src, d=dstw: e.dma_start(out=d, in_=s),
                                waits=w, inc="wld", amt=16)
                for jh in range(2):
                    for half in range(2):
                        for j16 in range(16):
                            for hh in range(4):
                                j = 16 * jh + j16
                                st = (jh == 0 and j16 == 0)
                                sp = (jh == 1 and j16 == 15)
                                w2 = []
                                if half == 0 and j16 == 0 and hh == 0:
                                    w2.append(("wld", wtk[jh]))
                                if jh == 0 and half == 0 and j16 == 0 and hh == 0:
                                    if ev_before:
                                        w2.append((pref + "ev", ev_before))
                                    if hg == 0:
                                        w2.extend(entry_guard)
                                E("tensor",
                                  lambda e, b=jh, hh=hh, j16=j16, j=j, half=half, st=st, sp=sp:
                                  e.matmul(out=pall[:, 1024 * hh + 512 * half:
                                                    1024 * hh + 512 * half + 512],
                                           lhsT=wb[b][:, 512 * j16 + 128 * hh:
                                                      512 * j16 + 128 * (hh + 1)],
                                           rhs=big[:, NT * j + 512 * half:NT * j + 512 * half + 512],
                                           start=st, stop=sp),
                                  waits=w2,
                                  inc=(pref + "mm") if (j16 == 15 and half == 1 and hh == 3)
                                  else None)
                    strip_free[jh] = T(pref + "mm")
                mmt = T(pref + "mm")
                for half in range(2):
                    for hh in range(4):
                        h = 4 * hg + hh
                        slot = (hg * 8 + half * 4 + hh) % 4
                        wv = [(pref + "mm", mmt), ("prep", prep_all)] + slotg.get(slot, [])
                        ev = E("scalar", lambda e, hh=hh, half=half, h=h, s=slot:
                               e.activation(out=kst[s][:],
                                            in_=pall[:, 1024 * hh + 512 * half:
                                                     1024 * hh + 512 * half + 512],
                                            func=AF.Identity, bias=bias_t[:, h:h + 1],
                                            scale=scale),
                               waits=wv, inc=pref + "ev")
                        r0 = 512 * (h // 4) + 128 * (h % 4)
                        stk = E("scalar", lambda e, s=slot, r0=r0, half=half:
                                e.dma_start(out=send.ap()[r0:r0 + 128,
                                                          512 * half:512 * (half + 1)],
                                            in_=kst[s][:]),
                                waits=[(pref + "ev", ev)], inc=pref + "st", amt=16)
                        slotg[slot] = [(pref + "st", stk)]

        proj_phase(wkT, "k", s_bk, 1.0, k_send, [])

        proj_phase(wqT, "q", s_bqs, SCLQ, q_send, [("kev", T("kev"))])

        strip_free = {0: None, 1: None}
        vbank_rd = {}
        for g8 in range(8):
            wtk = {}
            for jh in range(2):
                src = wvT[2048 * jh:2048 * (jh + 1), 512 * g8:512 * (g8 + 1)].rearrange(
                    "(j p) c -> p j c", p=128)
                dstw = wb[jh][:].rearrange("p (j c) -> p j c", c=512)
                w = [("qmm", T("qmm"))]
                if strip_free[jh] is not None:
                    w.append(("vjh", strip_free[jh]))
                wtk[jh] = E("gpsimd", lambda e, s=src, d=dstw: e.dma_start(out=d, in_=s),
                            waits=w, inc="wld", amt=16)
            for jh in range(2):
                for tg in range(2):
                    for j16 in range(16):
                        for tt in range(4):
                            t8 = 4 * tg + tt
                            j = 16 * jh + j16
                            st = (jh == 0 and j16 == 0)
                            w2 = []
                            if tg == 0 and j16 == 0 and tt == 0:
                                w2.append(("wld", wtk[jh]))
                                if st:
                                    w2.append(("qev", T("qev")))
                            if st and j16 == 0 and tt == 0:
                                bt = vbank_rd.get(tg)
                                if bt:
                                    w2.append(("vev", bt))
                            E("tensor", lambda e, b=jh, t8=t8, j16=j16, j=j, st=st:
                              e.matmul(out=PB(t8),
                                       lhsT=big[:, NT * j + 128 * t8:NT * j + 128 * (t8 + 1)],
                                       rhs=wb[b][:, 512 * j16:512 * (j16 + 1)],
                                       start=st, stop=False),
                              waits=w2,
                              inc="vjh" if (tg == 1 and j16 == 15 and tt == 3) else None)
                strip_free[jh] = T("vjh")
            for tg in range(2):
                for tt in range(4):
                    t8 = 4 * tg + tt
                    bm = E("tensor", lambda e, t8=t8, g8=g8:
                           e.matmul(out=PB(t8), lhsT=s_or[:],
                                    rhs=bvb[0:1, 512 * g8:512 * (g8 + 1)],
                                    start=False, stop=True),
                           waits=[("cst", cst_all), ("brl", brl_all)], inc="vmm")
                    slot = (g8 * 8 + t8) % 4
                    wv = [("vmm", bm)] + slotg.get(slot, [])
                    ev = E("vector", lambda e, t8=t8, s=slot: e.tensor_copy(out=kst[s][:],
                                                                            in_=PB(t8)),
                           waits=wv, inc="vev")
                    if tt == 3:
                        vbank_rd[tg] = ev
                    stk = E("scalar", lambda e, s=slot, g8=g8, t8=t8:
                            e.dma_start(out=v_send.ap()[1024 * g8 + 128 * t8:
                                                        1024 * g8 + 128 * (t8 + 1), :],
                                        in_=kst[s][:]),
                            waits=[("vev", ev)], inc="vst", amt=16)
                    slotg[slot] = [("vst", stk)]
        brl_bo = E("gpsimd", lambda e: e.dma_start(out=bob, in_=bor),
                   waits=[("vmm", T("vmm"))], inc="brl", amt=16)

        cck = E("gpsimd", lambda e: e.collective_compute(
            "AllToAll", ALU.bypass, replica_groups=[list(range(8))],
            ins=[k_send.ap().opt()], outs=[kgD.ap().opt()]),
            waits=[("kst", T("kst")), ("qst", T("qst")), ("vst", T("vst")),
                   ("qmm", T("qmm")), ("vmm", T("vmm")), ("xld", T("xld")),
                   ("cst", cst_all), ("brl", T("brl")), ("wld", T("wld"))], inc="cck")
        ccq = E("gpsimd", lambda e: e.collective_compute(
            "AllToAll", ALU.bypass, replica_groups=[list(range(8))],
            ins=[q_send.ap().opt()], outs=[qgD.ap().opt()]),
            waits=[("cck", cck)], inc="ccq")
        ccv = E("gpsimd", lambda e: e.collective_compute(
            "AllToAll", ALU.bypass, replica_groups=[list(range(8))],
            ins=[v_send.ap().opt()], outs=[vgD.ap().opt()]),
            waits=[("ccq", ccq)], inc="ccv")

        def kres(par):
            return big[:, 12288 * par:12288 * par + 4096]

        def vres(par):
            return big[:, 12288 * par + 4096:12288 * par + 8192]

        def qres(par):
            return big[:, 12288 * par + 8192:12288 * par + 12288]

        mqc = [mqct[:, 0:4], mqct[:, 4:8]]
        nmq = [nmqt[0:1, 0:512], nmqt[0:1, 512:1024]]
        mxps = [pall[:, 2560:2688], pall[:, 3072:3200]]
        pden = pall[0:1, 2048:2560]
        pbc = PB(4)
        sc_tok = {0: None, 1: None}
        av_tok = {2: None, 3: None}
        attT_tok = [None] * 4
        rbc_tok = [None, None]
        aost_tok = [None] * 3
        mx_tok = [None, None]
        den_free = None
        ld_prev = {}
        sci = 0
        ui = 0
        for p in range(NP):
            b, hh = p // 4, p % 4
            par = p % 2
            w0 = [("cck", cck), ("ccq", ccq), ("ccv", ccv)]
            if ld_prev.get(par) is not None:
                w0.append(("dnm", ld_prev[par]))
            lds = 0
            for B in range(8):
                s = 4 * b + _rho(B)
                co = _co(B)
                E("sync", lambda e, s=s, co=co, hh=hh, B=B, par=par:
                  e.dma_start(out=kres(par)[:, 512 * B:512 * (B + 1)],
                              in_=kgD.ap()[512 * s + 128 * hh:512 * s + 128 * (hh + 1),
                                           co:co + 512]),
                  waits=(w0 if B == 0 else []), inc="p2ld", amt=16)
                E("sync", lambda e, s=s, co=co, hh=hh, B=B, par=par:
                  e.dma_start(out=vres(par)[:, 512 * B:512 * (B + 1)].rearrange(
                      "p (u c) -> p u c", c=128),
                      in_=vgD.ap()[1024 * s + co:1024 * s + co + 512,
                                   128 * hh:128 * (hh + 1)].rearrange("(u p) c -> p u c", p=128)),
                  inc="p2ld", amt=16)
                lds = E("sync", lambda e, s=s, co=co, hh=hh, B=B, par=par:
                        e.dma_start(out=qres(par)[:, 512 * B:512 * (B + 1)],
                                    in_=qgD.ap()[512 * s + 128 * hh:512 * s + 128 * (hh + 1),
                                                 co:co + 512]),
                        inc="p2ld", amt=16)

            def max_pass(P, qt):
                nonlocal ui
                for q4 in range(4):
                    for B in range(P + 1):
                        mi = (S.cnt.get("_mxi", 0)) % 2
                        S.cnt["_mxi"] = S.cnt.get("_mxi", 0) + 1
                        wmx = [("p2ld", lds)]
                        if mx_tok[mi] is not None:
                            wmx.append(("mxr", mx_tok[mi]))
                        mm = E("tensor", lambda e, mi=mi, P=P, q4=q4, B=B, par=par:
                               e.matmul(out=mxps[mi],
                                        lhsT=qres(par)[:, 512 * P + 128 * q4:
                                                       512 * P + 128 * (q4 + 1)],
                                        rhs=kres(par)[:, 512 * B:512 * B + 128],
                                        start=True, stop=True),
                               waits=wmx, inc="mxm")
                        rdt = E("vector", lambda e, mi=mi: e.reduce_max(
                            out=mqt[:, 0:1], in_=mxps[mi], axis=AXX),
                            waits=[("mxm", mm)], inc="mxr", fence=True)
                        if B == 0:
                            E("vector", lambda e, qt=qt, q4=q4: e.tensor_copy(
                                out=mqc[qt][:, q4:q4 + 1], in_=mqt[:, 0:1]),
                              inc="mqv", fence=True)
                        else:
                            E("vector", lambda e, qt=qt, q4=q4: e.tensor_max(
                                out=mqc[qt][:, q4:q4 + 1], in0=mqc[qt][:, q4:q4 + 1],
                                in1=mqt[:, 0:1]), inc="mqv", fence=True)
                        mx_tok[mi] = rdt
                for q4 in range(4):
                    scb = [0, 1][(S.cnt.get("_sci", 0)) % 2]
                    S.cnt["_sci"] = S.cnt.get("_sci", 0) + 1
                    wtp = [("mqv", T("mqv"))]
                    if sc_tok[scb] is not None:
                        wtp.append(sc_tok[scb])
                    tp = E("tensor", lambda e, scb=scb, qt=qt, q4=q4: e.transpose(
                        out=PBb(scb)[0:1, 0:128], in_=mqc[qt][:, q4:q4 + 1],
                        identity=s_id[:]),
                        waits=wtp, inc="mtp")
                    ng = E("vector", lambda e, scb=scb, qt=qt, q4=q4: e.tensor_scalar(
                        out=nmq[qt][0:1, 128 * q4:128 * (q4 + 1)], in0=PBb(scb)[0:1, 0:128],
                        scalar1=-1.0, scalar2=None, op0=ALU.mult),
                        waits=[("mtp", tp)], inc="nmg")
                    sc_tok[scb] = ("nmg", ng)

            def units(P, qt, nmg_tok):
                nonlocal ui, den_free
                avb = 2 + qt
                for B in range(P + 1):
                    for u in range(4):
                        first = (B == 0 and u == 0)
                        lastu = (B == P and u == 3)
                        scb = [0, 1][(S.cnt.get("_sci", 0)) % 2]
                        S.cnt["_sci"] = S.cnt.get("_sci", 0) + 1
                        ats = ui % 4
                        wsc = [("p2ld", lds), ("nmg", nmg_tok)]
                        if sc_tok[scb] is not None:
                            wsc.append(sc_tok[scb])
                        smm = E("tensor", lambda e, scb=scb, B=B, u=u, P=P, par=par:
                                e.matmul(out=PB(scb),
                                         lhsT=kres(par)[:, 512 * B + 128 * u:
                                                        512 * B + 128 * (u + 1)],
                                         rhs=qres(par)[:, 512 * P:512 * (P + 1)],
                                         start=True, stop=False),
                                waits=wsc, inc="smm")
                        smm = E("tensor", lambda e, scb=scb, qt=qt:
                                e.matmul(out=PB(scb), lhsT=s_or[:],
                                         rhs=nmq[qt][0:1, :], start=False, stop=True),
                                inc="smm")
                        wxp = [("smm", smm)]
                        if attT_tok[ats] is not None:
                            wxp.append(("dnm", attT_tok[ats]))
                        sxp = E("scalar", lambda e, scb=scb, a=ats:
                                e.activation(out=attT[a][:], in_=PB(scb), func=AF.Exp,
                                             bias=0.0, scale=1.0),
                                waits=wxp, inc="sxp")
                        sc_tok[scb] = ("sxp", sxp)
                        gate = ("sxp", sxp)
                        if B == P:
                            mku = E("vector", lambda e, a=ats, u=u:
                                    e.tensor_mul(out=attT[a][:], in0=attT[a][:],
                                                 in1=s_msk[:, 512 * u:512 * (u + 1)]),
                                    waits=[("sxp", sxp)], inc="msk")
                            gate = ("msk", mku)
                        wav = [gate]
                        if first and av_tok[avb] is not None:
                            wav.append(("aev", av_tok[avb]))
                        E("tensor", lambda e, avb=avb, B=B, u=u, par=par, a=ats,
                          first=first, lastu=lastu:
                          e.matmul(out=PB(avb),
                                   lhsT=vres(par)[:, 512 * B + 128 * u:512 * B + 128 * (u + 1)],
                                   rhs=attT[a][:], start=first, stop=lastu),
                          waits=wav, inc="avm")
                        wdn = []
                        if first and den_free is not None:
                            wdn.append(den_free)
                        dnm = E("tensor", lambda e, a=ats, first=first, lastu=lastu:
                                e.matmul(out=pden, lhsT=s_oc[:], rhs=attT[a][:],
                                         start=first, stop=lastu),
                                waits=wdn, inc="dnm")
                        attT_tok[ats] = dnm
                        ui += 1
                dl1 = E("scalar", lambda e: e.activation(
                    out=r_t[:], in_=pden, func=AF.Ln, bias=0.0, scale=1.0),
                    waits=[("dnm", T("dnm"))], inc="dnl")
                rd = E("scalar", lambda e, qt=qt: e.activation(out=r_rd[qt][:], in_=r_t[:],
                                                               func=AF.Exp, bias=0.0,
                                                               scale=-1.0),
                       inc="rdn", fence=True)
                bcm = E("tensor", lambda e: e.matmul(out=pbc, lhsT=s_orf[:],
                                                     rhs=r_rd[qt][:],
                                                     start=True, stop=True),
                        waits=[("rdn", rd), ("dnl", dl1)], inc="bcm")
                wcc = [("bcm", bcm)]
                if rbc_tok[qt] is not None:
                    wcc.append(("aev", rbc_tok[qt]))
                bcc = E("vector", lambda e, qt=qt: e.tensor_copy(out=rbc[qt][:], in_=pbc),
                        waits=wcc, inc="bcc")
                den_free = ("bcc", bcc)
                aslot = (p * 8 + P) % 3
                wev = [("bcc", bcc)]
                if aost_tok[aslot] is not None:
                    wev.append(("aos", aost_tok[aslot]))
                aev = E("vector", lambda e, avb=avb, qt=qt, s=aslot:
                        e.tensor_tensor(out=aost[s][:], in0=PB(avb), in1=rbc[qt][:],
                                        op=ALU.mult),
                        waits=wev, inc="aev")
                av_tok[avb] = aev
                rbc_tok[qt] = aev
                d = 4 * b + _rho(P)
                co = _co(P)
                aost_tok[aslot] = E("scalar", lambda e, s=aslot, d=d, co=co, hh=hh:
                                    e.dma_start(out=ao_send.ap()[
                                        512 * d + 128 * hh:512 * d + 128 * (hh + 1),
                                        co:co + 512], in_=aost[s][:]),
                                    waits=[("aev", aev)], inc="aos", amt=16)

            max_pass(0, 0)
            nmg_toks = {0: T("nmg")}
            for P in range(8):
                if P < 7:
                    max_pass(P + 1, (P + 1) % 2)
                    nmg_toks[P + 1] = T("nmg")
                units(P, P % 2, nmg_toks[P])
            ld_prev[par] = T("dnm")

        cca = E("gpsimd", lambda e: e.collective_compute(
            "AllToAll", ALU.bypass, replica_groups=[list(range(8))],
            ins=[ao_send.ap().opt()], outs=[aogD.ap().opt()]),
            waits=[("aos", T("aos")), ("p2ld", T("p2ld")), ("kst", T("kst")),
                   ("qst", T("qst")), ("vst", T("vst"))], inc="cca")

        agl = E("sync", lambda e: e.dma_start(
            out=big[:].rearrange("p (j c) -> p j c", c=NT),
            in_=aogD.ap().rearrange("(j p) c -> p j c", p=128)),
            waits=[("cca", cca), ("dnm", T("dnm")), ("smm", T("smm"))], inc="agl", amt=16)
        strip_free = {0: None, 1: None}
        wo_bank_rd = {}
        xres_free = {}
        x1_free = {}
        for g8 in range(8):
            wtk = {}
            for jh in range(2):
                src = woT[2048 * jh:2048 * (jh + 1), 512 * g8:512 * (g8 + 1)].rearrange(
                    "(j p) c -> p j c", p=128)
                dstw = wb[jh][:].rearrange("p (j c) -> p j c", c=512)
                w = [("smm", T("smm")), ("cca", cca)]
                if strip_free[jh] is not None:
                    w.append(("ojh", strip_free[jh]))
                wtk[jh] = E("gpsimd", lambda e, s=src, d=dstw: e.dma_start(out=d, in_=s),
                            waits=w, inc="wld", amt=16)
            for jh in range(2):
                for tg in range(2):
                    for j16 in range(16):
                        for tt in range(4):
                            t8 = 4 * tg + tt
                            j = 16 * jh + j16
                            st = (jh == 0 and j16 == 0)
                            w2 = []
                            if tg == 0 and j16 == 0 and tt == 0:
                                w2.append(("wld", wtk[jh]))
                                w2.append(("agl", agl))
                                if st and g8 == 0:
                                    w2.append(("aev", T("aev")))
                                    w2.append(("bcc", T("bcc")))
                                    w2.append(("dnl", T("dnl")))
                            if st and j16 == 0 and tt == 0:
                                bt = wo_bank_rd.get(tg)
                                if bt:
                                    w2.append(("x1e", bt))
                            E("tensor", lambda e, b=jh, t8=t8, j16=j16, j=j, st=st:
                              e.matmul(out=PB(t8),
                                       lhsT=big[:, NT * j + 128 * t8:NT * j + 128 * (t8 + 1)],
                                       rhs=wb[b][:, 512 * j16:512 * (j16 + 1)],
                                       start=st, stop=False),
                              waits=w2,
                              inc="ojh" if (tg == 1 and j16 == 15 and tt == 3) else None)
                strip_free[jh] = T("ojh")
            for tg in range(2):
                for tt in range(4):
                    t8 = 4 * tg + tt
                    bm = E("tensor", lambda e, t8=t8, g8=g8:
                           e.matmul(out=PB(t8), lhsT=s_or[:],
                                    rhs=bob[0:1, 512 * g8:512 * (g8 + 1)],
                                    start=False, stop=True),
                           waits=[("brl", brl_bo)], inc="omm")
                    slot = (g8 * 8 + t8) % 2
                    wx = []
                    if xres_free.get(slot) is not None:
                        wx.append(("x1e", xres_free[slot]))
                    xl = E("sync", lambda e, s=slot, t8=t8, g8=g8:
                           e.dma_start(out=xres[s][:],
                                       in_=xs[128 * t8:128 * (t8 + 1),
                                              512 * g8:512 * (g8 + 1)]),
                           waits=wx, inc="xrl", amt=16)
                    w1 = [("omm", bm), ("xrl", xl)]
                    if x1_free.get(slot) is not None:
                        w1.append(("x1s", x1_free[slot]))
                    x1e = E("vector", lambda e, t8=t8, s=slot:
                            e.tensor_add(out=x1st[s][:], in0=PB(t8), in1=xres[s][:]),
                            waits=w1, inc="x1e")
                    xres_free[slot] = x1e
                    if tt == 3:
                        wo_bank_rd[tg] = x1e
                    x1_free[slot] = E("scalar", lambda e, s=slot, t8=t8, g8=g8:
                                      e.dma_start(out=x1d.ap()[128 * t8:128 * (t8 + 1),
                                                               512 * g8:512 * (g8 + 1)],
                                                  in_=x1st[s][:]),
                                      waits=[("x1e", x1e)], inc="x1s", amt=16)

        btl = E("gpsimd", lambda e: e.dma_start(out=wb[0][0:64, 0:4096], in_=bTm),
                waits=[("ojh", T("ojh")), ("omm", T("omm"))], inc="btl", amt=16)
        bTb = wb[0][0:64, 0:4096]
        xn2f = wb[1][:, 0:4096]
        tr_bank = {5: None, 6: None}
        pml_rd = None
        pmlt_rd = None
        pmgT_rd = None
        pd_rd = {3: None, 4: None}
        ost_free = [None] * 2
        xa_free = {0: None, 1: None}
        lnx2_tok = {}
        L = lambda ee: mlt[:, 64 + ee:65 + ee]
        for t8 in range(8):
            wxl = [("x1s", T("x1s"))]
            if lnx2_tok.get(t8 % 2) is not None:
                wxl.append(("lnx", lnx2_tok[t8 % 2]))
            if xa_free[t8 % 2] is not None:
                wxl.append(("ooe", xa_free[t8 % 2]))
            xl = E("sync", lambda e, t=t8: e.dma_start(out=xa[t % 2][:],
                                                       in_=x1d.ap()[128 * t:128 * (t + 1), :]),
                   waits=wxl, inc="xld", amt=16)
            a_x = ln_tile(xa[t8 % 2], xn[t8 % 2], t8 % 2, [("xld", xl)])
            lnx2_tok[t8 % 2] = a_x
            for g in range(4):
                bank = 5 + ((t8 * 4 + g) % 2)
                w = [("lnx", a_x), ("ojh", T("ojh"))]
                if t8 == 0 and g < 2:
                    w.append(("x1e", T("x1e")))
                if tr_bank[bank] is not None:
                    w.append(("tev", tr_bank[bank]))
                last = None
                for j2 in range(8):
                    j = 8 * g + j2
                    last = E("tensor", lambda e, b=bank, j2=j2, j=j, t=t8: e.transpose(
                        out=PBb(b)[:, 128 * j2:128 * (j2 + 1)],
                        in_=xn[t % 2][:, 128 * j:128 * (j + 1)], identity=s_id[:]),
                        waits=(w if j2 == 0 else []), inc=("ttr" if j2 == 7 else None))
                tr_bank[bank] = E("vector", lambda e, b=bank, g=g: e.tensor_copy(
                    out=xn2f[:, 1024 * g:1024 * (g + 1)], in_=PBb(b)[:]),
                    waits=[("ttr", last), ("ojh", T("ojh"))], inc="tev")
            tevt = T("tev")
            for j in range(32):
                w2 = []
                if j == 0:
                    w2 = [("tev", tevt), ("cst", cst_all)]
                    if pml_rd is not None:
                        w2.append(("mle", pml_rd))
                    if t8 == 0:
                        w2.append(("x1e", T("x1e")))
                E("tensor", lambda e, j=j: e.matmul(out=pall[0:68, 0:128],
                                                    lhsT=rhswb[:, 68 * j:68 * (j + 1)],
                                                    rhs=xn2f[:, 128 * j:128 * (j + 1)],
                                                    start=(j == 0), stop=(j == 31)),
                  waits=w2, inc=("mlm" if j == 31 else None))
            pml_rd = E("vector", lambda e: e.tensor_copy(out=sml[0:68, :], in_=pall[0:68, 0:128]),
                       waits=[("mlm", T("mlm"))], inc="mle")
            w3 = [("mle", pml_rd)]
            if pmlt_rd is not None:
                w3.append(("mlte", pmlt_rd))
            if t8 == 0:
                w3.append(("x1e", T("x1e")))
            trm = E("tensor", lambda e: e.transpose(out=PBb(1)[:, 0:68], in_=sml[0:68, :],
                                                    identity=s_id[0:68, 0:68]),
                    waits=w3, inc="mltm")
            pmlt_rd = E("vector", lambda e: e.tensor_copy(out=mlt[:], in_=PBb(1)[:, 0:68]),
                        waits=[("mltm", trm)], inc="mlte")
            fw = [("mlte", pmlt_rd)]
            for e_i in range(4):
                terms = [(a_i, ALU.is_ge if a_i < e_i else ALU.is_gt)
                         for a_i in range(4) if a_i != e_i]
                for k_i, (a_i, op) in enumerate(terms):
                    if k_i == 0:
                        E("vector", lambda e, a=a_i, ei=e_i, op=op: e.tensor_tensor(
                            out=gl[:, ei:ei + 1], in0=L(a), in1=L(ei), op=op), waits=fw,
                          fence=True)
                        fw = []
                    else:
                        E("vector", lambda e, a=a_i, ei=e_i, op=op: e.tensor_tensor(
                            out=gl[:, 4:5], in0=L(a), in1=L(ei), op=op), fence=True)
                        E("vector", lambda e, ei=e_i: e.tensor_add(
                            out=gl[:, ei:ei + 1], in0=gl[:, ei:ei + 1], in1=gl[:, 4:5]),
                          fence=True)
            E("vector", lambda e: e.tensor_max(out=gl[:, 5:6], in0=L(0), in1=L(1)), fence=True)
            E("vector", lambda e: e.tensor_max(out=gl[:, 5:6], in0=gl[:, 5:6], in1=L(2)),
              fence=True)
            E("vector", lambda e: e.tensor_max(out=gl[:, 5:6], in0=gl[:, 5:6], in1=L(3)),
              fence=True)
            vnm = E("vector", lambda e: e.tensor_scalar(out=gl[:, 6:7], in0=gl[:, 5:6],
                                                        scalar1=-1.0, scalar2=None,
                                                        op0=ALU.mult), inc="gnm", fence=True)
            for e_i in range(4):
                E("scalar", lambda e, ei=e_i: e.activation(out=gl[:, 7 + ei:8 + ei], in_=L(ei),
                                                           func=AF.Exp, bias=gl[:, 6:7],
                                                           scale=1.0),
                  waits=([("gnm", vnm)] if e_i == 0 else []),
                  inc=("gee" if e_i == 3 else None))
            gee = T("gee")
            for e_i in range(4):
                E("vector", lambda e, ei=e_i: e.tensor_single_scalar(
                    out=gl[:, 11 + ei:12 + ei], in_=gl[:, ei:ei + 1], scalar=1.5, op=ALU.is_le),
                  waits=([("gee", gee)] if e_i == 0 else []), fence=True)
            for e_i in range(4):
                E("vector", lambda e, ei=e_i: e.tensor_mul(out=gl[:, 15 + ei:16 + ei],
                                                           in0=gl[:, 7 + ei:8 + ei],
                                                           in1=gl[:, 11 + ei:12 + ei]),
                  fence=True)
            E("vector", lambda e: e.tensor_add(out=gl[:, 19:20], in0=gl[:, 15:16],
                                               in1=gl[:, 16:17]), fence=True)
            E("vector", lambda e: e.tensor_add(out=gl[:, 19:20], in0=gl[:, 19:20],
                                               in1=gl[:, 17:18]), fence=True)
            vdn = E("vector", lambda e: e.tensor_add(out=gl[:, 19:20], in0=gl[:, 19:20],
                                                     in1=gl[:, 18:19]), inc="gdn", fence=True)
            E("scalar", lambda e: e.activation(out=gl[:, 20:21], in_=gl[:, 19:20],
                                               func=AF.Ln, bias=0.0, scale=1.0),
              waits=[("gdn", vdn)])
            grg = E("scalar", lambda e: e.activation(out=gl[:, 20:21], in_=gl[:, 20:21],
                                                     func=AF.Exp, bias=0.0, scale=-1.0),
                    inc="grg", fence=True)
            for e_i in range(4):
                E("vector", lambda e, ei=e_i: e.tensor_scalar(
                    out=gl[:, 21 + ei:22 + ei], in0=gl[:, 15 + ei:16 + ei],
                    scalar1=gl[:, 20:21], scalar2=4.0, op0=ALU.mult, op1=ALU.mult),
                  waits=([("grg", grg)] if e_i == 0 else []), fence=True)
            gms = None
            for e_i in range(4):
                gms = E("vector", lambda e, ei=e_i: e.tensor_scalar(
                    out=mg[:, 16 * ei:16 * (ei + 1)], in0=mlt[:, 16 * ei:16 * (ei + 1)],
                    scalar1=gl[:, 21 + ei:22 + ei], scalar2=None, op0=ALU.mult),
                    inc=("gms" if e_i == 3 else None), fence=True)
            gms = T("gms")
            w4 = [("gms", gms)]
            if pmgT_rd is not None:
                w4.append(("mge", pmgT_rd))
            if t8 == 0:
                w4.append(("x1e", T("x1e")))
            E("tensor", lambda e: e.transpose(out=PBb(2)[0:64, 0:128], in_=mg[:, 0:64],
                                              identity=s_id[:]),
              waits=w4, inc="mgm")
            pmgT_rd = E("vector", lambda e: e.tensor_copy(out=mgT[:], in_=PBb(2)[0:64, 0:128]),
                        waits=[("mgm", T("mgm"))], inc="mge")
            mge = pmgT_rd
            last_ooe = None
            for g8 in range(8):
                bank = 3 + (g8 % 2)
                wd = [("mge", mge), ("btl", btl)]
                if pd_rd[bank] is not None:
                    wd.append(("ooe", pd_rd[bank]))
                if t8 == 0 and g8 < 2:
                    wd.append(("x1e", T("x1e")))
                dm = E("tensor", lambda e, bank=bank, g8=g8: e.matmul(
                    out=PB(bank), lhsT=mgT[0:64, :], rhs=bTb[0:64, 512 * g8:512 * (g8 + 1)],
                    start=True, stop=True), waits=wd, inc="dm")
                oslot = (t8 * 8 + g8) % 2
                wo_ = [("dm", dm)]
                if ost_free[oslot] is not None:
                    wo_.append(("outs", ost_free[oslot]))
                ooe = E("vector", lambda e, bank=bank, s=oslot, t=t8, g8=g8: e.tensor_add(
                    out=ost[s][:], in0=PB(bank), in1=xa[t % 2][:, 512 * g8:512 * (g8 + 1)]),
                    waits=wo_, inc="ooe")
                pd_rd[bank] = ooe
                last_ooe = ooe
                ost_free[oslot] = E("scalar", lambda e, s=oslot, t=t8, g8=g8: e.dma_start(
                    out=out[128 * t:128 * (t + 1), 512 * g8:512 * (g8 + 1)], in_=ost[s][:]),
                    waits=[("ooe", ooe)], inc="outs", amt=16)
            xa_free[t8 % 2] = last_ooe

        if taps:
            E("gpsimd", lambda e: e.dma_start(out=tap["ksend"].ap(), in_=k_send.ap()),
              waits=[("kst", T("kst"))], inc="tapd", amt=16)
            E("gpsimd", lambda e: e.dma_start(out=tap["qsend"].ap(), in_=q_send.ap()),
              waits=[("qst", T("qst")), ("mxst", T("mxst"))], inc="tapd", amt=16)
            E("gpsimd", lambda e: e.dma_start(out=tap["vsend"].ap(), in_=v_send.ap()),
              waits=[("vst", T("vst"))], inc="tapd", amt=16)
            E("gpsimd", lambda e: e.dma_start(out=tap["aosend"].ap(), in_=ao_send.ap()),
              waits=[("aos", T("aos"))], inc="tapd", amt=16)
            E("gpsimd", lambda e: e.dma_start(out=tap["aog"].ap(), in_=aogD.ap()),
              waits=[("cca", cca)], inc="tapd", amt=16)
            E("gpsimd", lambda e: e.dma_start(out=tap["x1"].ap(), in_=x1d.ap()),
              waits=[("x1s", T("x1s"))], inc="tapd", amt=16)
            E("gpsimd", lambda e: e.dma_start(out=tap["kg"].ap(), in_=kgD.ap()),
              waits=[("cck", cck)], inc="tapd", amt=16)
            E("gpsimd", lambda e: e.dma_start(out=tap["qg"].ap(), in_=qgD.ap()),
              waits=[("ccq", ccq)], inc="tapd", amt=16)
            E("gpsimd", lambda e: e.dma_start(out=tap["vg"].ap(), in_=vgD.ap()),
              waits=[("ccv", ccv)], inc="tapd", amt=16)
            S.wait_only("gpsimd", [("tapd", T("tapd"))])

        S.wait_only("scalar", [("outs", T("outs"))])

        for chname in S.cnt:
            sems[chname] = ctx.enter_context(nc.semaphore(chname))
        with allow_ici_all_to_all():
            with nc.Block() as block:
                S.replay(block, sems)
    return nc


def _prep_inputs(L0_attn_x, L0_attn_Wq, L0_attn_bq, L0_attn_Wk, L0_attn_bk,
                 L0_attn_Wv, L0_attn_bv, L0_attn_Wo, L0_attn_bo,
                 L0_ffn_A, L0_ffn_B, L0_ffn_Wr):
    x = np.asarray(L0_attn_x, dtype=np.float32)
    C = lambda a: np.ascontiguousarray(np.asarray(a, dtype=np.float32))
    A = np.asarray(L0_ffn_A, dtype=np.float32)
    B = np.asarray(L0_ffn_B, dtype=np.float32)
    Wr = np.asarray(L0_ffn_Wr, dtype=np.float32)
    aT = A.transpose(2, 0, 1).reshape(D, 64)
    wrT = np.asarray(Wr).T
    rhsw = np.concatenate([aT.reshape(32, 128, 64),
                           np.ascontiguousarray(wrT).reshape(32, 128, 4)], axis=2)
    shared = dict(
        wqT=C(np.asarray(L0_attn_Wq).T), wkT=C(np.asarray(L0_attn_Wk).T),
        wvT=C(np.asarray(L0_attn_Wv).T), woT=C(np.asarray(L0_attn_Wo).T),
        bqc=C(np.asarray(L0_attn_bq).reshape(32, 128).T),
        bkc=C(np.asarray(L0_attn_bk).reshape(32, 128).T),
        bvr=C(np.asarray(L0_attn_bv).reshape(1, D)),
        bor=C(np.asarray(L0_attn_bo).reshape(1, D)),
        rhsw=C(rhsw.transpose(1, 0, 2).reshape(128, 32 * 68)),
        bTm=C(B.transpose(0, 2, 1).reshape(64, D)))
    in_maps = []
    for c in range(8):
        g, qc = c // 4, c % 4
        xsl = np.concatenate([x[g, 512 * qc:512 * (qc + 1)],
                              x[g, 512 * (7 - qc):512 * (8 - qc)]], axis=0)
        m = dict(shared)
        m["xs"] = np.ascontiguousarray(xsl)
        in_maps.append(m)
    return in_maps


def kernel(**inputs):
    in_maps = _prep_inputs(**inputs)
    with allow_ici_all_to_all():
        nc = build_nc(taps=_TAPS)
        res = run_bass_kernel_spmd(nc, in_maps, core_ids=list(range(8)))
    outf = np.zeros((2, D, D), dtype=np.float32)
    for c in range(8):
        g, qc = c // 4, c % 4
        o = res.results[c]["out"]
        outf[g, 512 * qc:512 * (qc + 1)] = o[0:512]
        outf[g, 512 * (7 - qc):512 * (8 - qc)] = o[512:1024]
    if _TAPS:
        kernel._tap_results = res.results
    return outf


# version 02
# speedup vs baseline: 1.0953x; 1.0953x over previous
import math
from contextlib import ExitStack

import numpy as np
import ml_dtypes

import concourse.bass as bass
import concourse.mybir as mybir
from concourse.bass_utils import run_bass_kernel_spmd
from concourse.aot_env import allow_ici_all_to_all

F32 = mybir.dt.float32
BF16 = mybir.dt.bfloat16
AF = mybir.ActivationFunctionType
ALU = mybir.AluOpType
AXX = mybir.AxisListType.X

D = 4096
NT = 1024
NP = 8
SCLQ = 1.0 / math.sqrt(128.0)
EPS = 1e-5
_TAPS = False

ENGS = ("sync", "gpsimd", "scalar", "vector", "tensor")


def _rho(B):
    return B if B < 4 else 7 - B


def _co(B):
    return 0 if B < 4 else 512


class Sched:
    def __init__(self):
        self.ops = {e: [] for e in ENGS}
        self.cnt = {}
        self._pend = {e: [] for e in ENGS}

    def tok(self, ch):
        return self.cnt.get(ch, 0)

    def emit(self, eng, fn, waits=(), inc=None, amt=1, fence=False):
        if fence:
            self.fence(eng)
        waits = list(waits) + self._pend[eng]
        self._pend[eng] = []
        if inc is not None:
            self.cnt[inc] = self.cnt.get(inc, 0) + amt
        self.ops[eng].append((tuple((c, v) for c, v in waits if v), fn, inc, amt))
        return self.cnt.get(inc, 0)

    def fence(self, eng):
        if not self.ops[eng]:
            return
        w, fn, inc, amt = self.ops[eng][-1]
        if fn is None:
            return
        if inc is None:
            ch = "sf_" + eng
            self.cnt[ch] = self.cnt.get(ch, 0) + 1
            self.ops[eng][-1] = (w, fn, ch, 1)
            self._pend[eng].append((ch, self.cnt[ch]))
        else:
            self._pend[eng].append((inc, self.cnt[inc]))

    def wait_only(self, eng, waits):
        self.ops[eng].append((tuple(waits), None, None, 0))

    def replay(self, block, sems):
        def mk(ename):
            def body(eng):
                seen = {}
                for waits, fn, inc, amt in self.ops[ename]:
                    for ch, val in waits:
                        if val > seen.get(ch, 0):
                            eng.wait_ge(sems[ch], val)
                            seen[ch] = val
                    if fn is not None:
                        ins = fn(eng)
                        if inc is not None:
                            ins.then_inc(sems[inc], amt)
            return body

        block.sync(mk("sync"))
        block.gpsimd(mk("gpsimd"))
        block.scalar(mk("scalar"))
        block.vector(mk("vector"))
        block.tensor(mk("tensor"))


def build_nc(taps=False):
    nc = bass.Bass("TRN2", target_bir_lowering=False)

    xs = nc.dram_tensor("xs", [NT, D], F32, kind="ExternalInput").ap()
    wqT = nc.dram_tensor("wqT", [D, D], F32, kind="ExternalInput").ap()
    wkT = nc.dram_tensor("wkT", [D, D], F32, kind="ExternalInput").ap()
    wvT = nc.dram_tensor("wvT", [D, D], F32, kind="ExternalInput").ap()
    woT = nc.dram_tensor("woT", [D, D], F32, kind="ExternalInput").ap()
    bqc = nc.dram_tensor("bqc", [128, 32], F32, kind="ExternalInput").ap()
    bkc = nc.dram_tensor("bkc", [128, 32], F32, kind="ExternalInput").ap()
    bvr = nc.dram_tensor("bvr", [1, D], F32, kind="ExternalInput").ap()
    bor = nc.dram_tensor("bor", [1, D], F32, kind="ExternalInput").ap()
    rhsw = nc.dram_tensor("rhsw", [128, 32 * 68], F32, kind="ExternalInput").ap()
    bTm = nc.dram_tensor("bTm", [64, D], F32, kind="ExternalInput").ap()
    out = nc.dram_tensor("out", [NT, D], F32, kind="ExternalOutput").ap()

    ident = nc.inline_tensor(np.eye(128, dtype=ml_dtypes.bfloat16), "ident")
    ones_col = nc.inline_tensor(np.ones((128, 1), dtype=ml_dtypes.bfloat16), "ones_col")
    ones_row = nc.inline_tensor(np.ones((1, 128), dtype=ml_dtypes.bfloat16), "ones_row")
    ones_row_f = nc.inline_tensor(np.ones((1, 128), dtype=np.float32), "ones_row_f")
    mknp = np.zeros((128, 4 * 512), dtype=ml_dtypes.bfloat16)
    pp = np.arange(128)[:, None]
    jj_ = np.arange(512)[None, :]
    for u in range(4):
        mknp[:, 512 * u:512 * (u + 1)] = ((128 * u + pp) <= jj_).astype(ml_dtypes.bfloat16)
    maskt = nc.inline_tensor(mknp, "maskt")

    k_send = nc.dram_tensor("k_send", [8 * 512, NT], BF16, kind="Internal")
    q_send = nc.dram_tensor("q_send", [8 * 512, NT], BF16, kind="Internal")
    v_send = nc.dram_tensor("v_send", [8 * 1024, 512], BF16, kind="Internal")
    kgD = nc.dram_tensor("kgD", [8 * 512, NT], BF16, kind="Internal")
    qgD = nc.dram_tensor("qgD", [8 * 512, NT], BF16, kind="Internal")
    vgD = nc.dram_tensor("vgD", [8 * 1024, 512], BF16, kind="Internal")
    ao_send = nc.dram_tensor("ao_send", [8 * 512, NT], BF16, kind="Internal")
    aogD = nc.dram_tensor("aogD", [8 * 512, NT], BF16, kind="Internal")
    x1d = nc.dram_tensor("x1d", [NT, D], F32, kind="Internal")
    tap = {}
    if taps:
        for n, sh, dt in [("ksend", (8 * 512, NT), BF16), ("qsend", (8 * 512, NT), BF16),
                          ("vsend", (8 * 1024, 512), BF16), ("aosend", (8 * 512, NT), BF16),
                          ("aog", (8 * 512, NT), BF16), ("x1", (NT, D), F32),
                          ("kg", (8 * 512, NT), BF16), ("qg", (8 * 512, NT), BF16),
                          ("vg", (8 * 1024, 512), BF16)]:
            tap[n] = nc.dram_tensor("tap_" + n, list(sh), dt, kind="ExternalOutput")

    S = Sched()
    E = S.emit
    T = S.tok

    with ExitStack() as ctx:
        SB = lambda n, sh, dt: ctx.enter_context(nc.sbuf_tensor(n, sh, dt))
        big = SB("big", [128, 32 * NT], BF16)
        wb = [SB(f"wb{i}", [128, 8192], BF16) for i in range(2)]
        xa = [SB(f"xa{i}", [128, D], F32) for i in range(2)]
        xn = [SB(f"xn{i}", [128, D], BF16) for i in range(2)]
        dump = SB("dump", [128, D], BF16)
        kst = [SB(f"kst{i}", [128, 512], BF16) for i in range(4)]
        k2t = [SB(f"k2t{i}", [128, 512], BF16) for i in range(2)]
        attT = [SB(f"attT{i}", [128, 512], BF16) for i in range(4)]
        rbc = [SB(f"rbc{i}", [128, 512], F32) for i in range(2)]
        aost = [SB(f"aost{i}", [128, 512], BF16) for i in range(3)]
        xres = [SB(f"xres{i}", [128, 512], F32) for i in range(2)]
        x1st = [SB(f"x1st{i}", [128, 512], F32) for i in range(2)]
        ost = [SB(f"ost{i}", [128, 512], F32) for i in range(2)]
        sml = SB("sml", [128, 128], BF16)
        mlt = SB("mlt", [128, 68], BF16)
        mg = SB("mg", [128, 64], BF16)
        mgT = SB("mgT", [64, 128], BF16)
        gl = SB("gl", [128, 28], F32)
        s_id = SB("s_id", [128, 128], BF16)
        s_msk = SB("s_msk", [128, 4 * 512], BF16)
        s_oc = SB("s_oc", [128, 1], BF16)
        s_or = SB("s_or", [1, 128], BF16)
        s_orf = SB("s_orf", [1, 128], F32)
        s_bq = SB("s_bq", [128, 32], F32)
        s_bqs = SB("s_bqs", [128, 32], F32)
        s_bk = SB("s_bk", [128, 32], F32)
        rhswb = SB("rhswb", [128, 32 * 68], BF16)
        urow = SB("urow", [128, 2048], F32)
        mqct = SB("mqct", [128, 8], BF16)
        mqt = SB("mqt", [128, 1], BF16)
        nmqt = SB("nmqt", [1, 1024], BF16)
        stat = SB("stat", [128, 16], F32)
        epsc = SB("epsc", [128, 1], F32)
        pall = ctx.enter_context(nc.psum_tensor("pall", [128, 4096], F32))
        sems = {}

        def PB(b):
            return pall[:, 512 * b:512 * (b + 1)]

        def PBb(b):
            return pall[:, 512 * b:512 * (b + 1)].bitcast(BF16)

        bvb = dump[0:1, :]
        bob = dump[0:1, :]
        r_t = urow[0:1, 0:512]
        r_rd = [urow[0:1, 512:1024], urow[0:1, 1024:1536]]
        r_ksq = urow[0:1, 1536:1568]
        r_qsq = urow[0:1, 1568:1600]
        r_m2 = urow[0:1, 1600:1608]
        r_mro = urow[0:1, 1608:1616]
        r_nM = urow[0:1, 1616:1624]
        r_mx1 = urow[0:1, 1624:1632]

        for dst, src in [(s_id[:], ident.ap()), (s_msk[:], maskt.ap()),
                         (s_oc[:], ones_col.ap()), (s_or[:], ones_row.ap()),
                         (s_orf[:], ones_row_f.ap()), (s_bq[:], bqc), (s_bk[:], bkc)]:
            E("sync", lambda e, d=dst, s=src: e.dma_start(out=d, in_=s), inc="cst", amt=16)
        for dst, src in [(rhswb[:], rhsw)]:
            E("gpsimd", lambda e, d=dst, s=src: e.dma_start(out=d, in_=s), inc="cst", amt=16)
        cst_all = T("cst")
        E("vector", lambda e: e.tensor_scalar(out=s_bqs[:], in0=s_bq[:], scalar1=SCLQ,
                                              scalar2=None, op0=ALU.mult),
          waits=[("cst", cst_all)], inc="prep")
        E("vector", lambda e: e.memset(urow[0:1, 1536:1600], -3.0e38), inc="prep")
        prep_all = E("vector", lambda e: e.memset(epsc[:], EPS), inc="prep")

        def ln_tile(xa_t, xn_t, par, wfirst):
            o = 8 * par
            c_s1, c_s2, c_mean, c_var, c_ms, c_rstd, c_nmr, c_m2 = [
                stat[:, o + i:o + i + 1] for i in range(8)]
            E("vector", lambda e: e.tensor_scalar(out=dump[:], in0=xa_t[:], scalar1=1.0,
                                                  scalar2=0.0, op0=ALU.mult, op1=ALU.add,
                                                  accum_out=c_s1),
              waits=wfirst)
            E("vector", lambda e: e.scalar_tensor_tensor(out=dump[:], in0=xa_t[:], scalar=1.0,
                                                         in1=xa_t[:], op0=ALU.mult, op1=ALU.mult,
                                                         accum_out=c_s2))
            E("vector", lambda e: e.tensor_scalar(out=c_mean, in0=c_s1, scalar1=1.0 / D,
                                                  scalar2=None, op0=ALU.mult), fence=True)
            E("vector", lambda e: e.scalar_tensor_tensor(out=c_var, in0=c_s1, scalar=c_mean,
                                                         in1=c_s2, op0=ALU.mult,
                                                         op1=ALU.subtract), fence=True)
            v_var = E("vector", lambda e: e.tensor_scalar(out=c_var, in0=c_var,
                                                          scalar1=-1.0 / D, scalar2=None,
                                                          op0=ALU.mult), inc="lnv", fence=True)
            E("scalar", lambda e: e.activation(out=c_rstd, in_=c_var, func=AF.Ln,
                                               bias=epsc[:], scale=1.0),
              waits=[("lnv", v_var), ("prep", prep_all)])
            a_r = E("scalar", lambda e: e.activation(out=c_rstd, in_=c_rstd, func=AF.Exp,
                                                     bias=0.0, scale=-0.5), inc="lnr",
                    fence=True)
            v_n = E("vector", lambda e: e.tensor_scalar(out=c_nmr, in0=c_mean, scalar1=c_rstd,
                                                        scalar2=-1.0, op0=ALU.mult, op1=ALU.mult),
                    waits=[("lnr", a_r)], inc="lnn")
            a_x = E("scalar", lambda e: e.activation(out=xn_t[:], in_=xa_t[:], func=AF.Identity,
                                                     bias=c_nmr, scale=c_rstd),
                    waits=[("lnn", v_n)], inc="lnx")
            return a_x

        tev_bank = {0: 0, 1: 0}
        lnx_tok = {}
        for t8 in range(8):
            wx = [("lnx", lnx_tok[t8 % 2])] if t8 >= 2 else []
            xl = E("sync", lambda e, t=t8: e.dma_start(out=xa[t % 2][:],
                                                       in_=xs[128 * t:128 * (t + 1), :]),
                   waits=wx, inc="xld", amt=16)
            a_x = ln_tile(xa[t8 % 2], xn[t8 % 2], t8 % 2,
                          [("xld", xl), ("cst", cst_all)])
            lnx_tok[t8 % 2] = a_x
            for g in range(4):
                bank = (t8 * 4 + g) % 2
                w = [("lnx", a_x)]
                if tev_bank[bank]:
                    w.append(("tev", tev_bank[bank]))
                last = None
                for j2 in range(8):
                    j = 8 * g + j2
                    last = E("tensor", lambda e, b=bank, j2=j2, j=j, t=t8: e.transpose(
                        out=PBb(b)[:, 128 * j2:128 * (j2 + 1)],
                        in_=xn[t % 2][:, 128 * j:128 * (j + 1)], identity=s_id[:]),
                        waits=(w if j2 == 0 else []), inc=("ttr" if j2 == 7 else None))
                dst = big[:].rearrange("p (j c) -> p j c", c=NT)[:, 8 * g:8 * (g + 1),
                                                                128 * t8:128 * (t8 + 1)]
                src = PBb(bank)[:].rearrange("p (j c) -> p j c", c=128)
                tev_bank[bank] = E("vector", lambda e, d=dst, s=src: e.tensor_copy(out=d, in_=s),
                                   waits=[("ttr", last)], inc="tev")
        tev_all = T("tev")
        brl_all = E("gpsimd", lambda e: e.dma_start(out=bvb, in_=bvr),
                    waits=[("lnv", T("lnv"))], inc="brl", amt=16)

        slotg = {}

        def proj_phase(wT, pref, bias_t, scale, send, entry_guard):
            strip_free = {0: None, 1: None}
            for hg in range(8):
                ev_before = T(pref + "ev")
                wtk = {}
                for jh in range(2):
                    buf = jh
                    src = wT[2048 * jh:2048 * (jh + 1), 512 * hg:512 * (hg + 1)].rearrange(
                        "(j p) c -> p j c", p=128)
                    dstw = wb[buf][:].rearrange("p (j c) -> p j c", c=512)
                    w = [("tev", tev_all)]
                    if strip_free[buf] is not None:
                        w.append((pref + "mm", strip_free[buf]))
                    if pref == "q":
                        w.append(("kmm", T("kmm")))
                    wtk[jh] = E("gpsimd", lambda e, s=src, d=dstw: e.dma_start(out=d, in_=s),
                                waits=w, inc="wld", amt=16)
                for jh in range(2):
                    for j16 in range(16):
                        for hh in range(4):
                            for half in range(2):
                                j = 16 * jh + j16
                                st = (jh == 0 and j16 == 0)
                                sp = (jh == 1 and j16 == 15)
                                w2 = []
                                if half == 0 and j16 == 0 and hh == 0:
                                    w2.append(("wld", wtk[jh]))
                                if jh == 0 and half == 0 and j16 == 0 and hh == 0:
                                    if ev_before:
                                        w2.append((pref + "ev", ev_before))
                                    if hg == 0:
                                        w2.extend(entry_guard)
                                E("tensor",
                                  lambda e, b=jh, hh=hh, j16=j16, j=j, half=half, st=st, sp=sp:
                                  e.matmul(out=pall[:, 1024 * hh + 512 * half:
                                                    1024 * hh + 512 * half + 512],
                                           lhsT=wb[b][:, 512 * j16 + 128 * hh:
                                                      512 * j16 + 128 * (hh + 1)],
                                           rhs=big[:, NT * j + 512 * half:NT * j + 512 * half + 512],
                                           start=st, stop=sp),
                                  waits=w2,
                                  inc=(pref + "mm") if (j16 == 15 and half == 1 and hh == 3)
                                  else None)
                    strip_free[jh] = T(pref + "mm")
                mmt = T(pref + "mm")
                for half in range(2):
                    for hh in range(4):
                        h = 4 * hg + hh
                        slot = (hg * 8 + half * 4 + hh) % 4
                        wv = [(pref + "mm", mmt), ("prep", prep_all)] + slotg.get(slot, [])
                        ev = E("scalar", lambda e, hh=hh, half=half, h=h, s=slot:
                               e.activation(out=kst[s][:],
                                            in_=pall[:, 1024 * hh + 512 * half:
                                                     1024 * hh + 512 * half + 512],
                                            func=AF.Identity, bias=bias_t[:, h:h + 1],
                                            scale=scale),
                               waits=wv, inc=pref + "ev")
                        r0 = 512 * (h // 4) + 128 * (h % 4)
                        stk = E("scalar", lambda e, s=slot, r0=r0, half=half:
                                e.dma_start(out=send.ap()[r0:r0 + 128,
                                                          512 * half:512 * (half + 1)],
                                            in_=kst[s][:]),
                                waits=[(pref + "ev", ev)], inc=pref + "st", amt=16)
                        slotg[slot] = [(pref + "st", stk)]

        proj_phase(wkT, "k", s_bk, 1.0, k_send, [])

        proj_phase(wqT, "q", s_bqs, SCLQ, q_send, [("kev", T("kev"))])

        strip_free = {0: None, 1: None}
        vbank_rd = {}
        for g8 in range(8):
            wtk = {}
            for jh in range(2):
                src = wvT[2048 * jh:2048 * (jh + 1), 512 * g8:512 * (g8 + 1)].rearrange(
                    "(j p) c -> p j c", p=128)
                dstw = wb[jh][:].rearrange("p (j c) -> p j c", c=512)
                w = [("qmm", T("qmm"))]
                if strip_free[jh] is not None:
                    w.append(("vjh", strip_free[jh]))
                wtk[jh] = E("gpsimd", lambda e, s=src, d=dstw: e.dma_start(out=d, in_=s),
                            waits=w, inc="wld", amt=16)
            for jh in range(2):
                for tg in range(2):
                    for j16 in range(16):
                        for tt in range(4):
                            t8 = 4 * tg + tt
                            j = 16 * jh + j16
                            st = (jh == 0 and j16 == 0)
                            w2 = []
                            if tg == 0 and j16 == 0 and tt == 0:
                                w2.append(("wld", wtk[jh]))
                                if st:
                                    w2.append(("qev", T("qev")))
                            if st and j16 == 0 and tt == 0:
                                bt = vbank_rd.get(tg)
                                if bt:
                                    w2.append(("vev", bt))
                            E("tensor", lambda e, b=jh, t8=t8, j16=j16, j=j, st=st:
                              e.matmul(out=PB(t8),
                                       lhsT=big[:, NT * j + 128 * t8:NT * j + 128 * (t8 + 1)],
                                       rhs=wb[b][:, 512 * j16:512 * (j16 + 1)],
                                       start=st, stop=False),
                              waits=w2,
                              inc="vjh" if (tg == 1 and j16 == 15 and tt == 3) else None)
                strip_free[jh] = T("vjh")
            for tg in range(2):
                for tt in range(4):
                    t8 = 4 * tg + tt
                    bm = E("tensor", lambda e, t8=t8, g8=g8:
                           e.matmul(out=PB(t8), lhsT=s_or[:],
                                    rhs=bvb[0:1, 512 * g8:512 * (g8 + 1)],
                                    start=False, stop=True),
                           waits=[("cst", cst_all), ("brl", brl_all)], inc="vmm")
                    slot = (g8 * 8 + t8) % 4
                    wv = [("vmm", bm)] + slotg.get(slot, [])
                    ev = E("vector", lambda e, t8=t8, s=slot: e.tensor_copy(out=kst[s][:],
                                                                            in_=PB(t8)),
                           waits=wv, inc="vev")
                    if tt == 3:
                        vbank_rd[tg] = ev
                    stk = E("scalar", lambda e, s=slot, g8=g8, t8=t8:
                            e.dma_start(out=v_send.ap()[1024 * g8 + 128 * t8:
                                                        1024 * g8 + 128 * (t8 + 1), :],
                                        in_=kst[s][:]),
                            waits=[("vev", ev)], inc="vst", amt=16)
                    slotg[slot] = [("vst", stk)]
        brl_bo = E("gpsimd", lambda e: e.dma_start(out=bob, in_=bor),
                   waits=[("vmm", T("vmm"))], inc="brl", amt=16)

        cck = E("gpsimd", lambda e: e.collective_compute(
            "AllToAll", ALU.bypass, replica_groups=[list(range(8))],
            ins=[k_send.ap().opt()], outs=[kgD.ap().opt()]),
            waits=[("kst", T("kst")), ("qst", T("qst")), ("vst", T("vst")),
                   ("qmm", T("qmm")), ("vmm", T("vmm")), ("xld", T("xld")),
                   ("cst", cst_all), ("brl", T("brl")), ("wld", T("wld"))], inc="cck")
        ccq = E("gpsimd", lambda e: e.collective_compute(
            "AllToAll", ALU.bypass, replica_groups=[list(range(8))],
            ins=[q_send.ap().opt()], outs=[qgD.ap().opt()]),
            waits=[("cck", cck)], inc="ccq")
        ccv = E("gpsimd", lambda e: e.collective_compute(
            "AllToAll", ALU.bypass, replica_groups=[list(range(8))],
            ins=[v_send.ap().opt()], outs=[vgD.ap().opt()]),
            waits=[("ccq", ccq)], inc="ccv")

        def kres(par):
            return big[:, 12288 * par:12288 * par + 4096]

        def vres(par):
            return big[:, 12288 * par + 4096:12288 * par + 8192]

        def qres(par):
            return big[:, 12288 * par + 8192:12288 * par + 12288]

        mqc = [mqct[:, 0:4], mqct[:, 4:8]]
        nmq = [nmqt[0:1, 0:512], nmqt[0:1, 512:1024]]
        mxps = [pall[:, 2560:2688], pall[:, 3072:3200]]
        pden = pall[0:1, 2048:2560]
        pbc = PB(4)
        sc_tok = {0: None, 1: None, 7: None}
        av_tok = {2: None, 3: None}
        attT_tok = [None] * 4
        rbc_tok = [None, None]
        aost_tok = [None] * 3
        mx_tok = [None, None]
        den_free = None
        ld_prev = {}
        sci = 0
        ui = 0
        for p in range(NP):
            b, hh = p // 4, p % 4
            par = p % 2
            w0 = [("cck", cck), ("ccq", ccq), ("ccv", ccv)]
            if ld_prev.get(par) is not None:
                w0.append(("dnm", ld_prev[par]))
            lds = 0
            for B in range(8):
                s = 4 * b + _rho(B)
                co = _co(B)
                E("sync", lambda e, s=s, co=co, hh=hh, B=B, par=par:
                  e.dma_start(out=kres(par)[:, 512 * B:512 * (B + 1)],
                              in_=kgD.ap()[512 * s + 128 * hh:512 * s + 128 * (hh + 1),
                                           co:co + 512]),
                  waits=(w0 if B == 0 else []), inc="p2ld", amt=16)
                E("sync", lambda e, s=s, co=co, hh=hh, B=B, par=par:
                  e.dma_start(out=vres(par)[:, 512 * B:512 * (B + 1)].rearrange(
                      "p (u c) -> p u c", c=128),
                      in_=vgD.ap()[1024 * s + co:1024 * s + co + 512,
                                   128 * hh:128 * (hh + 1)].rearrange("(u p) c -> p u c", p=128)),
                  inc="p2ld", amt=16)
                lds = E("sync", lambda e, s=s, co=co, hh=hh, B=B, par=par:
                        e.dma_start(out=qres(par)[:, 512 * B:512 * (B + 1)],
                                    in_=qgD.ap()[512 * s + 128 * hh:512 * s + 128 * (hh + 1),
                                                 co:co + 512]),
                        inc="p2ld", amt=16)

            def mx_minis(P, qt):
                for q4 in range(4):
                    for B in range(P + 1):
                        yield (P, q4, B)
                yield None

            def emit_mx_mini(item, qt):
                if item is None:
                    emit_mx_tail(qt)
                    return
                Pm, q4, B = item
                if True:
                    if True:
                        mi = (S.cnt.get("_mxi", 0)) % 2
                        S.cnt["_mxi"] = S.cnt.get("_mxi", 0) + 1
                        wmx = [("p2ld", lds)]
                        if mx_tok[mi] is not None:
                            wmx.append(("mxr", mx_tok[mi]))
                        mm = E("tensor", lambda e, mi=mi, P=Pm, q4=q4, B=B, par=par:
                               e.matmul(out=mxps[mi],
                                        lhsT=qres(par)[:, 512 * P + 128 * q4:
                                                       512 * P + 128 * (q4 + 1)],
                                        rhs=kres(par)[:, 512 * B:512 * B + 128],
                                        start=True, stop=True),
                               waits=wmx, inc="mxm")
                        rdt = E("vector", lambda e, mi=mi: e.reduce_max(
                            out=mqt[:, 0:1], in_=mxps[mi], axis=AXX),
                            waits=[("mxm", mm)], inc="mxr", fence=True)
                        if B == 0:
                            E("vector", lambda e, qt=qt, q4=q4: e.tensor_copy(
                                out=mqc[qt][:, q4:q4 + 1], in_=mqt[:, 0:1]),
                              inc="mqv", fence=True)
                        else:
                            E("vector", lambda e, qt=qt, q4=q4: e.tensor_max(
                                out=mqc[qt][:, q4:q4 + 1], in0=mqc[qt][:, q4:q4 + 1],
                                in1=mqt[:, 0:1]), inc="mqv", fence=True)
                        mx_tok[mi] = rdt

            def emit_mx_tail(qt):
                for q4 in range(4):
                    scb = [0, 1, 7][(S.cnt.get("_sci", 0)) % 3]
                    S.cnt["_sci"] = S.cnt.get("_sci", 0) + 1
                    wtp = [("mqv", T("mqv"))]
                    if sc_tok[scb] is not None:
                        wtp.append(sc_tok[scb])
                    tp = E("tensor", lambda e, scb=scb, qt=qt, q4=q4: e.transpose(
                        out=PBb(scb)[0:1, 0:128], in_=mqc[qt][:, q4:q4 + 1],
                        identity=s_id[:]),
                        waits=wtp, inc="mtp")
                    ng = E("vector", lambda e, scb=scb, qt=qt, q4=q4: e.tensor_scalar(
                        out=nmq[qt][0:1, 128 * q4:128 * (q4 + 1)], in0=PBb(scb)[0:1, 0:128],
                        scalar1=-1.0, scalar2=None, op0=ALU.mult),
                        waits=[("mtp", tp)], inc="nmg")
                    sc_tok[scb] = ("nmg", ng)

            def units(P, qt, nmg_tok, minis, mqt_next):
                nonlocal ui, den_free
                avb = 2 + qt
                for B in range(P + 1):
                    for u in range(4):
                        if minis is not None:
                            item = next(minis, "done")
                            if item != "done":
                                emit_mx_mini(item, mqt_next)
                        first = (B == 0 and u == 0)
                        lastu = (B == P and u == 3)
                        scb = [0, 1, 7][(S.cnt.get("_sci", 0)) % 3]
                        S.cnt["_sci"] = S.cnt.get("_sci", 0) + 1
                        ats = ui % 4
                        wsc = [("p2ld", lds), ("nmg", nmg_tok)]
                        if sc_tok[scb] is not None:
                            wsc.append(sc_tok[scb])
                        smm = E("tensor", lambda e, scb=scb, B=B, u=u, P=P, par=par:
                                e.matmul(out=PB(scb),
                                         lhsT=kres(par)[:, 512 * B + 128 * u:
                                                        512 * B + 128 * (u + 1)],
                                         rhs=qres(par)[:, 512 * P:512 * (P + 1)],
                                         start=True, stop=False),
                                waits=wsc, inc="smm")
                        smm = E("tensor", lambda e, scb=scb, qt=qt:
                                e.matmul(out=PB(scb), lhsT=s_or[:],
                                         rhs=nmq[qt][0:1, :], start=False, stop=True),
                                inc="smm")
                        wxp = [("smm", smm)]
                        if attT_tok[ats] is not None:
                            wxp.append(("dnm", attT_tok[ats]))
                        sxp = E("scalar", lambda e, scb=scb, a=ats:
                                e.activation(out=attT[a][:], in_=PB(scb), func=AF.Exp,
                                             bias=0.0, scale=1.0),
                                waits=wxp, inc="sxp")
                        sc_tok[scb] = ("sxp", sxp)
                        gate = ("sxp", sxp)
                        if B == P:
                            mku = E("vector", lambda e, a=ats, u=u:
                                    e.tensor_mul(out=attT[a][:], in0=attT[a][:],
                                                 in1=s_msk[:, 512 * u:512 * (u + 1)]),
                                    waits=[("sxp", sxp)], inc="msk")
                            gate = ("msk", mku)
                        wav = [gate]
                        if first and av_tok[avb] is not None:
                            wav.append(("aev", av_tok[avb]))
                        E("tensor", lambda e, avb=avb, B=B, u=u, par=par, a=ats,
                          first=first, lastu=lastu:
                          e.matmul(out=PB(avb),
                                   lhsT=vres(par)[:, 512 * B + 128 * u:512 * B + 128 * (u + 1)],
                                   rhs=attT[a][:], start=first, stop=lastu),
                          waits=wav, inc="avm")
                        wdn = []
                        if first and den_free is not None:
                            wdn.append(den_free)
                        dnm = E("tensor", lambda e, a=ats, first=first, lastu=lastu:
                                e.matmul(out=pden, lhsT=s_oc[:], rhs=attT[a][:],
                                         start=first, stop=lastu),
                                waits=wdn, inc="dnm")
                        attT_tok[ats] = dnm
                        ui += 1
                if minis is not None:
                    for item in minis:
                        emit_mx_mini(item, mqt_next)
                dl1 = E("scalar", lambda e: e.activation(
                    out=r_t[:], in_=pden, func=AF.Ln, bias=0.0, scale=1.0),
                    waits=[("dnm", T("dnm"))], inc="dnl")
                rd = E("scalar", lambda e, qt=qt: e.activation(out=r_rd[qt][:], in_=r_t[:],
                                                               func=AF.Exp, bias=0.0,
                                                               scale=-1.0),
                       inc="rdn", fence=True)
                bcm = E("tensor", lambda e: e.matmul(out=pbc, lhsT=s_orf[:],
                                                     rhs=r_rd[qt][:],
                                                     start=True, stop=True),
                        waits=[("rdn", rd), ("dnl", dl1)], inc="bcm")
                wcc = [("bcm", bcm)]
                if rbc_tok[qt] is not None:
                    wcc.append(("aev", rbc_tok[qt]))
                bcc = E("vector", lambda e, qt=qt: e.tensor_copy(out=rbc[qt][:], in_=pbc),
                        waits=wcc, inc="bcc")
                den_free = ("bcc", bcc)
                aslot = (p * 8 + P) % 3
                wev = [("bcc", bcc)]
                if aost_tok[aslot] is not None:
                    wev.append(("aos", aost_tok[aslot]))
                aev = E("vector", lambda e, avb=avb, qt=qt, s=aslot:
                        e.tensor_tensor(out=aost[s][:], in0=PB(avb), in1=rbc[qt][:],
                                        op=ALU.mult),
                        waits=wev, inc="aev")
                av_tok[avb] = aev
                rbc_tok[qt] = aev
                d = 4 * b + _rho(P)
                co = _co(P)
                aost_tok[aslot] = E("scalar", lambda e, s=aslot, d=d, co=co, hh=hh:
                                    e.dma_start(out=ao_send.ap()[
                                        512 * d + 128 * hh:512 * d + 128 * (hh + 1),
                                        co:co + 512], in_=aost[s][:]),
                                    waits=[("aev", aev)], inc="aos", amt=16)

            for item in mx_minis(0, 0):
                emit_mx_mini(item, 0)
            nmg_tok = T("nmg")
            for P in range(8):
                minis = mx_minis(P + 1, (P + 1) % 2) if P < 7 else None
                units(P, P % 2, nmg_tok, minis, (P + 1) % 2)
                nmg_tok = T("nmg")
            ld_prev[par] = T("dnm")

        cca = E("gpsimd", lambda e: e.collective_compute(
            "AllToAll", ALU.bypass, replica_groups=[list(range(8))],
            ins=[ao_send.ap().opt()], outs=[aogD.ap().opt()]),
            waits=[("aos", T("aos")), ("p2ld", T("p2ld")), ("kst", T("kst")),
                   ("qst", T("qst")), ("vst", T("vst"))], inc="cca")

        agl = E("sync", lambda e: e.dma_start(
            out=big[:].rearrange("p (j c) -> p j c", c=NT),
            in_=aogD.ap().rearrange("(j p) c -> p j c", p=128)),
            waits=[("cca", cca), ("dnm", T("dnm")), ("smm", T("smm"))], inc="agl", amt=16)
        strip_free = {0: None, 1: None}
        wo_bank_rd = {}
        xres_free = {}
        x1_free = {}
        for g8 in range(8):
            wtk = {}
            for jh in range(2):
                src = woT[2048 * jh:2048 * (jh + 1), 512 * g8:512 * (g8 + 1)].rearrange(
                    "(j p) c -> p j c", p=128)
                dstw = wb[jh][:].rearrange("p (j c) -> p j c", c=512)
                w = [("smm", T("smm")), ("cca", cca)]
                if strip_free[jh] is not None:
                    w.append(("ojh", strip_free[jh]))
                wtk[jh] = E("gpsimd", lambda e, s=src, d=dstw: e.dma_start(out=d, in_=s),
                            waits=w, inc="wld", amt=16)
            for jh in range(2):
                for tg in range(2):
                    for j16 in range(16):
                        for tt in range(4):
                            t8 = 4 * tg + tt
                            j = 16 * jh + j16
                            st = (jh == 0 and j16 == 0)
                            w2 = []
                            if tg == 0 and j16 == 0 and tt == 0:
                                w2.append(("wld", wtk[jh]))
                                w2.append(("agl", agl))
                                if st and g8 == 0:
                                    w2.append(("aev", T("aev")))
                                    w2.append(("bcc", T("bcc")))
                                    w2.append(("dnl", T("dnl")))
                            if st and j16 == 0 and tt == 0:
                                bt = wo_bank_rd.get(tg)
                                if bt:
                                    w2.append(("x1e", bt))
                            E("tensor", lambda e, b=jh, t8=t8, j16=j16, j=j, st=st:
                              e.matmul(out=PB(t8),
                                       lhsT=big[:, NT * j + 128 * t8:NT * j + 128 * (t8 + 1)],
                                       rhs=wb[b][:, 512 * j16:512 * (j16 + 1)],
                                       start=st, stop=False),
                              waits=w2,
                              inc="ojh" if (tg == 1 and j16 == 15 and tt == 3) else None)
                strip_free[jh] = T("ojh")
            for tg in range(2):
                for tt in range(4):
                    t8 = 4 * tg + tt
                    bm = E("tensor", lambda e, t8=t8, g8=g8:
                           e.matmul(out=PB(t8), lhsT=s_or[:],
                                    rhs=bob[0:1, 512 * g8:512 * (g8 + 1)],
                                    start=False, stop=True),
                           waits=[("brl", brl_bo)], inc="omm")
                    slot = (g8 * 8 + t8) % 2
                    wx = []
                    if xres_free.get(slot) is not None:
                        wx.append(("x1e", xres_free[slot]))
                    xl = E("sync", lambda e, s=slot, t8=t8, g8=g8:
                           e.dma_start(out=xres[s][:],
                                       in_=xs[128 * t8:128 * (t8 + 1),
                                              512 * g8:512 * (g8 + 1)]),
                           waits=wx, inc="xrl", amt=16)
                    w1 = [("omm", bm), ("xrl", xl)]
                    if x1_free.get(slot) is not None:
                        w1.append(("x1s", x1_free[slot]))
                    x1e = E("vector", lambda e, t8=t8, s=slot:
                            e.tensor_add(out=x1st[s][:], in0=PB(t8), in1=xres[s][:]),
                            waits=w1, inc="x1e")
                    xres_free[slot] = x1e
                    if tt == 3:
                        wo_bank_rd[tg] = x1e
                    x1_free[slot] = E("scalar", lambda e, s=slot, t8=t8, g8=g8:
                                      e.dma_start(out=x1d.ap()[128 * t8:128 * (t8 + 1),
                                                               512 * g8:512 * (g8 + 1)],
                                                  in_=x1st[s][:]),
                                      waits=[("x1e", x1e)], inc="x1s", amt=16)

        btl = E("gpsimd", lambda e: e.dma_start(out=wb[0][0:64, 0:4096], in_=bTm),
                waits=[("ojh", T("ojh")), ("omm", T("omm"))], inc="btl", amt=16)
        bTb = wb[0][0:64, 0:4096]
        xn2f = wb[1][:, 0:4096]
        tr_bank = {5: None, 6: None}
        pml_rd = None
        pmlt_rd = None
        pmgT_rd = None
        pd_rd = {3: None, 4: None}
        ost_free = [None] * 2
        xa_free = {0: None, 1: None}
        lnx2_tok = {}
        L = lambda ee: mlt[:, 64 + ee:65 + ee]
        for t8 in range(8):
            wxl = [("x1s", T("x1s"))]
            if lnx2_tok.get(t8 % 2) is not None:
                wxl.append(("lnx", lnx2_tok[t8 % 2]))
            if xa_free[t8 % 2] is not None:
                wxl.append(("ooe", xa_free[t8 % 2]))
            xl = E("sync", lambda e, t=t8: e.dma_start(out=xa[t % 2][:],
                                                       in_=x1d.ap()[128 * t:128 * (t + 1), :]),
                   waits=wxl, inc="xld", amt=16)
            a_x = ln_tile(xa[t8 % 2], xn[t8 % 2], t8 % 2, [("xld", xl)])
            lnx2_tok[t8 % 2] = a_x
            for g in range(4):
                bank = 5 + ((t8 * 4 + g) % 2)
                w = [("lnx", a_x), ("ojh", T("ojh"))]
                if t8 == 0 and g < 2:
                    w.append(("x1e", T("x1e")))
                if tr_bank[bank] is not None:
                    w.append(("tev", tr_bank[bank]))
                last = None
                for j2 in range(8):
                    j = 8 * g + j2
                    last = E("tensor", lambda e, b=bank, j2=j2, j=j, t=t8: e.transpose(
                        out=PBb(b)[:, 128 * j2:128 * (j2 + 1)],
                        in_=xn[t % 2][:, 128 * j:128 * (j + 1)], identity=s_id[:]),
                        waits=(w if j2 == 0 else []), inc=("ttr" if j2 == 7 else None))
                tr_bank[bank] = E("vector", lambda e, b=bank, g=g: e.tensor_copy(
                    out=xn2f[:, 1024 * g:1024 * (g + 1)], in_=PBb(b)[:]),
                    waits=[("ttr", last), ("ojh", T("ojh"))], inc="tev")
            tevt = T("tev")
            for j in range(32):
                w2 = []
                if j == 0:
                    w2 = [("tev", tevt), ("cst", cst_all)]
                    if pml_rd is not None:
                        w2.append(("mle", pml_rd))
                    if t8 == 0:
                        w2.append(("x1e", T("x1e")))
                E("tensor", lambda e, j=j: e.matmul(out=pall[0:68, 0:128],
                                                    lhsT=rhswb[:, 68 * j:68 * (j + 1)],
                                                    rhs=xn2f[:, 128 * j:128 * (j + 1)],
                                                    start=(j == 0), stop=(j == 31)),
                  waits=w2, inc=("mlm" if j == 31 else None))
            pml_rd = E("vector", lambda e: e.tensor_copy(out=sml[0:68, :], in_=pall[0:68, 0:128]),
                       waits=[("mlm", T("mlm"))], inc="mle")
            w3 = [("mle", pml_rd)]
            if pmlt_rd is not None:
                w3.append(("mlte", pmlt_rd))
            if t8 == 0:
                w3.append(("x1e", T("x1e")))
            trm = E("tensor", lambda e: e.transpose(out=PBb(1)[:, 0:68], in_=sml[0:68, :],
                                                    identity=s_id[0:68, 0:68]),
                    waits=w3, inc="mltm")
            pmlt_rd = E("vector", lambda e: e.tensor_copy(out=mlt[:], in_=PBb(1)[:, 0:68]),
                        waits=[("mltm", trm)], inc="mlte")
            fw = [("mlte", pmlt_rd)]
            for e_i in range(4):
                terms = [(a_i, ALU.is_ge if a_i < e_i else ALU.is_gt)
                         for a_i in range(4) if a_i != e_i]
                for k_i, (a_i, op) in enumerate(terms):
                    if k_i == 0:
                        E("vector", lambda e, a=a_i, ei=e_i, op=op: e.tensor_tensor(
                            out=gl[:, ei:ei + 1], in0=L(a), in1=L(ei), op=op), waits=fw,
                          fence=True)
                        fw = []
                    else:
                        E("vector", lambda e, a=a_i, ei=e_i, op=op: e.tensor_tensor(
                            out=gl[:, 4:5], in0=L(a), in1=L(ei), op=op), fence=True)
                        E("vector", lambda e, ei=e_i: e.tensor_add(
                            out=gl[:, ei:ei + 1], in0=gl[:, ei:ei + 1], in1=gl[:, 4:5]),
                          fence=True)
            E("vector", lambda e: e.tensor_max(out=gl[:, 5:6], in0=L(0), in1=L(1)), fence=True)
            E("vector", lambda e: e.tensor_max(out=gl[:, 5:6], in0=gl[:, 5:6], in1=L(2)),
              fence=True)
            E("vector", lambda e: e.tensor_max(out=gl[:, 5:6], in0=gl[:, 5:6], in1=L(3)),
              fence=True)
            vnm = E("vector", lambda e: e.tensor_scalar(out=gl[:, 6:7], in0=gl[:, 5:6],
                                                        scalar1=-1.0, scalar2=None,
                                                        op0=ALU.mult), inc="gnm", fence=True)
            for e_i in range(4):
                E("scalar", lambda e, ei=e_i: e.activation(out=gl[:, 7 + ei:8 + ei], in_=L(ei),
                                                           func=AF.Exp, bias=gl[:, 6:7],
                                                           scale=1.0),
                  waits=([("gnm", vnm)] if e_i == 0 else []),
                  inc=("gee" if e_i == 3 else None))
            gee = T("gee")
            for e_i in range(4):
                E("vector", lambda e, ei=e_i: e.tensor_single_scalar(
                    out=gl[:, 11 + ei:12 + ei], in_=gl[:, ei:ei + 1], scalar=1.5, op=ALU.is_le),
                  waits=([("gee", gee)] if e_i == 0 else []), fence=True)
            for e_i in range(4):
                E("vector", lambda e, ei=e_i: e.tensor_mul(out=gl[:, 15 + ei:16 + ei],
                                                           in0=gl[:, 7 + ei:8 + ei],
                                                           in1=gl[:, 11 + ei:12 + ei]),
                  fence=True)
            E("vector", lambda e: e.tensor_add(out=gl[:, 19:20], in0=gl[:, 15:16],
                                               in1=gl[:, 16:17]), fence=True)
            E("vector", lambda e: e.tensor_add(out=gl[:, 19:20], in0=gl[:, 19:20],
                                               in1=gl[:, 17:18]), fence=True)
            vdn = E("vector", lambda e: e.tensor_add(out=gl[:, 19:20], in0=gl[:, 19:20],
                                                     in1=gl[:, 18:19]), inc="gdn", fence=True)
            E("scalar", lambda e: e.activation(out=gl[:, 20:21], in_=gl[:, 19:20],
                                               func=AF.Ln, bias=0.0, scale=1.0),
              waits=[("gdn", vdn)])
            grg = E("scalar", lambda e: e.activation(out=gl[:, 20:21], in_=gl[:, 20:21],
                                                     func=AF.Exp, bias=0.0, scale=-1.0),
                    inc="grg", fence=True)
            for e_i in range(4):
                E("vector", lambda e, ei=e_i: e.tensor_scalar(
                    out=gl[:, 21 + ei:22 + ei], in0=gl[:, 15 + ei:16 + ei],
                    scalar1=gl[:, 20:21], scalar2=4.0, op0=ALU.mult, op1=ALU.mult),
                  waits=([("grg", grg)] if e_i == 0 else []), fence=True)
            gms = None
            for e_i in range(4):
                gms = E("vector", lambda e, ei=e_i: e.tensor_scalar(
                    out=mg[:, 16 * ei:16 * (ei + 1)], in0=mlt[:, 16 * ei:16 * (ei + 1)],
                    scalar1=gl[:, 21 + ei:22 + ei], scalar2=None, op0=ALU.mult),
                    inc=("gms" if e_i == 3 else None), fence=True)
            gms = T("gms")
            w4 = [("gms", gms)]
            if pmgT_rd is not None:
                w4.append(("mge", pmgT_rd))
            if t8 == 0:
                w4.append(("x1e", T("x1e")))
            E("tensor", lambda e: e.transpose(out=PBb(2)[0:64, 0:128], in_=mg[:, 0:64],
                                              identity=s_id[:]),
              waits=w4, inc="mgm")
            pmgT_rd = E("vector", lambda e: e.tensor_copy(out=mgT[:], in_=PBb(2)[0:64, 0:128]),
                        waits=[("mgm", T("mgm"))], inc="mge")
            mge = pmgT_rd
            last_ooe = None
            for g8 in range(8):
                bank = 3 + (g8 % 2)
                wd = [("mge", mge), ("btl", btl)]
                if pd_rd[bank] is not None:
                    wd.append(("ooe", pd_rd[bank]))
                if t8 == 0 and g8 < 2:
                    wd.append(("x1e", T("x1e")))
                dm = E("tensor", lambda e, bank=bank, g8=g8: e.matmul(
                    out=PB(bank), lhsT=mgT[0:64, :], rhs=bTb[0:64, 512 * g8:512 * (g8 + 1)],
                    start=True, stop=True), waits=wd, inc="dm")
                oslot = (t8 * 8 + g8) % 2
                wo_ = [("dm", dm)]
                if ost_free[oslot] is not None:
                    wo_.append(("outs", ost_free[oslot]))
                ooe = E("vector", lambda e, bank=bank, s=oslot, t=t8, g8=g8: e.tensor_add(
                    out=ost[s][:], in0=PB(bank), in1=xa[t % 2][:, 512 * g8:512 * (g8 + 1)]),
                    waits=wo_, inc="ooe")
                pd_rd[bank] = ooe
                last_ooe = ooe
                ost_free[oslot] = E("scalar", lambda e, s=oslot, t=t8, g8=g8: e.dma_start(
                    out=out[128 * t:128 * (t + 1), 512 * g8:512 * (g8 + 1)], in_=ost[s][:]),
                    waits=[("ooe", ooe)], inc="outs", amt=16)
            xa_free[t8 % 2] = last_ooe

        if taps:
            E("gpsimd", lambda e: e.dma_start(out=tap["ksend"].ap(), in_=k_send.ap()),
              waits=[("kst", T("kst"))], inc="tapd", amt=16)
            E("gpsimd", lambda e: e.dma_start(out=tap["qsend"].ap(), in_=q_send.ap()),
              waits=[("qst", T("qst")), ("mxst", T("mxst"))], inc="tapd", amt=16)
            E("gpsimd", lambda e: e.dma_start(out=tap["vsend"].ap(), in_=v_send.ap()),
              waits=[("vst", T("vst"))], inc="tapd", amt=16)
            E("gpsimd", lambda e: e.dma_start(out=tap["aosend"].ap(), in_=ao_send.ap()),
              waits=[("aos", T("aos"))], inc="tapd", amt=16)
            E("gpsimd", lambda e: e.dma_start(out=tap["aog"].ap(), in_=aogD.ap()),
              waits=[("cca", cca)], inc="tapd", amt=16)
            E("gpsimd", lambda e: e.dma_start(out=tap["x1"].ap(), in_=x1d.ap()),
              waits=[("x1s", T("x1s"))], inc="tapd", amt=16)
            E("gpsimd", lambda e: e.dma_start(out=tap["kg"].ap(), in_=kgD.ap()),
              waits=[("cck", cck)], inc="tapd", amt=16)
            E("gpsimd", lambda e: e.dma_start(out=tap["qg"].ap(), in_=qgD.ap()),
              waits=[("ccq", ccq)], inc="tapd", amt=16)
            E("gpsimd", lambda e: e.dma_start(out=tap["vg"].ap(), in_=vgD.ap()),
              waits=[("ccv", ccv)], inc="tapd", amt=16)
            S.wait_only("gpsimd", [("tapd", T("tapd"))])

        S.wait_only("scalar", [("outs", T("outs"))])

        for chname in S.cnt:
            sems[chname] = ctx.enter_context(nc.semaphore(chname))
        with allow_ici_all_to_all():
            with nc.Block() as block:
                S.replay(block, sems)
    return nc


def _prep_inputs(L0_attn_x, L0_attn_Wq, L0_attn_bq, L0_attn_Wk, L0_attn_bk,
                 L0_attn_Wv, L0_attn_bv, L0_attn_Wo, L0_attn_bo,
                 L0_ffn_A, L0_ffn_B, L0_ffn_Wr):
    x = np.asarray(L0_attn_x, dtype=np.float32)
    C = lambda a: np.ascontiguousarray(np.asarray(a, dtype=np.float32))
    A = np.asarray(L0_ffn_A, dtype=np.float32)
    B = np.asarray(L0_ffn_B, dtype=np.float32)
    Wr = np.asarray(L0_ffn_Wr, dtype=np.float32)
    aT = A.transpose(2, 0, 1).reshape(D, 64)
    wrT = np.asarray(Wr).T
    rhsw = np.concatenate([aT.reshape(32, 128, 64),
                           np.ascontiguousarray(wrT).reshape(32, 128, 4)], axis=2)
    shared = dict(
        wqT=C(np.asarray(L0_attn_Wq).T), wkT=C(np.asarray(L0_attn_Wk).T),
        wvT=C(np.asarray(L0_attn_Wv).T), woT=C(np.asarray(L0_attn_Wo).T),
        bqc=C(np.asarray(L0_attn_bq).reshape(32, 128).T),
        bkc=C(np.asarray(L0_attn_bk).reshape(32, 128).T),
        bvr=C(np.asarray(L0_attn_bv).reshape(1, D)),
        bor=C(np.asarray(L0_attn_bo).reshape(1, D)),
        rhsw=C(rhsw.transpose(1, 0, 2).reshape(128, 32 * 68)),
        bTm=C(B.transpose(0, 2, 1).reshape(64, D)))
    in_maps = []
    for c in range(8):
        g, qc = c // 4, c % 4
        xsl = np.concatenate([x[g, 512 * qc:512 * (qc + 1)],
                              x[g, 512 * (7 - qc):512 * (8 - qc)]], axis=0)
        m = dict(shared)
        m["xs"] = np.ascontiguousarray(xsl)
        in_maps.append(m)
    return in_maps


def kernel(**inputs):
    in_maps = _prep_inputs(**inputs)
    with allow_ici_all_to_all():
        nc = build_nc(taps=_TAPS)
        res = run_bass_kernel_spmd(nc, in_maps, core_ids=list(range(8)))
    outf = np.zeros((2, D, D), dtype=np.float32)
    for c in range(8):
        g, qc = c // 4, c % 4
        o = res.results[c]["out"]
        outf[g, 512 * qc:512 * (qc + 1)] = o[0:512]
        outf[g, 512 * (7 - qc):512 * (8 - qc)] = o[512:1024]
    if _TAPS:
        kernel._tap_results = res.results
    return outf


# version 04
# speedup vs baseline: 1.3479x; 1.2306x over previous
import math
from contextlib import ExitStack

import numpy as np
import ml_dtypes

import concourse.bass as bass
import concourse.mybir as mybir
from concourse.bass_utils import run_bass_kernel_spmd
from concourse.aot_env import allow_ici_all_to_all

F32 = mybir.dt.float32
BF16 = mybir.dt.bfloat16
AF = mybir.ActivationFunctionType
ALU = mybir.AluOpType
AXX = mybir.AxisListType.X

D = 4096
NT = 1024
NP = 8
SCLQ = 1.0 / math.sqrt(128.0)
EPS = 1e-5
_TAPS = False

ENGS = ("sync", "gpsimd", "scalar", "vector", "tensor")


def _rho(B):
    return B if B < 4 else 7 - B


def _co(B):
    return 0 if B < 4 else 512


class Sched:
    def __init__(self):
        self.ops = {e: [] for e in ENGS}
        self.cnt = {}
        self._pend = {e: [] for e in ENGS}

    def tok(self, ch):
        return self.cnt.get(ch, 0)

    def emit(self, eng, fn, waits=(), inc=None, amt=1, fence=False):
        if fence:
            self.fence(eng)
        waits = list(waits) + self._pend[eng]
        self._pend[eng] = []
        if inc is not None:
            self.cnt[inc] = self.cnt.get(inc, 0) + amt
        self.ops[eng].append((tuple((c, v) for c, v in waits if v), fn, inc, amt))
        return self.cnt.get(inc, 0)

    def fence(self, eng):
        if not self.ops[eng]:
            return
        w, fn, inc, amt = self.ops[eng][-1]
        if fn is None:
            return
        if inc is None:
            ch = "sf_" + eng
            self.cnt[ch] = self.cnt.get(ch, 0) + 1
            self.ops[eng][-1] = (w, fn, ch, 1)
            self._pend[eng].append((ch, self.cnt[ch]))
        else:
            self._pend[eng].append((inc, self.cnt[inc]))

    def wait_only(self, eng, waits):
        self.ops[eng].append((tuple(waits), None, None, 0))

    def replay(self, block, sems):
        def mk(ename):
            def body(eng):
                seen = {}
                for waits, fn, inc, amt in self.ops[ename]:
                    for ch, val in waits:
                        if val > seen.get(ch, 0):
                            eng.wait_ge(sems[ch], val)
                            seen[ch] = val
                    if fn is not None:
                        ins = fn(eng)
                        if inc is not None:
                            ins.then_inc(sems[inc], amt)
            return body

        block.sync(mk("sync"))
        block.gpsimd(mk("gpsimd"))
        block.scalar(mk("scalar"))
        block.vector(mk("vector"))
        block.tensor(mk("tensor"))


def build_nc(taps=False):
    nc = bass.Bass("TRN2", target_bir_lowering=False)

    xs = nc.dram_tensor("xs", [NT, D], F32, kind="ExternalInput").ap()
    wqT = nc.dram_tensor("wqT", [D, D], F32, kind="ExternalInput").ap()
    wkT = nc.dram_tensor("wkT", [D, D], F32, kind="ExternalInput").ap()
    wvT = nc.dram_tensor("wvT", [D, D], F32, kind="ExternalInput").ap()
    woT = nc.dram_tensor("woT", [D, D], F32, kind="ExternalInput").ap()
    bqc = nc.dram_tensor("bqc", [128, 32], F32, kind="ExternalInput").ap()
    bkc = nc.dram_tensor("bkc", [128, 32], F32, kind="ExternalInput").ap()
    bvr = nc.dram_tensor("bvr", [1, D], F32, kind="ExternalInput").ap()
    bor = nc.dram_tensor("bor", [1, D], F32, kind="ExternalInput").ap()
    rhsw = nc.dram_tensor("rhsw", [128, 32 * 68], F32, kind="ExternalInput").ap()
    bTm = nc.dram_tensor("bTm", [64, D], F32, kind="ExternalInput").ap()
    out = nc.dram_tensor("out", [NT, D], F32, kind="ExternalOutput").ap()

    ident = nc.inline_tensor(np.eye(128, dtype=ml_dtypes.bfloat16), "ident")
    ones_col = nc.inline_tensor(np.ones((128, 1), dtype=ml_dtypes.bfloat16), "ones_col")
    ones_row = nc.inline_tensor(np.ones((1, 128), dtype=ml_dtypes.bfloat16), "ones_row")
    ones_row_f = nc.inline_tensor(np.ones((1, 128), dtype=np.float32), "ones_row_f")
    mknp = np.zeros((128, 4 * 512), dtype=ml_dtypes.bfloat16)
    pp = np.arange(128)[:, None]
    jj_ = np.arange(512)[None, :]
    for u in range(4):
        mknp[:, 512 * u:512 * (u + 1)] = ((128 * u + pp) <= jj_).astype(ml_dtypes.bfloat16)
    maskt = nc.inline_tensor(mknp, "maskt")

    k_send = nc.dram_tensor("k_send", [8 * 512, NT], BF16, kind="Internal")
    q_send = nc.dram_tensor("q_send", [8 * 512, NT], BF16, kind="Internal")
    v_send = nc.dram_tensor("v_send", [8 * 1024, 512], BF16, kind="Internal")
    kgD = nc.dram_tensor("kgD", [8 * 512, NT], BF16, kind="Internal")
    qgD = nc.dram_tensor("qgD", [8 * 512, NT], BF16, kind="Internal")
    vgD = nc.dram_tensor("vgD", [8 * 1024, 512], BF16, kind="Internal")
    ao_send = nc.dram_tensor("ao_send", [8 * 512, NT], BF16, kind="Internal")
    aogD = nc.dram_tensor("aogD", [8 * 512, NT], BF16, kind="Internal")
    x1d = nc.dram_tensor("x1d", [NT, D], F32, kind="Internal")
    tap = {}
    if taps:
        for n, sh, dt in [("ksend", (8 * 512, NT), BF16), ("qsend", (8 * 512, NT), BF16),
                          ("vsend", (8 * 1024, 512), BF16), ("aosend", (8 * 512, NT), BF16),
                          ("aog", (8 * 512, NT), BF16), ("x1", (NT, D), F32),
                          ("kg", (8 * 512, NT), BF16), ("qg", (8 * 512, NT), BF16),
                          ("vg", (8 * 1024, 512), BF16)]:
            tap[n] = nc.dram_tensor("tap_" + n, list(sh), dt, kind="ExternalOutput")

    S = Sched()
    E = S.emit
    T = S.tok

    with ExitStack() as ctx:
        SB = lambda n, sh, dt: ctx.enter_context(nc.sbuf_tensor(n, sh, dt))
        big = SB("big", [128, 32 * NT], BF16)
        wb = [SB(f"wb{i}", [128, 8192], BF16) for i in range(2)]
        xa = [SB(f"xa{i}", [128, D], F32) for i in range(2)]
        xn = [SB(f"xn{i}", [128, D], BF16) for i in range(2)]
        dump = SB("dump", [128, D], BF16)
        kst = [SB(f"kst{i}", [128, 512], BF16) for i in range(4)]
        k2t = [SB(f"k2t{i}", [128, 512], BF16) for i in range(2)]
        attT = [SB(f"attT{i}", [128, 512], BF16) for i in range(4)]
        rbc = [SB(f"rbc{i}", [128, 512], F32) for i in range(2)]
        aost = [SB(f"aost{i}", [128, 512], BF16) for i in range(3)]
        xres = [SB(f"xres{i}", [128, 512], F32) for i in range(2)]
        x1st = [SB(f"x1st{i}", [128, 512], F32) for i in range(2)]
        ost = [SB(f"ost{i}", [128, 512], F32) for i in range(2)]
        sml = SB("sml", [128, 128], BF16)
        mlt = SB("mlt", [128, 68], BF16)
        mg = SB("mg", [128, 64], BF16)
        mgT = SB("mgT", [64, 128], BF16)
        gl = SB("gl", [128, 28], F32)
        s_id = SB("s_id", [128, 128], BF16)
        s_msk = SB("s_msk", [128, 4 * 512], BF16)
        s_oc = SB("s_oc", [128, 1], BF16)
        s_or = SB("s_or", [1, 128], BF16)
        s_orf = SB("s_orf", [1, 128], F32)
        s_bq = SB("s_bq", [128, 32], F32)
        s_bqs = SB("s_bqs", [128, 32], F32)
        s_bk = SB("s_bk", [128, 32], F32)
        rhswb = SB("rhswb", [128, 32 * 68], BF16)
        urow = SB("urow", [128, 2048], F32)
        mqct = SB("mqct", [128, 8], BF16)
        mqt = SB("mqt", [128, 1], BF16)
        nmqt = SB("nmqt", [1, 1024], BF16)
        stat = SB("stat", [128, 16], F32)
        epsc = SB("epsc", [128, 1], F32)
        pall = ctx.enter_context(nc.psum_tensor("pall", [128, 4096], F32))
        sems = {}

        def PB(b):
            return pall[:, 512 * b:512 * (b + 1)]

        def PBb(b):
            return pall[:, 512 * b:512 * (b + 1)].bitcast(BF16)

        bvb = dump[0:1, :]
        bob = dump[0:1, :]
        r_t = urow[0:1, 0:512]
        r_rd = [urow[0:1, 512:1024], urow[0:1, 1024:1536]]
        r_ksq = urow[0:1, 1536:1568]
        r_qsq = urow[0:1, 1568:1600]
        r_m2 = urow[0:1, 1600:1608]
        r_mro = urow[0:1, 1608:1616]
        r_nM = urow[0:1, 1616:1624]
        r_mx1 = urow[0:1, 1624:1632]

        for dst, src in [(s_id[:], ident.ap()), (s_msk[:], maskt.ap()),
                         (s_oc[:], ones_col.ap()), (s_or[:], ones_row.ap()),
                         (s_orf[:], ones_row_f.ap()), (s_bq[:], bqc), (s_bk[:], bkc)]:
            E("sync", lambda e, d=dst, s=src: e.dma_start(out=d, in_=s), inc="cst", amt=16)
        for dst, src in [(rhswb[:], rhsw)]:
            E("gpsimd", lambda e, d=dst, s=src: e.dma_start(out=d, in_=s), inc="cst", amt=16)
        cst_all = T("cst")
        E("vector", lambda e: e.tensor_scalar(out=s_bqs[:], in0=s_bq[:], scalar1=SCLQ,
                                              scalar2=None, op0=ALU.mult),
          waits=[("cst", cst_all)], inc="prep")
        E("vector", lambda e: e.memset(urow[0:1, 1536:1600], -3.0e38), inc="prep")
        prep_all = E("vector", lambda e: e.memset(epsc[:], EPS), inc="prep")

        def ln_tile(xa_t, xn_t, par, wfirst):
            o = 8 * par
            c_s1, c_s2, c_mean, c_var, c_ms, c_rstd, c_nmr, c_m2 = [
                stat[:, o + i:o + i + 1] for i in range(8)]
            E("vector", lambda e: e.tensor_scalar(out=dump[:], in0=xa_t[:], scalar1=1.0,
                                                  scalar2=0.0, op0=ALU.mult, op1=ALU.add,
                                                  accum_out=c_s1),
              waits=wfirst)
            E("vector", lambda e: e.scalar_tensor_tensor(out=dump[:], in0=xa_t[:], scalar=1.0,
                                                         in1=xa_t[:], op0=ALU.mult, op1=ALU.mult,
                                                         accum_out=c_s2))
            E("vector", lambda e: e.tensor_scalar(out=c_mean, in0=c_s1, scalar1=1.0 / D,
                                                  scalar2=None, op0=ALU.mult), fence=True)
            E("vector", lambda e: e.scalar_tensor_tensor(out=c_var, in0=c_s1, scalar=c_mean,
                                                         in1=c_s2, op0=ALU.mult,
                                                         op1=ALU.subtract), fence=True)
            v_var = E("vector", lambda e: e.tensor_scalar(out=c_var, in0=c_var,
                                                          scalar1=-1.0 / D, scalar2=None,
                                                          op0=ALU.mult), inc="lnv", fence=True)
            E("scalar", lambda e: e.activation(out=c_rstd, in_=c_var, func=AF.Ln,
                                               bias=epsc[:], scale=1.0),
              waits=[("lnv", v_var), ("prep", prep_all)])
            a_r = E("scalar", lambda e: e.activation(out=c_rstd, in_=c_rstd, func=AF.Exp,
                                                     bias=0.0, scale=-0.5), inc="lnr",
                    fence=True)
            v_n = E("vector", lambda e: e.tensor_scalar(out=c_nmr, in0=c_mean, scalar1=c_rstd,
                                                        scalar2=-1.0, op0=ALU.mult, op1=ALU.mult),
                    waits=[("lnr", a_r)], inc="lnn")
            a_x = E("scalar", lambda e: e.activation(out=xn_t[:], in_=xa_t[:], func=AF.Identity,
                                                     bias=c_nmr, scale=c_rstd),
                    waits=[("lnn", v_n)], inc="lnx")
            return a_x

        tev_bank = {0: 0, 1: 0}
        lnx_tok = {}
        for t8 in range(8):
            wx = [("lnx", lnx_tok[t8 % 2])] if t8 >= 2 else []
            xl = E("sync", lambda e, t=t8: e.dma_start(out=xa[t % 2][:],
                                                       in_=xs[128 * t:128 * (t + 1), :]),
                   waits=wx, inc="xld", amt=16)
            a_x = ln_tile(xa[t8 % 2], xn[t8 % 2], t8 % 2,
                          [("xld", xl), ("cst", cst_all)])
            lnx_tok[t8 % 2] = a_x
            for g in range(4):
                bank = (t8 * 4 + g) % 2
                w = [("lnx", a_x)]
                if tev_bank[bank]:
                    w.append(("tev", tev_bank[bank]))
                last = None
                for j2 in range(8):
                    j = 8 * g + j2
                    last = E("tensor", lambda e, b=bank, j2=j2, j=j, t=t8: e.transpose(
                        out=PBb(b)[:, 128 * j2:128 * (j2 + 1)],
                        in_=xn[t % 2][:, 128 * j:128 * (j + 1)], identity=s_id[:]),
                        waits=(w if j2 == 0 else []), inc=("ttr" if j2 == 7 else None))
                dst = big[:].rearrange("p (j c) -> p j c", c=NT)[:, 8 * g:8 * (g + 1),
                                                                128 * t8:128 * (t8 + 1)]
                src = PBb(bank)[:].rearrange("p (j c) -> p j c", c=128)
                tev_bank[bank] = E("vector", lambda e, d=dst, s=src: e.tensor_copy(out=d, in_=s),
                                   waits=[("ttr", last)], inc="tev")
        tev_all = T("tev")
        brl_all = E("gpsimd", lambda e: e.dma_start(out=bvb, in_=bvr),
                    waits=[("lnv", T("lnv"))], inc="brl", amt=16)

        slotg = {}

        def proj_phase(wT, pref, bias_t, scale, send, entry_guard):
            strip_free = {0: None, 1: None}
            for hg in range(8):
                ev_before = T(pref + "ev")
                wtk = {}
                for jh in range(2):
                    buf = jh
                    src = wT[2048 * jh:2048 * (jh + 1), 512 * hg:512 * (hg + 1)].rearrange(
                        "(j p) c -> p j c", p=128)
                    dstw = wb[buf][:].rearrange("p (j c) -> p j c", c=512)
                    w = [("tev", tev_all)]
                    if strip_free[buf] is not None:
                        w.append((pref + "mm", strip_free[buf]))
                    if pref == "q":
                        w.append(("kmm", T("kmm")))
                    wtk[jh] = E("gpsimd", lambda e, s=src, d=dstw: e.dma_start(out=d, in_=s),
                                waits=w, inc="wld", amt=16)
                for jh in range(2):
                    for hh in range(4):
                        for j16 in range(16):
                            for half in range(2):
                                j = 16 * jh + j16
                                st = (jh == 0 and j16 == 0)
                                sp = (jh == 1 and j16 == 15)
                                w2 = []
                                if hh == 0 and j16 == 0 and half == 0:
                                    w2.append(("wld", wtk[jh]))
                                if jh == 0 and j16 == 0 and half == 0:
                                    bt = S.bank_ev.get((pref, hh))
                                    if bt:
                                        w2.append((pref + "ev", bt))
                                    if hg == 0:
                                        w2.extend(entry_guard)
                                E("tensor",
                                  lambda e, b=jh, hh=hh, j16=j16, j=j, half=half, st=st, sp=sp:
                                  e.matmul(out=pall[:, 1024 * hh + 512 * half:
                                                    1024 * hh + 512 * half + 512],
                                           lhsT=wb[b][:, 512 * j16 + 128 * hh:
                                                      512 * j16 + 128 * (hh + 1)],
                                           rhs=big[:, NT * j + 512 * half:NT * j + 512 * half + 512],
                                           start=st, stop=sp),
                                  waits=w2,
                                  inc=(pref + "mm") if (j16 == 15 and half == 1)
                                  else None)
                            if jh == 1 and j16 == 15:
                                mmt = T(pref + "mm")
                                h = 4 * hg + hh
                                evl = None
                                for half2 in range(2):
                                    slot = (hg * 8 + hh * 2 + half2) % 4
                                    wv = [(pref + "mm", mmt), ("prep", prep_all)] +                                         slotg.get(slot, [])
                                    ev = E("scalar", lambda e, hh=hh, half=half2, h=h, s=slot:
                                           e.activation(out=kst[s][:],
                                                        in_=pall[:, 1024 * hh + 512 * half:
                                                                 1024 * hh + 512 * half + 512],
                                                        func=AF.Identity,
                                                        bias=bias_t[:, h:h + 1],
                                                        scale=scale),
                                           waits=wv, inc=pref + "ev")
                                    evl = ev
                                    r0 = 512 * (h // 4) + 128 * (h % 4)
                                    stk = E("scalar", lambda e, s=slot, r0=r0, half=half2:
                                            e.dma_start(out=send.ap()[r0:r0 + 128,
                                                                      512 * half:
                                                                      512 * (half + 1)],
                                                        in_=kst[s][:]),
                                            waits=[(pref + "ev", ev)],
                                            inc=pref + "st", amt=16)
                                    slotg[slot] = [(pref + "st", stk)]
                                S.bank_ev[(pref, hh)] = evl
                    strip_free[jh] = T(pref + "mm")

        S.bank_ev = {}
        proj_phase(wkT, "k", s_bk, 1.0, k_send, [])

        proj_phase(wqT, "q", s_bqs, SCLQ, q_send, [("kev", T("kev"))])

        strip_free = {0: None, 1: None}
        vbank_rd = {}
        for g8 in range(8):
            wtk = {}
            for jh in range(2):
                src = wvT[2048 * jh:2048 * (jh + 1), 512 * g8:512 * (g8 + 1)].rearrange(
                    "(j p) c -> p j c", p=128)
                dstw = wb[jh][:].rearrange("p (j c) -> p j c", c=512)
                w = [("qmm", T("qmm"))]
                if strip_free[jh] is not None:
                    w.append(("vjh", strip_free[jh]))
                wtk[jh] = E("gpsimd", lambda e, s=src, d=dstw: e.dma_start(out=d, in_=s),
                            waits=w, inc="wld", amt=16)
            for jh in range(2):
                for tg in range(2):
                    for j16 in range(16):
                        for tt in range(4):
                            t8 = 4 * tg + tt
                            j = 16 * jh + j16
                            st = (jh == 0 and j16 == 0)
                            w2 = []
                            if tg == 0 and j16 == 0 and tt == 0:
                                w2.append(("wld", wtk[jh]))
                                if st:
                                    w2.append(("qev", T("qev")))
                            if st and j16 == 0 and tt == 0:
                                bt = vbank_rd.get(tg)
                                if bt:
                                    w2.append(("vev", bt))
                            E("tensor", lambda e, b=jh, t8=t8, j16=j16, j=j, st=st:
                              e.matmul(out=PB(t8),
                                       lhsT=big[:, NT * j + 128 * t8:NT * j + 128 * (t8 + 1)],
                                       rhs=wb[b][:, 512 * j16:512 * (j16 + 1)],
                                       start=st, stop=False),
                              waits=w2,
                              inc="vjh" if (tg == 1 and j16 == 15 and tt == 3) else None)
                strip_free[jh] = T("vjh")
            for tg in range(2):
                for tt in range(4):
                    t8 = 4 * tg + tt
                    bm = E("tensor", lambda e, t8=t8, g8=g8:
                           e.matmul(out=PB(t8), lhsT=s_or[:],
                                    rhs=bvb[0:1, 512 * g8:512 * (g8 + 1)],
                                    start=False, stop=True),
                           waits=[("cst", cst_all), ("brl", brl_all)], inc="vmm")
                    slot = (g8 * 8 + t8) % 4
                    wv = [("vmm", bm)] + slotg.get(slot, [])
                    ev = E("vector", lambda e, t8=t8, s=slot: e.tensor_copy(out=kst[s][:],
                                                                            in_=PB(t8)),
                           waits=wv, inc="vev")
                    if tt == 3:
                        vbank_rd[tg] = ev
                    stk = E("scalar", lambda e, s=slot, g8=g8, t8=t8:
                            e.dma_start(out=v_send.ap()[1024 * g8 + 128 * t8:
                                                        1024 * g8 + 128 * (t8 + 1), :],
                                        in_=kst[s][:]),
                            waits=[("vev", ev)], inc="vst", amt=16)
                    slotg[slot] = [("vst", stk)]
        brl_bo = E("gpsimd", lambda e: e.dma_start(out=bob, in_=bor),
                   waits=[("vmm", T("vmm"))], inc="brl", amt=16)

        cck = E("gpsimd", lambda e: e.collective_compute(
            "AllToAll", ALU.bypass, replica_groups=[list(range(8))],
            ins=[k_send.ap().opt()], outs=[kgD.ap().opt()]),
            waits=[("kst", T("kst")), ("qst", T("qst")), ("vst", T("vst")),
                   ("qmm", T("qmm")), ("vmm", T("vmm")), ("xld", T("xld")),
                   ("cst", cst_all), ("brl", T("brl")), ("wld", T("wld"))], inc="cck")
        ccq = E("gpsimd", lambda e: e.collective_compute(
            "AllToAll", ALU.bypass, replica_groups=[list(range(8))],
            ins=[q_send.ap().opt()], outs=[qgD.ap().opt()]),
            waits=[("cck", cck)], inc="ccq")
        ccv = E("gpsimd", lambda e: e.collective_compute(
            "AllToAll", ALU.bypass, replica_groups=[list(range(8))],
            ins=[v_send.ap().opt()], outs=[vgD.ap().opt()]),
            waits=[("ccq", ccq)], inc="ccv")

        def kres(par):
            return big[:, 12288 * par:12288 * par + 4096]

        def vres(par):
            return big[:, 12288 * par + 4096:12288 * par + 8192]

        def qres(par):
            return big[:, 12288 * par + 8192:12288 * par + 12288]

        mqc = [mqct[:, 0:4], mqct[:, 4:8]]
        nmq = [nmqt[0:1, 0:512], nmqt[0:1, 512:1024]]
        mxps = [pall[:, 2560:2688], pall[:, 3072:3200]]
        pden = pall[0:1, 2048:2560]
        pbc = PB(4)
        sc_tok = {0: None, 1: None, 7: None}
        av_tok = {2: None, 3: None}
        attT_tok = [None] * 4
        rbc_tok = [None, None]
        aost_tok = [None] * 3
        mx_tok = [None, None]
        den_free = None
        ld_prev = {}
        sci = 0
        ui = 0
        for p in range(NP):
            b, hh = p // 4, p % 4
            par = p % 2
            w0 = [("cck", cck), ("ccq", ccq), ("ccv", ccv)]
            if ld_prev.get(par) is not None:
                w0.append(("dnm", ld_prev[par]))
            lds = 0
            for B in range(8):
                s = 4 * b + _rho(B)
                co = _co(B)
                E("sync", lambda e, s=s, co=co, hh=hh, B=B, par=par:
                  e.dma_start(out=kres(par)[:, 512 * B:512 * (B + 1)],
                              in_=kgD.ap()[512 * s + 128 * hh:512 * s + 128 * (hh + 1),
                                           co:co + 512]),
                  waits=(w0 if B == 0 else []), inc="p2ld", amt=16)
                E("sync", lambda e, s=s, co=co, hh=hh, B=B, par=par:
                  e.dma_start(out=vres(par)[:, 512 * B:512 * (B + 1)].rearrange(
                      "p (u c) -> p u c", c=128),
                      in_=vgD.ap()[1024 * s + co:1024 * s + co + 512,
                                   128 * hh:128 * (hh + 1)].rearrange("(u p) c -> p u c", p=128)),
                  inc="p2ld", amt=16)
                lds = E("sync", lambda e, s=s, co=co, hh=hh, B=B, par=par:
                        e.dma_start(out=qres(par)[:, 512 * B:512 * (B + 1)],
                                    in_=qgD.ap()[512 * s + 128 * hh:512 * s + 128 * (hh + 1),
                                                 co:co + 512]),
                        inc="p2ld", amt=16)

            def mx_minis(P, qt):
                for q4 in range(4):
                    for B in range(P + 1):
                        yield (P, q4, B)
                yield None

            def emit_mx_mini(item, qt):
                if item is None:
                    emit_mx_tail(qt)
                    return
                Pm, q4, B = item
                if True:
                    if True:
                        mi = (S.cnt.get("_mxi", 0)) % 2
                        S.cnt["_mxi"] = S.cnt.get("_mxi", 0) + 1
                        wmx = [("p2ld", lds)]
                        if mx_tok[mi] is not None:
                            wmx.append(("mxr", mx_tok[mi]))
                        mm = E("tensor", lambda e, mi=mi, P=Pm, q4=q4, B=B, par=par:
                               e.matmul(out=mxps[mi],
                                        lhsT=qres(par)[:, 512 * P + 128 * q4:
                                                       512 * P + 128 * (q4 + 1)],
                                        rhs=kres(par)[:, 512 * B:512 * B + 128],
                                        start=True, stop=True),
                               waits=wmx, inc="mxm")
                        rdt = E("vector", lambda e, mi=mi: e.reduce_max(
                            out=mqt[:, 0:1], in_=mxps[mi], axis=AXX),
                            waits=[("mxm", mm)], inc="mxr", fence=True)
                        if B == 0:
                            E("vector", lambda e, qt=qt, q4=q4: e.tensor_copy(
                                out=mqc[qt][:, q4:q4 + 1], in_=mqt[:, 0:1]),
                              inc="mqv", fence=True)
                        else:
                            E("vector", lambda e, qt=qt, q4=q4: e.tensor_max(
                                out=mqc[qt][:, q4:q4 + 1], in0=mqc[qt][:, q4:q4 + 1],
                                in1=mqt[:, 0:1]), inc="mqv", fence=True)
                        mx_tok[mi] = rdt

            def emit_mx_tail(qt):
                for q4 in range(4):
                    scb = [0, 1, 7][(S.cnt.get("_sci", 0)) % 3]
                    S.cnt["_sci"] = S.cnt.get("_sci", 0) + 1
                    wtp = [("mqv", T("mqv"))]
                    if sc_tok[scb] is not None:
                        wtp.append(sc_tok[scb])
                    tp = E("tensor", lambda e, scb=scb, qt=qt, q4=q4: e.transpose(
                        out=PBb(scb)[0:1, 0:128], in_=mqc[qt][:, q4:q4 + 1],
                        identity=s_id[:]),
                        waits=wtp, inc="mtp")
                    ng = E("vector", lambda e, scb=scb, qt=qt, q4=q4: e.tensor_scalar(
                        out=nmq[qt][0:1, 128 * q4:128 * (q4 + 1)], in0=PBb(scb)[0:1, 0:128],
                        scalar1=-1.0, scalar2=None, op0=ALU.mult),
                        waits=[("mtp", tp)], inc="nmg")
                    sc_tok[scb] = ("nmg", ng)

            def units(P, qt, nmg_tok, minis, mqt_next):
                nonlocal ui, den_free
                avb = 2 + qt
                pend = None
                for B in range(P + 1):
                    for u in range(4):
                        if minis is not None:
                            item = next(minis, "done")
                            if item != "done":
                                emit_mx_mini(item, mqt_next)
                        first = (B == 0 and u == 0)
                        lastu = (B == P and u == 3)
                        scb = [0, 1, 7][(S.cnt.get("_sci", 0)) % 3]
                        S.cnt["_sci"] = S.cnt.get("_sci", 0) + 1
                        ats = ui % 4
                        wsc = [("p2ld", lds), ("nmg", nmg_tok)]
                        if sc_tok[scb] is not None:
                            wsc.append(sc_tok[scb])
                        smm = E("tensor", lambda e, scb=scb, B=B, u=u, P=P, par=par:
                                e.matmul(out=PB(scb),
                                         lhsT=kres(par)[:, 512 * B + 128 * u:
                                                        512 * B + 128 * (u + 1)],
                                         rhs=qres(par)[:, 512 * P:512 * (P + 1)],
                                         start=True, stop=False),
                                waits=wsc, inc="smm")
                        smm = E("tensor", lambda e, scb=scb, qt=qt:
                                e.matmul(out=PB(scb), lhsT=s_or[:],
                                         rhs=nmq[qt][0:1, :], start=False, stop=True),
                                inc="smm")
                        wxp = [("smm", smm)]
                        if attT_tok[ats] is not None:
                            wxp.append(("dnm", attT_tok[ats]))
                        sxp = E("scalar", lambda e, scb=scb, a=ats:
                                e.activation(out=attT[a][:], in_=PB(scb), func=AF.Exp,
                                             bias=0.0, scale=1.0),
                                waits=wxp, inc="sxp")
                        sc_tok[scb] = ("sxp", sxp)
                        gate = ("sxp", sxp)
                        if B == P:
                            mku = E("vector", lambda e, a=ats, u=u:
                                    e.tensor_mul(out=attT[a][:], in0=attT[a][:],
                                                 in1=s_msk[:, 512 * u:512 * (u + 1)]),
                                    waits=[("sxp", sxp)], inc="msk")
                            gate = ("msk", mku)
                        def mk_avden(gate=gate, avb=avb, B=B, u=u, par=par, a=ats,
                                     first=first, lastu=lastu):
                            wav = [gate]
                            if first and av_tok[avb] is not None:
                                wav.append(("aev", av_tok[avb]))
                            E("tensor", lambda e:
                              e.matmul(out=PB(avb),
                                       lhsT=vres(par)[:, 512 * B + 128 * u:
                                                      512 * B + 128 * (u + 1)],
                                       rhs=attT[a][:], start=first, stop=lastu),
                              waits=wav, inc="avm")
                            wdn = []
                            if first and den_free is not None:
                                wdn.append(den_free)
                            dnm = E("tensor", lambda e:
                                    e.matmul(out=pden, lhsT=s_oc[:], rhs=attT[a][:],
                                             start=first, stop=lastu),
                                    waits=wdn, inc="dnm")
                            attT_tok[a] = dnm
                        if pend is not None:
                            pend()
                        pend = mk_avden
                        ui += 1
                if pend is not None:
                    pend()
                if minis is not None:
                    for item in minis:
                        emit_mx_mini(item, mqt_next)
                dl1 = E("scalar", lambda e: e.activation(
                    out=r_t[:], in_=pden, func=AF.Ln, bias=0.0, scale=1.0),
                    waits=[("dnm", T("dnm"))], inc="dnl")
                rd = E("scalar", lambda e, qt=qt: e.activation(out=r_rd[qt][:], in_=r_t[:],
                                                               func=AF.Exp, bias=0.0,
                                                               scale=-1.0),
                       inc="rdn", fence=True)
                bcm = E("tensor", lambda e: e.matmul(out=pbc, lhsT=s_orf[:],
                                                     rhs=r_rd[qt][:],
                                                     start=True, stop=True),
                        waits=[("rdn", rd), ("dnl", dl1)], inc="bcm")
                wcc = [("bcm", bcm)]
                if rbc_tok[qt] is not None:
                    wcc.append(("aev", rbc_tok[qt]))
                bcc = E("vector", lambda e, qt=qt: e.tensor_copy(out=rbc[qt][:], in_=pbc),
                        waits=wcc, inc="bcc")
                den_free = ("bcc", bcc)
                aslot = (p * 8 + P) % 3
                wev = [("bcc", bcc)]
                if aost_tok[aslot] is not None:
                    wev.append(("aos", aost_tok[aslot]))
                aev = E("vector", lambda e, avb=avb, qt=qt, s=aslot:
                        e.tensor_tensor(out=aost[s][:], in0=PB(avb), in1=rbc[qt][:],
                                        op=ALU.mult),
                        waits=wev, inc="aev")
                av_tok[avb] = aev
                rbc_tok[qt] = aev
                d = 4 * b + _rho(P)
                co = _co(P)
                aost_tok[aslot] = E("scalar", lambda e, s=aslot, d=d, co=co, hh=hh:
                                    e.dma_start(out=ao_send.ap()[
                                        512 * d + 128 * hh:512 * d + 128 * (hh + 1),
                                        co:co + 512], in_=aost[s][:]),
                                    waits=[("aev", aev)], inc="aos", amt=16)

            for item in mx_minis(0, 0):
                emit_mx_mini(item, 0)
            nmg_tok = T("nmg")
            for P in range(8):
                minis = mx_minis(P + 1, (P + 1) % 2) if P < 7 else None
                units(P, P % 2, nmg_tok, minis, (P + 1) % 2)
                nmg_tok = T("nmg")
            ld_prev[par] = T("dnm")

        cca = E("gpsimd", lambda e: e.collective_compute(
            "AllToAll", ALU.bypass, replica_groups=[list(range(8))],
            ins=[ao_send.ap().opt()], outs=[aogD.ap().opt()]),
            waits=[("aos", T("aos")), ("p2ld", T("p2ld")), ("kst", T("kst")),
                   ("qst", T("qst")), ("vst", T("vst"))], inc="cca")

        agl = E("sync", lambda e: e.dma_start(
            out=big[:].rearrange("p (j c) -> p j c", c=NT),
            in_=aogD.ap().rearrange("(j p) c -> p j c", p=128)),
            waits=[("cca", cca), ("dnm", T("dnm")), ("smm", T("smm"))], inc="agl", amt=16)
        strip_free = {0: None, 1: None}
        wo_bank_rd = {}
        xres_free = {}
        x1_free = {}
        for g8 in range(8):
            wtk = {}
            for jh in range(2):
                src = woT[2048 * jh:2048 * (jh + 1), 512 * g8:512 * (g8 + 1)].rearrange(
                    "(j p) c -> p j c", p=128)
                dstw = wb[jh][:].rearrange("p (j c) -> p j c", c=512)
                w = [("smm", T("smm")), ("cca", cca)]
                if strip_free[jh] is not None:
                    w.append(("ojh", strip_free[jh]))
                wtk[jh] = E("gpsimd", lambda e, s=src, d=dstw: e.dma_start(out=d, in_=s),
                            waits=w, inc="wld", amt=16)
            for jh in range(2):
                for tg in range(2):
                    for j16 in range(16):
                        for tt in range(4):
                            t8 = 4 * tg + tt
                            j = 16 * jh + j16
                            st = (jh == 0 and j16 == 0)
                            w2 = []
                            if tg == 0 and j16 == 0 and tt == 0:
                                w2.append(("wld", wtk[jh]))
                                w2.append(("agl", agl))
                                if st and g8 == 0:
                                    w2.append(("aev", T("aev")))
                                    w2.append(("bcc", T("bcc")))
                                    w2.append(("dnl", T("dnl")))
                            if st and j16 == 0 and tt == 0:
                                bt = wo_bank_rd.get(tg)
                                if bt:
                                    w2.append(("x1e", bt))
                            E("tensor", lambda e, b=jh, t8=t8, j16=j16, j=j, st=st:
                              e.matmul(out=PB(t8),
                                       lhsT=big[:, NT * j + 128 * t8:NT * j + 128 * (t8 + 1)],
                                       rhs=wb[b][:, 512 * j16:512 * (j16 + 1)],
                                       start=st, stop=False),
                              waits=w2,
                              inc="ojh" if (tg == 1 and j16 == 15 and tt == 3) else None)
                strip_free[jh] = T("ojh")
            for tg in range(2):
                for tt in range(4):
                    t8 = 4 * tg + tt
                    bm = E("tensor", lambda e, t8=t8, g8=g8:
                           e.matmul(out=PB(t8), lhsT=s_or[:],
                                    rhs=bob[0:1, 512 * g8:512 * (g8 + 1)],
                                    start=False, stop=True),
                           waits=[("brl", brl_bo)], inc="omm")
                    slot = (g8 * 8 + t8) % 2
                    wx = []
                    if xres_free.get(slot) is not None:
                        wx.append(("x1e", xres_free[slot]))
                    xl = E("sync", lambda e, s=slot, t8=t8, g8=g8:
                           e.dma_start(out=xres[s][:],
                                       in_=xs[128 * t8:128 * (t8 + 1),
                                              512 * g8:512 * (g8 + 1)]),
                           waits=wx, inc="xrl", amt=16)
                    w1 = [("omm", bm), ("xrl", xl)]
                    if x1_free.get(slot) is not None:
                        w1.append(("x1s", x1_free[slot]))
                    x1e = E("vector", lambda e, t8=t8, s=slot:
                            e.tensor_add(out=x1st[s][:], in0=PB(t8), in1=xres[s][:]),
                            waits=w1, inc="x1e")
                    xres_free[slot] = x1e
                    if tt == 3:
                        wo_bank_rd[tg] = x1e
                    x1_free[slot] = E("scalar", lambda e, s=slot, t8=t8, g8=g8:
                                      e.dma_start(out=x1d.ap()[128 * t8:128 * (t8 + 1),
                                                               512 * g8:512 * (g8 + 1)],
                                                  in_=x1st[s][:]),
                                      waits=[("x1e", x1e)], inc="x1s", amt=16)

        btl = E("gpsimd", lambda e: e.dma_start(out=wb[0][0:64, 0:4096], in_=bTm),
                waits=[("ojh", T("ojh")), ("omm", T("omm"))], inc="btl", amt=16)
        bTb = wb[0][0:64, 0:4096]
        xn2f = wb[1][:, 0:4096]
        tr_bank = {5: None, 6: None}
        pml_rd = None
        pmlt_rd = None
        pmgT_rd = None
        pd_rd = {3: None, 4: None}
        ost_free = [None] * 2
        xa_free = {0: None, 1: None}
        lnx2_tok = {}
        L = lambda ee: mlt[:, 64 + ee:65 + ee]
        for t8 in range(8):
            wxl = [("x1s", T("x1s"))]
            if lnx2_tok.get(t8 % 2) is not None:
                wxl.append(("lnx", lnx2_tok[t8 % 2]))
            if xa_free[t8 % 2] is not None:
                wxl.append(("ooe", xa_free[t8 % 2]))
            xl = E("sync", lambda e, t=t8: e.dma_start(out=xa[t % 2][:],
                                                       in_=x1d.ap()[128 * t:128 * (t + 1), :]),
                   waits=wxl, inc="xld", amt=16)
            a_x = ln_tile(xa[t8 % 2], xn[t8 % 2], t8 % 2, [("xld", xl)])
            lnx2_tok[t8 % 2] = a_x
            for g in range(4):
                bank = 5 + ((t8 * 4 + g) % 2)
                w = [("lnx", a_x), ("ojh", T("ojh"))]
                if t8 == 0 and g < 2:
                    w.append(("x1e", T("x1e")))
                if tr_bank[bank] is not None:
                    w.append(("tev", tr_bank[bank]))
                last = None
                for j2 in range(8):
                    j = 8 * g + j2
                    last = E("tensor", lambda e, b=bank, j2=j2, j=j, t=t8: e.transpose(
                        out=PBb(b)[:, 128 * j2:128 * (j2 + 1)],
                        in_=xn[t % 2][:, 128 * j:128 * (j + 1)], identity=s_id[:]),
                        waits=(w if j2 == 0 else []), inc=("ttr" if j2 == 7 else None))
                tr_bank[bank] = E("vector", lambda e, b=bank, g=g: e.tensor_copy(
                    out=xn2f[:, 1024 * g:1024 * (g + 1)], in_=PBb(b)[:]),
                    waits=[("ttr", last), ("ojh", T("ojh"))], inc="tev")
            tevt = T("tev")
            for j in range(32):
                w2 = []
                if j == 0:
                    w2 = [("tev", tevt), ("cst", cst_all)]
                    if pml_rd is not None:
                        w2.append(("mle", pml_rd))
                    if t8 == 0:
                        w2.append(("x1e", T("x1e")))
                E("tensor", lambda e, j=j: e.matmul(out=pall[0:68, 0:128],
                                                    lhsT=rhswb[:, 68 * j:68 * (j + 1)],
                                                    rhs=xn2f[:, 128 * j:128 * (j + 1)],
                                                    start=(j == 0), stop=(j == 31)),
                  waits=w2, inc=("mlm" if j == 31 else None))
            pml_rd = E("vector", lambda e: e.tensor_copy(out=sml[0:68, :], in_=pall[0:68, 0:128]),
                       waits=[("mlm", T("mlm"))], inc="mle")
            w3 = [("mle", pml_rd)]
            if pmlt_rd is not None:
                w3.append(("mlte", pmlt_rd))
            if t8 == 0:
                w3.append(("x1e", T("x1e")))
            trm = E("tensor", lambda e: e.transpose(out=PBb(1)[:, 0:68], in_=sml[0:68, :],
                                                    identity=s_id[0:68, 0:68]),
                    waits=w3, inc="mltm")
            pmlt_rd = E("vector", lambda e: e.tensor_copy(out=mlt[:], in_=PBb(1)[:, 0:68]),
                        waits=[("mltm", trm)], inc="mlte")
            fw = [("mlte", pmlt_rd)]
            for e_i in range(4):
                terms = [(a_i, ALU.is_ge if a_i < e_i else ALU.is_gt)
                         for a_i in range(4) if a_i != e_i]
                for k_i, (a_i, op) in enumerate(terms):
                    if k_i == 0:
                        E("vector", lambda e, a=a_i, ei=e_i, op=op: e.tensor_tensor(
                            out=gl[:, ei:ei + 1], in0=L(a), in1=L(ei), op=op), waits=fw,
                          fence=True)
                        fw = []
                    else:
                        E("vector", lambda e, a=a_i, ei=e_i, op=op: e.tensor_tensor(
                            out=gl[:, 4:5], in0=L(a), in1=L(ei), op=op), fence=True)
                        E("vector", lambda e, ei=e_i: e.tensor_add(
                            out=gl[:, ei:ei + 1], in0=gl[:, ei:ei + 1], in1=gl[:, 4:5]),
                          fence=True)
            E("vector", lambda e: e.tensor_max(out=gl[:, 5:6], in0=L(0), in1=L(1)), fence=True)
            E("vector", lambda e: e.tensor_max(out=gl[:, 5:6], in0=gl[:, 5:6], in1=L(2)),
              fence=True)
            E("vector", lambda e: e.tensor_max(out=gl[:, 5:6], in0=gl[:, 5:6], in1=L(3)),
              fence=True)
            vnm = E("vector", lambda e: e.tensor_scalar(out=gl[:, 6:7], in0=gl[:, 5:6],
                                                        scalar1=-1.0, scalar2=None,
                                                        op0=ALU.mult), inc="gnm", fence=True)
            for e_i in range(4):
                E("scalar", lambda e, ei=e_i: e.activation(out=gl[:, 7 + ei:8 + ei], in_=L(ei),
                                                           func=AF.Exp, bias=gl[:, 6:7],
                                                           scale=1.0),
                  waits=([("gnm", vnm)] if e_i == 0 else []),
                  inc=("gee" if e_i == 3 else None))
            gee = T("gee")
            for e_i in range(4):
                E("vector", lambda e, ei=e_i: e.tensor_single_scalar(
                    out=gl[:, 11 + ei:12 + ei], in_=gl[:, ei:ei + 1], scalar=1.5, op=ALU.is_le),
                  waits=([("gee", gee)] if e_i == 0 else []), fence=True)
            for e_i in range(4):
                E("vector", lambda e, ei=e_i: e.tensor_mul(out=gl[:, 15 + ei:16 + ei],
                                                           in0=gl[:, 7 + ei:8 + ei],
                                                           in1=gl[:, 11 + ei:12 + ei]),
                  fence=True)
            E("vector", lambda e: e.tensor_add(out=gl[:, 19:20], in0=gl[:, 15:16],
                                               in1=gl[:, 16:17]), fence=True)
            E("vector", lambda e: e.tensor_add(out=gl[:, 19:20], in0=gl[:, 19:20],
                                               in1=gl[:, 17:18]), fence=True)
            vdn = E("vector", lambda e: e.tensor_add(out=gl[:, 19:20], in0=gl[:, 19:20],
                                                     in1=gl[:, 18:19]), inc="gdn", fence=True)
            E("scalar", lambda e: e.activation(out=gl[:, 20:21], in_=gl[:, 19:20],
                                               func=AF.Ln, bias=0.0, scale=1.0),
              waits=[("gdn", vdn)])
            grg = E("scalar", lambda e: e.activation(out=gl[:, 20:21], in_=gl[:, 20:21],
                                                     func=AF.Exp, bias=0.0, scale=-1.0),
                    inc="grg", fence=True)
            for e_i in range(4):
                E("vector", lambda e, ei=e_i: e.tensor_scalar(
                    out=gl[:, 21 + ei:22 + ei], in0=gl[:, 15 + ei:16 + ei],
                    scalar1=gl[:, 20:21], scalar2=4.0, op0=ALU.mult, op1=ALU.mult),
                  waits=([("grg", grg)] if e_i == 0 else []), fence=True)
            gms = None
            for e_i in range(4):
                gms = E("vector", lambda e, ei=e_i: e.tensor_scalar(
                    out=mg[:, 16 * ei:16 * (ei + 1)], in0=mlt[:, 16 * ei:16 * (ei + 1)],
                    scalar1=gl[:, 21 + ei:22 + ei], scalar2=None, op0=ALU.mult),
                    inc=("gms" if e_i == 3 else None), fence=True)
            gms = T("gms")
            w4 = [("gms", gms)]
            if pmgT_rd is not None:
                w4.append(("mge", pmgT_rd))
            if t8 == 0:
                w4.append(("x1e", T("x1e")))
            E("tensor", lambda e: e.transpose(out=PBb(2)[0:64, 0:128], in_=mg[:, 0:64],
                                              identity=s_id[:]),
              waits=w4, inc="mgm")
            pmgT_rd = E("vector", lambda e: e.tensor_copy(out=mgT[:], in_=PBb(2)[0:64, 0:128]),
                        waits=[("mgm", T("mgm"))], inc="mge")
            mge = pmgT_rd
            last_ooe = None
            for g8 in range(8):
                bank = 3 + (g8 % 2)
                wd = [("mge", mge), ("btl", btl)]
                if pd_rd[bank] is not None:
                    wd.append(("ooe", pd_rd[bank]))
                if t8 == 0 and g8 < 2:
                    wd.append(("x1e", T("x1e")))
                dm = E("tensor", lambda e, bank=bank, g8=g8: e.matmul(
                    out=PB(bank), lhsT=mgT[0:64, :], rhs=bTb[0:64, 512 * g8:512 * (g8 + 1)],
                    start=True, stop=True), waits=wd, inc="dm")
                oslot = (t8 * 8 + g8) % 2
                wo_ = [("dm", dm)]
                if ost_free[oslot] is not None:
                    wo_.append(("outs", ost_free[oslot]))
                ooe = E("vector", lambda e, bank=bank, s=oslot, t=t8, g8=g8: e.tensor_add(
                    out=ost[s][:], in0=PB(bank), in1=xa[t % 2][:, 512 * g8:512 * (g8 + 1)]),
                    waits=wo_, inc="ooe")
                pd_rd[bank] = ooe
                last_ooe = ooe
                ost_free[oslot] = E("scalar", lambda e, s=oslot, t=t8, g8=g8: e.dma_start(
                    out=out[128 * t:128 * (t + 1), 512 * g8:512 * (g8 + 1)], in_=ost[s][:]),
                    waits=[("ooe", ooe)], inc="outs", amt=16)
            xa_free[t8 % 2] = last_ooe

        if taps:
            E("gpsimd", lambda e: e.dma_start(out=tap["ksend"].ap(), in_=k_send.ap()),
              waits=[("kst", T("kst"))], inc="tapd", amt=16)
            E("gpsimd", lambda e: e.dma_start(out=tap["qsend"].ap(), in_=q_send.ap()),
              waits=[("qst", T("qst")), ("mxst", T("mxst"))], inc="tapd", amt=16)
            E("gpsimd", lambda e: e.dma_start(out=tap["vsend"].ap(), in_=v_send.ap()),
              waits=[("vst", T("vst"))], inc="tapd", amt=16)
            E("gpsimd", lambda e: e.dma_start(out=tap["aosend"].ap(), in_=ao_send.ap()),
              waits=[("aos", T("aos"))], inc="tapd", amt=16)
            E("gpsimd", lambda e: e.dma_start(out=tap["aog"].ap(), in_=aogD.ap()),
              waits=[("cca", cca)], inc="tapd", amt=16)
            E("gpsimd", lambda e: e.dma_start(out=tap["x1"].ap(), in_=x1d.ap()),
              waits=[("x1s", T("x1s"))], inc="tapd", amt=16)
            E("gpsimd", lambda e: e.dma_start(out=tap["kg"].ap(), in_=kgD.ap()),
              waits=[("cck", cck)], inc="tapd", amt=16)
            E("gpsimd", lambda e: e.dma_start(out=tap["qg"].ap(), in_=qgD.ap()),
              waits=[("ccq", ccq)], inc="tapd", amt=16)
            E("gpsimd", lambda e: e.dma_start(out=tap["vg"].ap(), in_=vgD.ap()),
              waits=[("ccv", ccv)], inc="tapd", amt=16)
            S.wait_only("gpsimd", [("tapd", T("tapd"))])

        S.wait_only("scalar", [("outs", T("outs"))])

        for chname in S.cnt:
            sems[chname] = ctx.enter_context(nc.semaphore(chname))
        with allow_ici_all_to_all():
            with nc.Block() as block:
                S.replay(block, sems)
    return nc


def _prep_inputs(L0_attn_x, L0_attn_Wq, L0_attn_bq, L0_attn_Wk, L0_attn_bk,
                 L0_attn_Wv, L0_attn_bv, L0_attn_Wo, L0_attn_bo,
                 L0_ffn_A, L0_ffn_B, L0_ffn_Wr):
    x = np.asarray(L0_attn_x, dtype=np.float32)
    C = lambda a: np.ascontiguousarray(np.asarray(a, dtype=np.float32))
    A = np.asarray(L0_ffn_A, dtype=np.float32)
    B = np.asarray(L0_ffn_B, dtype=np.float32)
    Wr = np.asarray(L0_ffn_Wr, dtype=np.float32)
    aT = A.transpose(2, 0, 1).reshape(D, 64)
    wrT = np.asarray(Wr).T
    rhsw = np.concatenate([aT.reshape(32, 128, 64),
                           np.ascontiguousarray(wrT).reshape(32, 128, 4)], axis=2)
    shared = dict(
        wqT=C(np.asarray(L0_attn_Wq).T), wkT=C(np.asarray(L0_attn_Wk).T),
        wvT=C(np.asarray(L0_attn_Wv).T), woT=C(np.asarray(L0_attn_Wo).T),
        bqc=C(np.asarray(L0_attn_bq).reshape(32, 128).T),
        bkc=C(np.asarray(L0_attn_bk).reshape(32, 128).T),
        bvr=C(np.asarray(L0_attn_bv).reshape(1, D)),
        bor=C(np.asarray(L0_attn_bo).reshape(1, D)),
        rhsw=C(rhsw.transpose(1, 0, 2).reshape(128, 32 * 68)),
        bTm=C(B.transpose(0, 2, 1).reshape(64, D)))
    in_maps = []
    for c in range(8):
        g, qc = c // 4, c % 4
        xsl = np.concatenate([x[g, 512 * qc:512 * (qc + 1)],
                              x[g, 512 * (7 - qc):512 * (8 - qc)]], axis=0)
        m = dict(shared)
        m["xs"] = np.ascontiguousarray(xsl)
        in_maps.append(m)
    return in_maps


def kernel(**inputs):
    in_maps = _prep_inputs(**inputs)
    with allow_ici_all_to_all():
        nc = build_nc(taps=_TAPS)
        res = run_bass_kernel_spmd(nc, in_maps, core_ids=list(range(8)))
    outf = np.zeros((2, D, D), dtype=np.float32)
    for c in range(8):
        g, qc = c // 4, c % 4
        o = res.results[c]["out"]
        outf[g, 512 * qc:512 * (qc + 1)] = o[0:512]
        outf[g, 512 * (7 - qc):512 * (8 - qc)] = o[512:1024]
    if _TAPS:
        kernel._tap_results = res.results
    return outf


# version 05
# speedup vs baseline: 1.3686x; 1.0154x over previous
import math
from contextlib import ExitStack

import numpy as np
import ml_dtypes

import concourse.bass as bass
import concourse.mybir as mybir
from concourse.bass_utils import run_bass_kernel_spmd
from concourse.aot_env import allow_ici_all_to_all

F32 = mybir.dt.float32
BF16 = mybir.dt.bfloat16
AF = mybir.ActivationFunctionType
ALU = mybir.AluOpType
AXX = mybir.AxisListType.X

D = 4096
NT = 1024
NP = 8
SCLQ = 1.0 / math.sqrt(128.0)
EPS = 1e-5
_TAPS = False

ENGS = ("sync", "gpsimd", "scalar", "vector", "tensor")


def _rho(B):
    return B if B < 4 else 7 - B


def _co(B):
    return 0 if B < 4 else 512


class Sched:
    def __init__(self):
        self.ops = {e: [] for e in ENGS}
        self.cnt = {}
        self._pend = {e: [] for e in ENGS}

    def tok(self, ch):
        return self.cnt.get(ch, 0)

    def emit(self, eng, fn, waits=(), inc=None, amt=1, fence=False):
        if fence:
            self.fence(eng)
        waits = list(waits) + self._pend[eng]
        self._pend[eng] = []
        if inc is not None:
            self.cnt[inc] = self.cnt.get(inc, 0) + amt
        self.ops[eng].append((tuple((c, v) for c, v in waits if v), fn, inc, amt))
        return self.cnt.get(inc, 0)

    def fence(self, eng):
        if not self.ops[eng]:
            return
        w, fn, inc, amt = self.ops[eng][-1]
        if fn is None:
            return
        if inc is None:
            ch = "sf_" + eng
            self.cnt[ch] = self.cnt.get(ch, 0) + 1
            self.ops[eng][-1] = (w, fn, ch, 1)
            self._pend[eng].append((ch, self.cnt[ch]))
        else:
            self._pend[eng].append((inc, self.cnt[inc]))

    def wait_only(self, eng, waits):
        self.ops[eng].append((tuple(waits), None, None, 0))

    def replay(self, block, sems):
        def mk(ename):
            def body(eng):
                seen = {}
                for waits, fn, inc, amt in self.ops[ename]:
                    for ch, val in waits:
                        if val > seen.get(ch, 0):
                            eng.wait_ge(sems[ch], val)
                            seen[ch] = val
                    if fn is not None:
                        ins = fn(eng)
                        if inc is not None:
                            ins.then_inc(sems[inc], amt)
            return body

        block.sync(mk("sync"))
        block.gpsimd(mk("gpsimd"))
        block.scalar(mk("scalar"))
        block.vector(mk("vector"))
        block.tensor(mk("tensor"))


def build_nc(taps=False):
    nc = bass.Bass("TRN2", target_bir_lowering=False)

    xs = nc.dram_tensor("xs", [NT, D], F32, kind="ExternalInput").ap()
    wqT = nc.dram_tensor("wqT", [D, D], F32, kind="ExternalInput").ap()
    wkT = nc.dram_tensor("wkT", [D, D], F32, kind="ExternalInput").ap()
    wvT = nc.dram_tensor("wvT", [D, D], F32, kind="ExternalInput").ap()
    woT = nc.dram_tensor("woT", [D, D], F32, kind="ExternalInput").ap()
    bqc = nc.dram_tensor("bqc", [128, 32], F32, kind="ExternalInput").ap()
    bkc = nc.dram_tensor("bkc", [128, 32], F32, kind="ExternalInput").ap()
    bvr = nc.dram_tensor("bvr", [1, D], F32, kind="ExternalInput").ap()
    bor = nc.dram_tensor("bor", [1, D], F32, kind="ExternalInput").ap()
    rhsw = nc.dram_tensor("rhsw", [128, 32 * 68], F32, kind="ExternalInput").ap()
    bTm = nc.dram_tensor("bTm", [64, D], F32, kind="ExternalInput").ap()
    out = nc.dram_tensor("out", [NT, D], F32, kind="ExternalOutput").ap()

    ident = nc.inline_tensor(np.eye(128, dtype=ml_dtypes.bfloat16), "ident")
    ones_col = nc.inline_tensor(np.ones((128, 1), dtype=ml_dtypes.bfloat16), "ones_col")
    ones_row = nc.inline_tensor(np.ones((1, 128), dtype=ml_dtypes.bfloat16), "ones_row")
    ones_row_f = nc.inline_tensor(np.ones((1, 128), dtype=np.float32), "ones_row_f")
    mknp = np.zeros((128, 4 * 512), dtype=ml_dtypes.bfloat16)
    pp = np.arange(128)[:, None]
    jj_ = np.arange(512)[None, :]
    for u in range(4):
        mknp[:, 512 * u:512 * (u + 1)] = ((128 * u + pp) <= jj_).astype(ml_dtypes.bfloat16)
    maskt = nc.inline_tensor(mknp, "maskt")

    k_send = nc.dram_tensor("k_send", [8 * 512, NT], BF16, kind="Internal")
    q_send = nc.dram_tensor("q_send", [8 * 512, NT], BF16, kind="Internal")
    v_send = nc.dram_tensor("v_send", [8 * 1024, 512], BF16, kind="Internal")
    kgD = nc.dram_tensor("kgD", [8 * 512, NT], BF16, kind="Internal")
    qgD = nc.dram_tensor("qgD", [8 * 512, NT], BF16, kind="Internal")
    vgD = nc.dram_tensor("vgD", [8 * 1024, 512], BF16, kind="Internal")
    ao_send = nc.dram_tensor("ao_send", [8 * 512, NT], BF16, kind="Internal")
    aogD = nc.dram_tensor("aogD", [8 * 512, NT], BF16, kind="Internal")
    x1d = nc.dram_tensor("x1d", [NT, D], F32, kind="Internal")
    tap = {}
    if taps:
        for n, sh, dt in [("ksend", (8 * 512, NT), BF16), ("qsend", (8 * 512, NT), BF16),
                          ("vsend", (8 * 1024, 512), BF16), ("aosend", (8 * 512, NT), BF16),
                          ("aog", (8 * 512, NT), BF16), ("x1", (NT, D), F32),
                          ("kg", (8 * 512, NT), BF16), ("qg", (8 * 512, NT), BF16),
                          ("vg", (8 * 1024, 512), BF16)]:
            tap[n] = nc.dram_tensor("tap_" + n, list(sh), dt, kind="ExternalOutput")

    S = Sched()
    E = S.emit
    T = S.tok

    with ExitStack() as ctx:
        SB = lambda n, sh, dt: ctx.enter_context(nc.sbuf_tensor(n, sh, dt))
        big = SB("big", [128, 32 * NT], BF16)
        wb = [SB(f"wb{i}", [128, 8192], BF16) for i in range(2)]
        xa = [SB(f"xa{i}", [128, D], F32) for i in range(2)]
        xn = [SB(f"xn{i}", [128, D], BF16) for i in range(2)]
        dump = SB("dump", [128, D], BF16)
        kst = [SB(f"kst{i}", [128, 512], BF16) for i in range(4)]
        k2t = [SB(f"k2t{i}", [128, 512], BF16) for i in range(2)]
        attT = [SB(f"attT{i}", [128, 512], BF16) for i in range(4)]
        rbc = [SB(f"rbc{i}", [128, 512], F32) for i in range(2)]
        aost = [SB(f"aost{i}", [128, 512], BF16) for i in range(3)]
        xres = [SB(f"xres{i}", [128, 512], F32) for i in range(2)]
        x1st = [SB(f"x1st{i}", [128, 512], F32) for i in range(2)]
        ost = [SB(f"ost{i}", [128, 512], F32) for i in range(2)]
        sml = SB("sml", [128, 128], BF16)
        mlt = SB("mlt", [128, 68], BF16)
        mg = SB("mg", [128, 64], BF16)
        mgT = SB("mgT", [64, 128], BF16)
        gl = SB("gl", [128, 28], F32)
        s_id = SB("s_id", [128, 128], BF16)
        s_msk = SB("s_msk", [128, 4 * 512], BF16)
        s_oc = SB("s_oc", [128, 1], BF16)
        s_or = SB("s_or", [1, 128], BF16)
        s_orf = SB("s_orf", [1, 128], F32)
        s_bq = SB("s_bq", [128, 32], F32)
        s_bqs = SB("s_bqs", [128, 32], F32)
        s_bk = SB("s_bk", [128, 32], F32)
        rhswb = SB("rhswb", [128, 32 * 68], BF16)
        urow = SB("urow", [128, 2048], F32)
        mqct = SB("mqct", [128, 8], BF16)
        mqt = SB("mqt", [128, 1], BF16)
        nmqt = SB("nmqt", [1, 1024], BF16)
        stat = SB("stat", [128, 16], F32)
        epsc = SB("epsc", [128, 1], F32)
        pall = ctx.enter_context(nc.psum_tensor("pall", [128, 4096], F32))
        sems = {}

        def PB(b):
            return pall[:, 512 * b:512 * (b + 1)]

        def PBb(b):
            return pall[:, 512 * b:512 * (b + 1)].bitcast(BF16)

        bvb = dump[0:1, :]
        bob = dump[0:1, :]
        r_t = urow[0:1, 0:512]
        r_rd = [urow[0:1, 512:1024], urow[0:1, 1024:1536]]
        r_ksq = urow[0:1, 1536:1568]
        r_qsq = urow[0:1, 1568:1600]
        r_m2 = urow[0:1, 1600:1608]
        r_mro = urow[0:1, 1608:1616]
        r_nM = urow[0:1, 1616:1624]
        r_mx1 = urow[0:1, 1624:1632]

        for dst, src in [(s_id[:], ident.ap()), (s_msk[:], maskt.ap()),
                         (s_oc[:], ones_col.ap()), (s_or[:], ones_row.ap()),
                         (s_orf[:], ones_row_f.ap()), (s_bq[:], bqc), (s_bk[:], bkc)]:
            E("sync", lambda e, d=dst, s=src: e.dma_start(out=d, in_=s), inc="cst", amt=16)
        for dst, src in [(rhswb[:], rhsw)]:
            E("gpsimd", lambda e, d=dst, s=src: e.dma_start(out=d, in_=s), inc="cst", amt=16)
        cst_all = T("cst")
        E("vector", lambda e: e.tensor_scalar(out=s_bqs[:], in0=s_bq[:], scalar1=SCLQ,
                                              scalar2=None, op0=ALU.mult),
          waits=[("cst", cst_all)], inc="prep")
        E("vector", lambda e: e.memset(urow[0:1, 1536:1600], -3.0e38), inc="prep")
        prep_all = E("vector", lambda e: e.memset(epsc[:], EPS), inc="prep")

        def ln_tile(xa_t, xn_t, par, wfirst):
            o = 8 * par
            c_s1, c_s2, c_mean, c_var, c_ms, c_rstd, c_nmr, c_m2 = [
                stat[:, o + i:o + i + 1] for i in range(8)]
            E("vector", lambda e: e.tensor_scalar(out=dump[:], in0=xa_t[:], scalar1=1.0,
                                                  scalar2=0.0, op0=ALU.mult, op1=ALU.add,
                                                  accum_out=c_s1),
              waits=wfirst)
            E("vector", lambda e: e.scalar_tensor_tensor(out=dump[:], in0=xa_t[:], scalar=1.0,
                                                         in1=xa_t[:], op0=ALU.mult, op1=ALU.mult,
                                                         accum_out=c_s2))
            E("vector", lambda e: e.tensor_scalar(out=c_mean, in0=c_s1, scalar1=1.0 / D,
                                                  scalar2=None, op0=ALU.mult), fence=True)
            E("vector", lambda e: e.scalar_tensor_tensor(out=c_var, in0=c_s1, scalar=c_mean,
                                                         in1=c_s2, op0=ALU.mult,
                                                         op1=ALU.subtract), fence=True)
            v_var = E("vector", lambda e: e.tensor_scalar(out=c_var, in0=c_var,
                                                          scalar1=-1.0 / D, scalar2=None,
                                                          op0=ALU.mult), inc="lnv", fence=True)
            E("scalar", lambda e: e.activation(out=c_rstd, in_=c_var, func=AF.Ln,
                                               bias=epsc[:], scale=1.0),
              waits=[("lnv", v_var), ("prep", prep_all)])
            a_r = E("scalar", lambda e: e.activation(out=c_rstd, in_=c_rstd, func=AF.Exp,
                                                     bias=0.0, scale=-0.5), inc="lnr",
                    fence=True)
            v_n = E("vector", lambda e: e.tensor_scalar(out=c_nmr, in0=c_mean, scalar1=c_rstd,
                                                        scalar2=-1.0, op0=ALU.mult, op1=ALU.mult),
                    waits=[("lnr", a_r)], inc="lnn")
            a_x = E("scalar", lambda e: e.activation(out=xn_t[:], in_=xa_t[:], func=AF.Identity,
                                                     bias=c_nmr, scale=c_rstd),
                    waits=[("lnn", v_n)], inc="lnx")
            return a_x

        tev_bank = {0: 0, 1: 0}
        lnx_tok = {}
        for t8 in range(8):
            wx = [("lnx", lnx_tok[t8 % 2])] if t8 >= 2 else []
            xl = E("sync", lambda e, t=t8: e.dma_start(out=xa[t % 2][:],
                                                       in_=xs[128 * t:128 * (t + 1), :]),
                   waits=wx, inc="xld", amt=16)
            a_x = ln_tile(xa[t8 % 2], xn[t8 % 2], t8 % 2,
                          [("xld", xl), ("cst", cst_all)])
            lnx_tok[t8 % 2] = a_x
            for g in range(4):
                bank = (t8 * 4 + g) % 2
                w = [("lnx", a_x)]
                if tev_bank[bank]:
                    w.append(("tev", tev_bank[bank]))
                last = None
                for j2 in range(8):
                    j = 8 * g + j2
                    last = E("tensor", lambda e, b=bank, j2=j2, j=j, t=t8: e.transpose(
                        out=PBb(b)[:, 128 * j2:128 * (j2 + 1)],
                        in_=xn[t % 2][:, 128 * j:128 * (j + 1)], identity=s_id[:]),
                        waits=(w if j2 == 0 else []), inc=("ttr" if j2 == 7 else None))
                dst = big[:].rearrange("p (j c) -> p j c", c=NT)[:, 8 * g:8 * (g + 1),
                                                                128 * t8:128 * (t8 + 1)]
                src = PBb(bank)[:].rearrange("p (j c) -> p j c", c=128)
                tev_bank[bank] = E("vector", lambda e, d=dst, s=src: e.tensor_copy(out=d, in_=s),
                                   waits=[("ttr", last)], inc="tev")
        tev_all = T("tev")
        brl_all = E("gpsimd", lambda e: e.dma_start(out=bvb, in_=bvr),
                    waits=[("lnv", T("lnv"))], inc="brl", amt=16)

        slotg = {}

        def proj_phase(wT, pref, bias_t, scale, send, entry_guard):
            strip_free = {0: None, 1: None}
            for hg in range(8):
                ev_before = T(pref + "ev")
                wtk = {}
                for jh in range(2):
                    buf = jh
                    src = wT[2048 * jh:2048 * (jh + 1), 512 * hg:512 * (hg + 1)].rearrange(
                        "(j p) c -> p j c", p=128)
                    dstw = wb[buf][:].rearrange("p (j c) -> p j c", c=512)
                    w = [("tev", tev_all)]
                    if strip_free[buf] is not None:
                        w.append((pref + "mm", strip_free[buf]))
                    if pref == "q":
                        w.append(("kmm", T("kmm")))
                    wtk[jh] = E("gpsimd", lambda e, s=src, d=dstw: e.dma_start(out=d, in_=s),
                                waits=w, inc="wld", amt=16)
                for jh in range(2):
                    for hh in range(4):
                        for j16 in range(16):
                            for half in range(2):
                                j = 16 * jh + j16
                                st = (jh == 0 and j16 == 0)
                                sp = (jh == 1 and j16 == 15)
                                w2 = []
                                if hh == 0 and j16 == 0 and half == 0:
                                    w2.append(("wld", wtk[jh]))
                                if jh == 0 and j16 == 0 and half == 0:
                                    bt = S.bank_ev.get((pref, hh))
                                    if bt:
                                        w2.append((pref + "ev", bt))
                                    if hg == 0:
                                        w2.extend(entry_guard)
                                E("tensor",
                                  lambda e, b=jh, hh=hh, j16=j16, j=j, half=half, st=st, sp=sp:
                                  e.matmul(out=pall[:, 1024 * hh + 512 * half:
                                                    1024 * hh + 512 * half + 512],
                                           lhsT=wb[b][:, 512 * j16 + 128 * hh:
                                                      512 * j16 + 128 * (hh + 1)],
                                           rhs=big[:, NT * j + 512 * half:NT * j + 512 * half + 512],
                                           start=st, stop=sp),
                                  waits=w2,
                                  inc=(pref + "mm") if (j16 == 15 and half == 1)
                                  else None)
                            if jh == 1 and j16 == 15:
                                mmt = T(pref + "mm")
                                h = 4 * hg + hh
                                evl = None
                                for half2 in range(2):
                                    slot = (hg * 8 + hh * 2 + half2) % 4
                                    wv = [(pref + "mm", mmt), ("prep", prep_all)] +                                         slotg.get(slot, [])
                                    ev = E("scalar", lambda e, hh=hh, half=half2, h=h, s=slot:
                                           e.activation(out=kst[s][:],
                                                        in_=pall[:, 1024 * hh + 512 * half:
                                                                 1024 * hh + 512 * half + 512],
                                                        func=AF.Identity,
                                                        bias=bias_t[:, h:h + 1],
                                                        scale=scale),
                                           waits=wv, inc=pref + "ev")
                                    evl = ev
                                    r0 = 512 * (h // 4) + 128 * (h % 4)
                                    stk = E("scalar", lambda e, s=slot, r0=r0, half=half2:
                                            e.dma_start(out=send.ap()[r0:r0 + 128,
                                                                      512 * half:
                                                                      512 * (half + 1)],
                                                        in_=kst[s][:]),
                                            waits=[(pref + "ev", ev)],
                                            inc=pref + "st", amt=16)
                                    slotg[slot] = [(pref + "st", stk)]
                                S.bank_ev[(pref, hh)] = evl
                    strip_free[jh] = T(pref + "mm")

        S.bank_ev = {}
        proj_phase(wkT, "k", s_bk, 1.0, k_send, [])

        proj_phase(wqT, "q", s_bqs, SCLQ, q_send, [("kev", T("kev"))])

        strip_free = {0: None, 1: None}
        vbank_rd = {}
        for g8 in range(8):
            wtk = {}
            for jh in range(2):
                src = wvT[2048 * jh:2048 * (jh + 1), 512 * g8:512 * (g8 + 1)].rearrange(
                    "(j p) c -> p j c", p=128)
                dstw = wb[jh][:].rearrange("p (j c) -> p j c", c=512)
                w = [("qmm", T("qmm"))]
                if strip_free[jh] is not None:
                    w.append(("vjh", strip_free[jh]))
                wtk[jh] = E("gpsimd", lambda e, s=src, d=dstw: e.dma_start(out=d, in_=s),
                            waits=w, inc="wld", amt=16)
            for jh in range(2):
                for t8 in range(8):
                    for j16 in range(16):
                        j = 16 * jh + j16
                        st = (jh == 0 and j16 == 0)
                        w2 = []
                        if t8 == 0 and j16 == 0:
                            w2.append(("wld", wtk[jh]))
                            if st:
                                w2.append(("qev", T("qev")))
                        if st and j16 == 0:
                            bt = vbank_rd.get(t8)
                            if bt:
                                w2.append(("vev", bt))
                        E("tensor", lambda e, b=jh, t8=t8, j16=j16, j=j, st=st:
                          e.matmul(out=PB(t8),
                                   lhsT=big[:, NT * j + 128 * t8:NT * j + 128 * (t8 + 1)],
                                   rhs=wb[b][:, 512 * j16:512 * (j16 + 1)],
                                   start=st, stop=False),
                          waits=w2,
                          inc="vjh" if (j16 == 15) else None)
                    if jh == 1:
                        bm = E("tensor", lambda e, t8=t8, g8=g8:
                               e.matmul(out=PB(t8), lhsT=s_or[:],
                                        rhs=bvb[0:1, 512 * g8:512 * (g8 + 1)],
                                        start=False, stop=True),
                               waits=[("cst", cst_all), ("brl", brl_all)], inc="vmm")
                        slot = (g8 * 8 + t8) % 4
                        wv = [("vmm", bm)] + slotg.get(slot, [])
                        ev = E("vector", lambda e, t8=t8, s=slot: e.tensor_copy(
                            out=kst[s][:], in_=PB(t8)),
                            waits=wv, inc="vev")
                        vbank_rd[t8] = ev
                        stk = E("scalar", lambda e, s=slot, g8=g8, t8=t8:
                                e.dma_start(out=v_send.ap()[1024 * g8 + 128 * t8:
                                                            1024 * g8 + 128 * (t8 + 1), :],
                                            in_=kst[s][:]),
                                waits=[("vev", ev)], inc="vst", amt=16)
                        slotg[slot] = [("vst", stk)]
                strip_free[jh] = T("vjh")
        brl_bo = E("gpsimd", lambda e: e.dma_start(out=bob, in_=bor),
                   waits=[("vmm", T("vmm"))], inc="brl", amt=16)

        cck = E("gpsimd", lambda e: e.collective_compute(
            "AllToAll", ALU.bypass, replica_groups=[list(range(8))],
            ins=[k_send.ap().opt()], outs=[kgD.ap().opt()]),
            waits=[("kst", T("kst")), ("qst", T("qst")), ("vst", T("vst")),
                   ("qmm", T("qmm")), ("vmm", T("vmm")), ("xld", T("xld")),
                   ("cst", cst_all), ("brl", T("brl")), ("wld", T("wld"))], inc="cck")
        ccq = E("gpsimd", lambda e: e.collective_compute(
            "AllToAll", ALU.bypass, replica_groups=[list(range(8))],
            ins=[q_send.ap().opt()], outs=[qgD.ap().opt()]),
            waits=[("cck", cck)], inc="ccq")
        ccv = E("gpsimd", lambda e: e.collective_compute(
            "AllToAll", ALU.bypass, replica_groups=[list(range(8))],
            ins=[v_send.ap().opt()], outs=[vgD.ap().opt()]),
            waits=[("ccq", ccq)], inc="ccv")

        def kres(par):
            return big[:, 12288 * par:12288 * par + 4096]

        def vres(par):
            return big[:, 12288 * par + 4096:12288 * par + 8192]

        def qres(par):
            return big[:, 12288 * par + 8192:12288 * par + 12288]

        mqc = [mqct[:, 0:4], mqct[:, 4:8]]
        nmq = [nmqt[0:1, 0:512], nmqt[0:1, 512:1024]]
        mxps = [pall[:, 2560:2688], pall[:, 3072:3200]]
        pden = pall[0:1, 2048:2560]
        pbc = PB(4)
        sc_tok = {0: None, 1: None, 7: None}
        av_tok = {2: None, 3: None}
        attT_tok = [None] * 4
        rbc_tok = [None, None]
        aost_tok = [None] * 3
        mx_tok = [None, None]
        den_free = None
        ld_prev = {}
        sci = 0
        ui = 0
        for p in range(NP):
            b, hh = p // 4, p % 4
            par = p % 2
            w0 = [("cck", cck), ("ccq", ccq), ("ccv", ccv)]
            if ld_prev.get(par) is not None:
                w0.append(("dnm", ld_prev[par]))
            lds = 0
            for B in range(8):
                s = 4 * b + _rho(B)
                co = _co(B)
                E("sync", lambda e, s=s, co=co, hh=hh, B=B, par=par:
                  e.dma_start(out=kres(par)[:, 512 * B:512 * (B + 1)],
                              in_=kgD.ap()[512 * s + 128 * hh:512 * s + 128 * (hh + 1),
                                           co:co + 512]),
                  waits=(w0 if B == 0 else []), inc="p2ld", amt=16)
                E("sync", lambda e, s=s, co=co, hh=hh, B=B, par=par:
                  e.dma_start(out=vres(par)[:, 512 * B:512 * (B + 1)].rearrange(
                      "p (u c) -> p u c", c=128),
                      in_=vgD.ap()[1024 * s + co:1024 * s + co + 512,
                                   128 * hh:128 * (hh + 1)].rearrange("(u p) c -> p u c", p=128)),
                  inc="p2ld", amt=16)
                lds = E("sync", lambda e, s=s, co=co, hh=hh, B=B, par=par:
                        e.dma_start(out=qres(par)[:, 512 * B:512 * (B + 1)],
                                    in_=qgD.ap()[512 * s + 128 * hh:512 * s + 128 * (hh + 1),
                                                 co:co + 512]),
                        inc="p2ld", amt=16)

            def mx_minis(P, qt):
                for q4 in range(4):
                    for B in range(P + 1):
                        yield (P, q4, B)
                yield None

            def emit_mx_mini(item, qt):
                if item is None:
                    emit_mx_tail(qt)
                    return
                Pm, q4, B = item
                if True:
                    if True:
                        mi = (S.cnt.get("_mxi", 0)) % 2
                        S.cnt["_mxi"] = S.cnt.get("_mxi", 0) + 1
                        wmx = [("p2ld", lds)]
                        if mx_tok[mi] is not None:
                            wmx.append(("mxr", mx_tok[mi]))
                        mm = E("tensor", lambda e, mi=mi, P=Pm, q4=q4, B=B, par=par:
                               e.matmul(out=mxps[mi],
                                        lhsT=qres(par)[:, 512 * P + 128 * q4:
                                                       512 * P + 128 * (q4 + 1)],
                                        rhs=kres(par)[:, 512 * B:512 * B + 128],
                                        start=True, stop=True),
                               waits=wmx, inc="mxm")
                        rdt = E("vector", lambda e, mi=mi: e.reduce_max(
                            out=mqt[:, 0:1], in_=mxps[mi], axis=AXX),
                            waits=[("mxm", mm)], inc="mxr", fence=True)
                        if B == 0:
                            E("vector", lambda e, qt=qt, q4=q4: e.tensor_copy(
                                out=mqc[qt][:, q4:q4 + 1], in_=mqt[:, 0:1]),
                              inc="mqv", fence=True)
                        else:
                            E("vector", lambda e, qt=qt, q4=q4: e.tensor_max(
                                out=mqc[qt][:, q4:q4 + 1], in0=mqc[qt][:, q4:q4 + 1],
                                in1=mqt[:, 0:1]), inc="mqv", fence=True)
                        mx_tok[mi] = rdt

            def emit_mx_tail(qt):
                for q4 in range(4):
                    scb = [0, 1, 7][(S.cnt.get("_sci", 0)) % 3]
                    S.cnt["_sci"] = S.cnt.get("_sci", 0) + 1
                    wtp = [("mqv", T("mqv"))]
                    if sc_tok[scb] is not None:
                        wtp.append(sc_tok[scb])
                    tp = E("tensor", lambda e, scb=scb, qt=qt, q4=q4: e.transpose(
                        out=PBb(scb)[0:1, 0:128], in_=mqc[qt][:, q4:q4 + 1],
                        identity=s_id[:]),
                        waits=wtp, inc="mtp")
                    ng = E("vector", lambda e, scb=scb, qt=qt, q4=q4: e.tensor_scalar(
                        out=nmq[qt][0:1, 128 * q4:128 * (q4 + 1)], in0=PBb(scb)[0:1, 0:128],
                        scalar1=-1.0, scalar2=None, op0=ALU.mult),
                        waits=[("mtp", tp)], inc="nmg")
                    sc_tok[scb] = ("nmg", ng)

            def units(P, qt, nmg_tok, minis, mqt_next):
                nonlocal ui, den_free
                avb = 2 + qt
                pend = None
                for B in range(P + 1):
                    for u in range(4):
                        if minis is not None:
                            item = next(minis, "done")
                            if item != "done":
                                emit_mx_mini(item, mqt_next)
                        first = (B == 0 and u == 0)
                        lastu = (B == P and u == 3)
                        scb = [0, 1, 7][(S.cnt.get("_sci", 0)) % 3]
                        S.cnt["_sci"] = S.cnt.get("_sci", 0) + 1
                        ats = ui % 4
                        wsc = [("p2ld", lds), ("nmg", nmg_tok)]
                        if sc_tok[scb] is not None:
                            wsc.append(sc_tok[scb])
                        smm = E("tensor", lambda e, scb=scb, B=B, u=u, P=P, par=par:
                                e.matmul(out=PB(scb),
                                         lhsT=kres(par)[:, 512 * B + 128 * u:
                                                        512 * B + 128 * (u + 1)],
                                         rhs=qres(par)[:, 512 * P:512 * (P + 1)],
                                         start=True, stop=False),
                                waits=wsc, inc="smm")
                        smm = E("tensor", lambda e, scb=scb, qt=qt:
                                e.matmul(out=PB(scb), lhsT=s_or[:],
                                         rhs=nmq[qt][0:1, :], start=False, stop=True),
                                inc="smm")
                        wxp = [("smm", smm)]
                        if attT_tok[ats] is not None:
                            wxp.append(("dnm", attT_tok[ats]))
                        sxp = E("scalar", lambda e, scb=scb, a=ats:
                                e.activation(out=attT[a][:], in_=PB(scb), func=AF.Exp,
                                             bias=0.0, scale=1.0),
                                waits=wxp, inc="sxp")
                        sc_tok[scb] = ("sxp", sxp)
                        gate = ("sxp", sxp)
                        if B == P:
                            mku = E("vector", lambda e, a=ats, u=u:
                                    e.tensor_mul(out=attT[a][:], in0=attT[a][:],
                                                 in1=s_msk[:, 512 * u:512 * (u + 1)]),
                                    waits=[("sxp", sxp)], inc="msk")
                            gate = ("msk", mku)
                        def mk_avden(gate=gate, avb=avb, B=B, u=u, par=par, a=ats,
                                     first=first, lastu=lastu):
                            wav = [gate]
                            if first and av_tok[avb] is not None:
                                wav.append(("aev", av_tok[avb]))
                            E("tensor", lambda e:
                              e.matmul(out=PB(avb),
                                       lhsT=vres(par)[:, 512 * B + 128 * u:
                                                      512 * B + 128 * (u + 1)],
                                       rhs=attT[a][:], start=first, stop=lastu),
                              waits=wav, inc="avm")
                            wdn = []
                            if first and den_free is not None:
                                wdn.append(den_free)
                            dnm = E("tensor", lambda e:
                                    e.matmul(out=pden, lhsT=s_oc[:], rhs=attT[a][:],
                                             start=first, stop=lastu),
                                    waits=wdn, inc="dnm")
                            attT_tok[a] = dnm
                        if pend is not None:
                            pend()
                        pend = mk_avden
                        ui += 1
                if pend is not None:
                    pend()
                if minis is not None:
                    for item in minis:
                        emit_mx_mini(item, mqt_next)
                dl1 = E("scalar", lambda e: e.activation(
                    out=r_t[:], in_=pden, func=AF.Ln, bias=0.0, scale=1.0),
                    waits=[("dnm", T("dnm"))], inc="dnl")
                rd = E("scalar", lambda e, qt=qt: e.activation(out=r_rd[qt][:], in_=r_t[:],
                                                               func=AF.Exp, bias=0.0,
                                                               scale=-1.0),
                       inc="rdn", fence=True)
                bcm = E("tensor", lambda e: e.matmul(out=pbc, lhsT=s_orf[:],
                                                     rhs=r_rd[qt][:],
                                                     start=True, stop=True),
                        waits=[("rdn", rd), ("dnl", dl1)], inc="bcm")
                wcc = [("bcm", bcm)]
                if rbc_tok[qt] is not None:
                    wcc.append(("aev", rbc_tok[qt]))
                bcc = E("vector", lambda e, qt=qt: e.tensor_copy(out=rbc[qt][:], in_=pbc),
                        waits=wcc, inc="bcc")
                den_free = ("bcc", bcc)
                aslot = (p * 8 + P) % 3
                wev = [("bcc", bcc)]
                if aost_tok[aslot] is not None:
                    wev.append(("aos", aost_tok[aslot]))
                aev = E("vector", lambda e, avb=avb, qt=qt, s=aslot:
                        e.tensor_tensor(out=aost[s][:], in0=PB(avb), in1=rbc[qt][:],
                                        op=ALU.mult),
                        waits=wev, inc="aev")
                av_tok[avb] = aev
                rbc_tok[qt] = aev
                d = 4 * b + _rho(P)
                co = _co(P)
                aost_tok[aslot] = E("scalar", lambda e, s=aslot, d=d, co=co, hh=hh:
                                    e.dma_start(out=ao_send.ap()[
                                        512 * d + 128 * hh:512 * d + 128 * (hh + 1),
                                        co:co + 512], in_=aost[s][:]),
                                    waits=[("aev", aev)], inc="aos", amt=16)

            for item in mx_minis(0, 0):
                emit_mx_mini(item, 0)
            nmg_tok = T("nmg")
            for P in range(8):
                minis = mx_minis(P + 1, (P + 1) % 2) if P < 7 else None
                units(P, P % 2, nmg_tok, minis, (P + 1) % 2)
                nmg_tok = T("nmg")
            ld_prev[par] = T("dnm")

        cca = E("gpsimd", lambda e: e.collective_compute(
            "AllToAll", ALU.bypass, replica_groups=[list(range(8))],
            ins=[ao_send.ap().opt()], outs=[aogD.ap().opt()]),
            waits=[("aos", T("aos")), ("p2ld", T("p2ld")), ("kst", T("kst")),
                   ("qst", T("qst")), ("vst", T("vst"))], inc="cca")

        agl = E("sync", lambda e: e.dma_start(
            out=big[:].rearrange("p (j c) -> p j c", c=NT),
            in_=aogD.ap().rearrange("(j p) c -> p j c", p=128)),
            waits=[("cca", cca), ("dnm", T("dnm")), ("smm", T("smm"))], inc="agl", amt=16)
        strip_free = {0: None, 1: None}
        wo_bank_rd = {}
        xres_free = {}
        x1_free = {}
        for g8 in range(8):
            wtk = {}
            for jh in range(2):
                src = woT[2048 * jh:2048 * (jh + 1), 512 * g8:512 * (g8 + 1)].rearrange(
                    "(j p) c -> p j c", p=128)
                dstw = wb[jh][:].rearrange("p (j c) -> p j c", c=512)
                w = [("smm", T("smm")), ("cca", cca)]
                if strip_free[jh] is not None:
                    w.append(("ojh", strip_free[jh]))
                wtk[jh] = E("gpsimd", lambda e, s=src, d=dstw: e.dma_start(out=d, in_=s),
                            waits=w, inc="wld", amt=16)
            for jh in range(2):
                for t8 in range(8):
                    for j16 in range(16):
                        j = 16 * jh + j16
                        st = (jh == 0 and j16 == 0)
                        w2 = []
                        if t8 == 0 and j16 == 0:
                            w2.append(("wld", wtk[jh]))
                            w2.append(("agl", agl))
                            if st and g8 == 0:
                                w2.append(("aev", T("aev")))
                                w2.append(("bcc", T("bcc")))
                                w2.append(("dnl", T("dnl")))
                        if st and j16 == 0:
                            bt = wo_bank_rd.get(t8)
                            if bt:
                                w2.append(("x1e", bt))
                        E("tensor", lambda e, b=jh, t8=t8, j16=j16, j=j, st=st:
                          e.matmul(out=PB(t8),
                                   lhsT=big[:, NT * j + 128 * t8:NT * j + 128 * (t8 + 1)],
                                   rhs=wb[b][:, 512 * j16:512 * (j16 + 1)],
                                   start=st, stop=False),
                          waits=w2,
                          inc="ojh" if (j16 == 15) else None)
                    if jh == 1:
                        bm = E("tensor", lambda e, t8=t8, g8=g8:
                               e.matmul(out=PB(t8), lhsT=s_or[:],
                                        rhs=bob[0:1, 512 * g8:512 * (g8 + 1)],
                                        start=False, stop=True),
                               waits=[("brl", brl_bo)], inc="omm")
                        slot = (g8 * 8 + t8) % 2
                        wx = []
                        if xres_free.get(slot) is not None:
                            wx.append(("x1e", xres_free[slot]))
                        xl = E("sync", lambda e, s=slot, t8=t8, g8=g8:
                               e.dma_start(out=xres[s][:],
                                           in_=xs[128 * t8:128 * (t8 + 1),
                                                  512 * g8:512 * (g8 + 1)]),
                               waits=wx, inc="xrl", amt=16)
                        w1 = [("omm", bm), ("xrl", xl)]
                        if x1_free.get(slot) is not None:
                            w1.append(("x1s", x1_free[slot]))
                        x1e = E("vector", lambda e, t8=t8, s=slot:
                                e.tensor_add(out=x1st[s][:], in0=PB(t8), in1=xres[s][:]),
                                waits=w1, inc="x1e")
                        xres_free[slot] = x1e
                        wo_bank_rd[t8] = x1e
                        x1_free[slot] = E("scalar", lambda e, s=slot, t8=t8, g8=g8:
                                          e.dma_start(out=x1d.ap()[128 * t8:128 * (t8 + 1),
                                                                   512 * g8:512 * (g8 + 1)],
                                                      in_=x1st[s][:]),
                                          waits=[("x1e", x1e)], inc="x1s", amt=16)
                strip_free[jh] = T("ojh")

        btl = E("gpsimd", lambda e: e.dma_start(out=wb[0][0:64, 0:4096], in_=bTm),
                waits=[("ojh", T("ojh")), ("omm", T("omm"))], inc="btl", amt=16)
        bTb = wb[0][0:64, 0:4096]
        xn2f = wb[1][:, 0:4096]
        tr_bank = {5: None, 6: None}
        pml_rd = None
        pmlt_rd = None
        pmgT_rd = None
        pd_rd = {3: None, 4: None}
        ost_free = [None] * 2
        xa_free = {0: None, 1: None}
        lnx2_tok = {}
        L = lambda ee: mlt[:, 64 + ee:65 + ee]
        for t8 in range(8):
            wxl = [("x1s", T("x1s"))]
            if lnx2_tok.get(t8 % 2) is not None:
                wxl.append(("lnx", lnx2_tok[t8 % 2]))
            if xa_free[t8 % 2] is not None:
                wxl.append(("ooe", xa_free[t8 % 2]))
            xl = E("sync", lambda e, t=t8: e.dma_start(out=xa[t % 2][:],
                                                       in_=x1d.ap()[128 * t:128 * (t + 1), :]),
                   waits=wxl, inc="xld", amt=16)
            a_x = ln_tile(xa[t8 % 2], xn[t8 % 2], t8 % 2, [("xld", xl)])
            lnx2_tok[t8 % 2] = a_x
            for g in range(4):
                bank = 5 + ((t8 * 4 + g) % 2)
                w = [("lnx", a_x), ("ojh", T("ojh"))]
                if t8 == 0 and g < 2:
                    w.append(("x1e", T("x1e")))
                if tr_bank[bank] is not None:
                    w.append(("tev", tr_bank[bank]))
                last = None
                for j2 in range(8):
                    j = 8 * g + j2
                    last = E("tensor", lambda e, b=bank, j2=j2, j=j, t=t8: e.transpose(
                        out=PBb(b)[:, 128 * j2:128 * (j2 + 1)],
                        in_=xn[t % 2][:, 128 * j:128 * (j + 1)], identity=s_id[:]),
                        waits=(w if j2 == 0 else []), inc=("ttr" if j2 == 7 else None))
                tr_bank[bank] = E("vector", lambda e, b=bank, g=g: e.tensor_copy(
                    out=xn2f[:, 1024 * g:1024 * (g + 1)], in_=PBb(b)[:]),
                    waits=[("ttr", last), ("ojh", T("ojh"))], inc="tev")
            tevt = T("tev")
            for j in range(32):
                w2 = []
                if j == 0:
                    w2 = [("tev", tevt), ("cst", cst_all)]
                    if pml_rd is not None:
                        w2.append(("mle", pml_rd))
                    if t8 == 0:
                        w2.append(("x1e", T("x1e")))
                E("tensor", lambda e, j=j: e.matmul(out=pall[0:68, 0:128],
                                                    lhsT=rhswb[:, 68 * j:68 * (j + 1)],
                                                    rhs=xn2f[:, 128 * j:128 * (j + 1)],
                                                    start=(j == 0), stop=(j == 31)),
                  waits=w2, inc=("mlm" if j == 31 else None))
            pml_rd = E("vector", lambda e: e.tensor_copy(out=sml[0:68, :], in_=pall[0:68, 0:128]),
                       waits=[("mlm", T("mlm"))], inc="mle")
            w3 = [("mle", pml_rd)]
            if pmlt_rd is not None:
                w3.append(("mlte", pmlt_rd))
            if t8 == 0:
                w3.append(("x1e", T("x1e")))
            trm = E("tensor", lambda e: e.transpose(out=PBb(1)[:, 0:68], in_=sml[0:68, :],
                                                    identity=s_id[0:68, 0:68]),
                    waits=w3, inc="mltm")
            pmlt_rd = E("vector", lambda e: e.tensor_copy(out=mlt[:], in_=PBb(1)[:, 0:68]),
                        waits=[("mltm", trm)], inc="mlte")
            fw = [("mlte", pmlt_rd)]
            for e_i in range(4):
                terms = [(a_i, ALU.is_ge if a_i < e_i else ALU.is_gt)
                         for a_i in range(4) if a_i != e_i]
                for k_i, (a_i, op) in enumerate(terms):
                    if k_i == 0:
                        E("vector", lambda e, a=a_i, ei=e_i, op=op: e.tensor_tensor(
                            out=gl[:, ei:ei + 1], in0=L(a), in1=L(ei), op=op), waits=fw,
                          fence=True)
                        fw = []
                    else:
                        E("vector", lambda e, a=a_i, ei=e_i, op=op: e.tensor_tensor(
                            out=gl[:, 4:5], in0=L(a), in1=L(ei), op=op), fence=True)
                        E("vector", lambda e, ei=e_i: e.tensor_add(
                            out=gl[:, ei:ei + 1], in0=gl[:, ei:ei + 1], in1=gl[:, 4:5]),
                          fence=True)
            E("vector", lambda e: e.tensor_max(out=gl[:, 5:6], in0=L(0), in1=L(1)), fence=True)
            E("vector", lambda e: e.tensor_max(out=gl[:, 5:6], in0=gl[:, 5:6], in1=L(2)),
              fence=True)
            E("vector", lambda e: e.tensor_max(out=gl[:, 5:6], in0=gl[:, 5:6], in1=L(3)),
              fence=True)
            vnm = E("vector", lambda e: e.tensor_scalar(out=gl[:, 6:7], in0=gl[:, 5:6],
                                                        scalar1=-1.0, scalar2=None,
                                                        op0=ALU.mult), inc="gnm", fence=True)
            for e_i in range(4):
                E("scalar", lambda e, ei=e_i: e.activation(out=gl[:, 7 + ei:8 + ei], in_=L(ei),
                                                           func=AF.Exp, bias=gl[:, 6:7],
                                                           scale=1.0),
                  waits=([("gnm", vnm)] if e_i == 0 else []),
                  inc=("gee" if e_i == 3 else None))
            gee = T("gee")
            for e_i in range(4):
                E("vector", lambda e, ei=e_i: e.tensor_single_scalar(
                    out=gl[:, 11 + ei:12 + ei], in_=gl[:, ei:ei + 1], scalar=1.5, op=ALU.is_le),
                  waits=([("gee", gee)] if e_i == 0 else []), fence=True)
            for e_i in range(4):
                E("vector", lambda e, ei=e_i: e.tensor_mul(out=gl[:, 15 + ei:16 + ei],
                                                           in0=gl[:, 7 + ei:8 + ei],
                                                           in1=gl[:, 11 + ei:12 + ei]),
                  fence=True)
            E("vector", lambda e: e.tensor_add(out=gl[:, 19:20], in0=gl[:, 15:16],
                                               in1=gl[:, 16:17]), fence=True)
            E("vector", lambda e: e.tensor_add(out=gl[:, 19:20], in0=gl[:, 19:20],
                                               in1=gl[:, 17:18]), fence=True)
            vdn = E("vector", lambda e: e.tensor_add(out=gl[:, 19:20], in0=gl[:, 19:20],
                                                     in1=gl[:, 18:19]), inc="gdn", fence=True)
            E("scalar", lambda e: e.activation(out=gl[:, 20:21], in_=gl[:, 19:20],
                                               func=AF.Ln, bias=0.0, scale=1.0),
              waits=[("gdn", vdn)])
            grg = E("scalar", lambda e: e.activation(out=gl[:, 20:21], in_=gl[:, 20:21],
                                                     func=AF.Exp, bias=0.0, scale=-1.0),
                    inc="grg", fence=True)
            for e_i in range(4):
                E("vector", lambda e, ei=e_i: e.tensor_scalar(
                    out=gl[:, 21 + ei:22 + ei], in0=gl[:, 15 + ei:16 + ei],
                    scalar1=gl[:, 20:21], scalar2=4.0, op0=ALU.mult, op1=ALU.mult),
                  waits=([("grg", grg)] if e_i == 0 else []), fence=True)
            gms = None
            for e_i in range(4):
                gms = E("vector", lambda e, ei=e_i: e.tensor_scalar(
                    out=mg[:, 16 * ei:16 * (ei + 1)], in0=mlt[:, 16 * ei:16 * (ei + 1)],
                    scalar1=gl[:, 21 + ei:22 + ei], scalar2=None, op0=ALU.mult),
                    inc=("gms" if e_i == 3 else None), fence=True)
            gms = T("gms")
            w4 = [("gms", gms)]
            if pmgT_rd is not None:
                w4.append(("mge", pmgT_rd))
            if t8 == 0:
                w4.append(("x1e", T("x1e")))
            E("tensor", lambda e: e.transpose(out=PBb(2)[0:64, 0:128], in_=mg[:, 0:64],
                                              identity=s_id[:]),
              waits=w4, inc="mgm")
            pmgT_rd = E("vector", lambda e: e.tensor_copy(out=mgT[:], in_=PBb(2)[0:64, 0:128]),
                        waits=[("mgm", T("mgm"))], inc="mge")
            mge = pmgT_rd
            last_ooe = None
            for g8 in range(8):
                bank = 3 + (g8 % 2)
                wd = [("mge", mge), ("btl", btl)]
                if pd_rd[bank] is not None:
                    wd.append(("ooe", pd_rd[bank]))
                if t8 == 0 and g8 < 2:
                    wd.append(("x1e", T("x1e")))
                dm = E("tensor", lambda e, bank=bank, g8=g8: e.matmul(
                    out=PB(bank), lhsT=mgT[0:64, :], rhs=bTb[0:64, 512 * g8:512 * (g8 + 1)],
                    start=True, stop=True), waits=wd, inc="dm")
                oslot = (t8 * 8 + g8) % 2
                wo_ = [("dm", dm)]
                if ost_free[oslot] is not None:
                    wo_.append(("outs", ost_free[oslot]))
                ooe = E("vector", lambda e, bank=bank, s=oslot, t=t8, g8=g8: e.tensor_add(
                    out=ost[s][:], in0=PB(bank), in1=xa[t % 2][:, 512 * g8:512 * (g8 + 1)]),
                    waits=wo_, inc="ooe")
                pd_rd[bank] = ooe
                last_ooe = ooe
                ost_free[oslot] = E("scalar", lambda e, s=oslot, t=t8, g8=g8: e.dma_start(
                    out=out[128 * t:128 * (t + 1), 512 * g8:512 * (g8 + 1)], in_=ost[s][:]),
                    waits=[("ooe", ooe)], inc="outs", amt=16)
            xa_free[t8 % 2] = last_ooe

        if taps:
            E("gpsimd", lambda e: e.dma_start(out=tap["ksend"].ap(), in_=k_send.ap()),
              waits=[("kst", T("kst"))], inc="tapd", amt=16)
            E("gpsimd", lambda e: e.dma_start(out=tap["qsend"].ap(), in_=q_send.ap()),
              waits=[("qst", T("qst")), ("mxst", T("mxst"))], inc="tapd", amt=16)
            E("gpsimd", lambda e: e.dma_start(out=tap["vsend"].ap(), in_=v_send.ap()),
              waits=[("vst", T("vst"))], inc="tapd", amt=16)
            E("gpsimd", lambda e: e.dma_start(out=tap["aosend"].ap(), in_=ao_send.ap()),
              waits=[("aos", T("aos"))], inc="tapd", amt=16)
            E("gpsimd", lambda e: e.dma_start(out=tap["aog"].ap(), in_=aogD.ap()),
              waits=[("cca", cca)], inc="tapd", amt=16)
            E("gpsimd", lambda e: e.dma_start(out=tap["x1"].ap(), in_=x1d.ap()),
              waits=[("x1s", T("x1s"))], inc="tapd", amt=16)
            E("gpsimd", lambda e: e.dma_start(out=tap["kg"].ap(), in_=kgD.ap()),
              waits=[("cck", cck)], inc="tapd", amt=16)
            E("gpsimd", lambda e: e.dma_start(out=tap["qg"].ap(), in_=qgD.ap()),
              waits=[("ccq", ccq)], inc="tapd", amt=16)
            E("gpsimd", lambda e: e.dma_start(out=tap["vg"].ap(), in_=vgD.ap()),
              waits=[("ccv", ccv)], inc="tapd", amt=16)
            S.wait_only("gpsimd", [("tapd", T("tapd"))])

        S.wait_only("scalar", [("outs", T("outs"))])

        for chname in S.cnt:
            sems[chname] = ctx.enter_context(nc.semaphore(chname))
        with allow_ici_all_to_all():
            with nc.Block() as block:
                S.replay(block, sems)
    return nc


def _prep_inputs(L0_attn_x, L0_attn_Wq, L0_attn_bq, L0_attn_Wk, L0_attn_bk,
                 L0_attn_Wv, L0_attn_bv, L0_attn_Wo, L0_attn_bo,
                 L0_ffn_A, L0_ffn_B, L0_ffn_Wr):
    x = np.asarray(L0_attn_x, dtype=np.float32)
    C = lambda a: np.ascontiguousarray(np.asarray(a, dtype=np.float32))
    A = np.asarray(L0_ffn_A, dtype=np.float32)
    B = np.asarray(L0_ffn_B, dtype=np.float32)
    Wr = np.asarray(L0_ffn_Wr, dtype=np.float32)
    aT = A.transpose(2, 0, 1).reshape(D, 64)
    wrT = np.asarray(Wr).T
    rhsw = np.concatenate([aT.reshape(32, 128, 64),
                           np.ascontiguousarray(wrT).reshape(32, 128, 4)], axis=2)
    shared = dict(
        wqT=C(np.asarray(L0_attn_Wq).T), wkT=C(np.asarray(L0_attn_Wk).T),
        wvT=C(np.asarray(L0_attn_Wv).T), woT=C(np.asarray(L0_attn_Wo).T),
        bqc=C(np.asarray(L0_attn_bq).reshape(32, 128).T),
        bkc=C(np.asarray(L0_attn_bk).reshape(32, 128).T),
        bvr=C(np.asarray(L0_attn_bv).reshape(1, D)),
        bor=C(np.asarray(L0_attn_bo).reshape(1, D)),
        rhsw=C(rhsw.transpose(1, 0, 2).reshape(128, 32 * 68)),
        bTm=C(B.transpose(0, 2, 1).reshape(64, D)))
    in_maps = []
    for c in range(8):
        g, qc = c // 4, c % 4
        xsl = np.concatenate([x[g, 512 * qc:512 * (qc + 1)],
                              x[g, 512 * (7 - qc):512 * (8 - qc)]], axis=0)
        m = dict(shared)
        m["xs"] = np.ascontiguousarray(xsl)
        in_maps.append(m)
    return in_maps


def kernel(**inputs):
    in_maps = _prep_inputs(**inputs)
    with allow_ici_all_to_all():
        nc = build_nc(taps=_TAPS)
        res = run_bass_kernel_spmd(nc, in_maps, core_ids=list(range(8)))
    outf = np.zeros((2, D, D), dtype=np.float32)
    for c in range(8):
        g, qc = c // 4, c % 4
        o = res.results[c]["out"]
        outf[g, 512 * qc:512 * (qc + 1)] = o[0:512]
        outf[g, 512 * (7 - qc):512 * (8 - qc)] = o[512:1024]
    if _TAPS:
        kernel._tap_results = res.results
    return outf


# version 06
# speedup vs baseline: 1.3694x; 1.0005x over previous
import math
from contextlib import ExitStack

import numpy as np
import ml_dtypes

import concourse.bass as bass
import concourse.mybir as mybir
from concourse.bass_utils import run_bass_kernel_spmd
from concourse.aot_env import allow_ici_all_to_all

F32 = mybir.dt.float32
BF16 = mybir.dt.bfloat16
AF = mybir.ActivationFunctionType
ALU = mybir.AluOpType
AXX = mybir.AxisListType.X

D = 4096
NT = 1024
NP = 8
SCLQ = 1.0 / math.sqrt(128.0)
EPS = 1e-5
_TAPS = False

ENGS = ("sync", "gpsimd", "scalar", "vector", "tensor")


def _rho(B):
    return B if B < 4 else 7 - B


def _co(B):
    return 0 if B < 4 else 512


class Sched:
    def __init__(self):
        self.ops = {e: [] for e in ENGS}
        self.cnt = {}
        self._pend = {e: [] for e in ENGS}

    def tok(self, ch):
        return self.cnt.get(ch, 0)

    def emit(self, eng, fn, waits=(), inc=None, amt=1, fence=False):
        if fence:
            self.fence(eng)
        waits = list(waits) + self._pend[eng]
        self._pend[eng] = []
        if inc is not None:
            self.cnt[inc] = self.cnt.get(inc, 0) + amt
        self.ops[eng].append((tuple((c, v) for c, v in waits if v), fn, inc, amt))
        return self.cnt.get(inc, 0)

    def fence(self, eng):
        if not self.ops[eng]:
            return
        w, fn, inc, amt = self.ops[eng][-1]
        if fn is None:
            return
        if inc is None:
            ch = "sf_" + eng
            self.cnt[ch] = self.cnt.get(ch, 0) + 1
            self.ops[eng][-1] = (w, fn, ch, 1)
            self._pend[eng].append((ch, self.cnt[ch]))
        else:
            self._pend[eng].append((inc, self.cnt[inc]))

    def wait_only(self, eng, waits):
        self.ops[eng].append((tuple(waits), None, None, 0))

    def replay(self, block, sems):
        def mk(ename):
            def body(eng):
                seen = {}
                for waits, fn, inc, amt in self.ops[ename]:
                    for ch, val in waits:
                        if val > seen.get(ch, 0):
                            eng.wait_ge(sems[ch], val)
                            seen[ch] = val
                    if fn is not None:
                        ins = fn(eng)
                        if inc is not None:
                            ins.then_inc(sems[inc], amt)
            return body

        block.sync(mk("sync"))
        block.gpsimd(mk("gpsimd"))
        block.scalar(mk("scalar"))
        block.vector(mk("vector"))
        block.tensor(mk("tensor"))


def build_nc(taps=False):
    nc = bass.Bass("TRN2", target_bir_lowering=False)

    xs = nc.dram_tensor("xs", [NT, D], F32, kind="ExternalInput").ap()
    wqT = nc.dram_tensor("wqT", [D, D], F32, kind="ExternalInput").ap()
    wkT = nc.dram_tensor("wkT", [D, D], F32, kind="ExternalInput").ap()
    wvT = nc.dram_tensor("wvT", [D, D], F32, kind="ExternalInput").ap()
    woT = nc.dram_tensor("woT", [D, D], F32, kind="ExternalInput").ap()
    bqc = nc.dram_tensor("bqc", [128, 32], F32, kind="ExternalInput").ap()
    bkc = nc.dram_tensor("bkc", [128, 32], F32, kind="ExternalInput").ap()
    bvr = nc.dram_tensor("bvr", [1, D], F32, kind="ExternalInput").ap()
    bor = nc.dram_tensor("bor", [1, D], F32, kind="ExternalInput").ap()
    rhsw = nc.dram_tensor("rhsw", [128, 32 * 68], F32, kind="ExternalInput").ap()
    bTm = nc.dram_tensor("bTm", [64, D], F32, kind="ExternalInput").ap()
    out = nc.dram_tensor("out", [NT, D], F32, kind="ExternalOutput").ap()

    ident = nc.inline_tensor(np.eye(128, dtype=ml_dtypes.bfloat16), "ident")
    ones_col = nc.inline_tensor(np.ones((128, 1), dtype=ml_dtypes.bfloat16), "ones_col")
    ones_row = nc.inline_tensor(np.ones((1, 128), dtype=ml_dtypes.bfloat16), "ones_row")
    ones_row_f = nc.inline_tensor(np.ones((1, 128), dtype=np.float32), "ones_row_f")
    mknp = np.zeros((128, 4 * 512), dtype=ml_dtypes.bfloat16)
    pp = np.arange(128)[:, None]
    jj_ = np.arange(512)[None, :]
    for u in range(4):
        mknp[:, 512 * u:512 * (u + 1)] = ((128 * u + pp) <= jj_).astype(ml_dtypes.bfloat16)
    maskt = nc.inline_tensor(mknp, "maskt")

    k_send = nc.dram_tensor("k_send", [8 * 512, NT], BF16, kind="Internal")
    q_send = nc.dram_tensor("q_send", [8 * 512, NT], BF16, kind="Internal")
    v_send = nc.dram_tensor("v_send", [8 * 1024, 512], BF16, kind="Internal")
    kgD = nc.dram_tensor("kgD", [8 * 512, NT], BF16, kind="Internal")
    qgD = nc.dram_tensor("qgD", [8 * 512, NT], BF16, kind="Internal")
    vgD = nc.dram_tensor("vgD", [8 * 1024, 512], BF16, kind="Internal")
    ao_send = nc.dram_tensor("ao_send", [8 * 512, NT], BF16, kind="Internal")
    aogD = nc.dram_tensor("aogD", [8 * 512, NT], BF16, kind="Internal")
    x1d = nc.dram_tensor("x1d", [NT, D], F32, kind="Internal")
    tap = {}
    if taps:
        for n, sh, dt in [("ksend", (8 * 512, NT), BF16), ("qsend", (8 * 512, NT), BF16),
                          ("vsend", (8 * 1024, 512), BF16), ("aosend", (8 * 512, NT), BF16),
                          ("aog", (8 * 512, NT), BF16), ("x1", (NT, D), F32),
                          ("kg", (8 * 512, NT), BF16), ("qg", (8 * 512, NT), BF16),
                          ("vg", (8 * 1024, 512), BF16)]:
            tap[n] = nc.dram_tensor("tap_" + n, list(sh), dt, kind="ExternalOutput")

    S = Sched()
    E = S.emit
    T = S.tok

    with ExitStack() as ctx:
        SB = lambda n, sh, dt: ctx.enter_context(nc.sbuf_tensor(n, sh, dt))
        big = SB("big", [128, 32 * NT], BF16)
        wb = [SB(f"wb{i}", [128, 8192], BF16) for i in range(2)]
        xa = [SB(f"xa{i}", [128, D], F32) for i in range(2)]
        xn = [SB(f"xn{i}", [128, D], BF16) for i in range(2)]
        dump = SB("dump", [128, D], BF16)
        kst = [SB(f"kst{i}", [128, 512], BF16) for i in range(4)]
        k2t = [SB(f"k2t{i}", [128, 512], BF16) for i in range(2)]
        attT = [SB(f"attT{i}", [128, 512], BF16) for i in range(4)]
        rbc = [SB(f"rbc{i}", [128, 512], F32) for i in range(2)]
        aost = [SB(f"aost{i}", [128, 512], BF16) for i in range(3)]
        xres = [SB(f"xres{i}", [128, 512], F32) for i in range(2)]
        x1st = [SB(f"x1st{i}", [128, 512], F32) for i in range(2)]
        ost = [SB(f"ost{i}", [128, 512], F32) for i in range(2)]
        sml = SB("sml", [128, 128], BF16)
        mlt = SB("mlt", [128, 136], BF16)
        mg = SB("mg", [128, 128], BF16)
        mgT = SB("mgT", [64, 256], BF16)
        gl = SB("gl", [128, 56], F32)
        s_id = SB("s_id", [128, 128], BF16)
        s_msk = SB("s_msk", [128, 4 * 512], BF16)
        s_oc = SB("s_oc", [128, 1], BF16)
        s_or = SB("s_or", [1, 128], BF16)
        s_orf = SB("s_orf", [1, 128], F32)
        s_bq = SB("s_bq", [128, 32], F32)
        s_bqs = SB("s_bqs", [128, 32], F32)
        s_bk = SB("s_bk", [128, 32], F32)
        rhswb = SB("rhswb", [128, 32 * 68], BF16)
        urow = SB("urow", [128, 2048], F32)
        mqct = SB("mqct", [128, 8], BF16)
        mqt = SB("mqt", [128, 1], BF16)
        nmqt = SB("nmqt", [1, 1024], BF16)
        stat = SB("stat", [128, 16], F32)
        epsc = SB("epsc", [128, 1], F32)
        pall = ctx.enter_context(nc.psum_tensor("pall", [128, 4096], F32))
        sems = {}

        def PB(b):
            return pall[:, 512 * b:512 * (b + 1)]

        def PBb(b):
            return pall[:, 512 * b:512 * (b + 1)].bitcast(BF16)

        bvb = dump[0:1, :]
        bob = dump[0:1, :]
        r_t = urow[0:1, 0:512]
        r_rd = [urow[0:1, 512:1024], urow[0:1, 1024:1536]]
        r_ksq = urow[0:1, 1536:1568]
        r_qsq = urow[0:1, 1568:1600]
        r_m2 = urow[0:1, 1600:1608]
        r_mro = urow[0:1, 1608:1616]
        r_nM = urow[0:1, 1616:1624]
        r_mx1 = urow[0:1, 1624:1632]

        for dst, src in [(s_id[:], ident.ap()), (s_msk[:], maskt.ap()),
                         (s_oc[:], ones_col.ap()), (s_or[:], ones_row.ap()),
                         (s_orf[:], ones_row_f.ap()), (s_bq[:], bqc), (s_bk[:], bkc)]:
            E("sync", lambda e, d=dst, s=src: e.dma_start(out=d, in_=s), inc="cst", amt=16)
        for dst, src in [(rhswb[:], rhsw)]:
            E("gpsimd", lambda e, d=dst, s=src: e.dma_start(out=d, in_=s), inc="cst", amt=16)
        cst_all = T("cst")
        E("vector", lambda e: e.tensor_scalar(out=s_bqs[:], in0=s_bq[:], scalar1=SCLQ,
                                              scalar2=None, op0=ALU.mult),
          waits=[("cst", cst_all)], inc="prep")
        E("vector", lambda e: e.memset(urow[0:1, 1536:1600], -3.0e38), inc="prep")
        prep_all = E("vector", lambda e: e.memset(epsc[:], EPS), inc="prep")

        def ln_tile(xa_t, xn_t, par, wfirst):
            o = 8 * par
            c_s1, c_s2, c_mean, c_var, c_ms, c_rstd, c_nmr, c_m2 = [
                stat[:, o + i:o + i + 1] for i in range(8)]
            E("vector", lambda e: e.tensor_scalar(out=dump[:], in0=xa_t[:], scalar1=1.0,
                                                  scalar2=0.0, op0=ALU.mult, op1=ALU.add,
                                                  accum_out=c_s1),
              waits=wfirst)
            E("vector", lambda e: e.scalar_tensor_tensor(out=dump[:], in0=xa_t[:], scalar=1.0,
                                                         in1=xa_t[:], op0=ALU.mult, op1=ALU.mult,
                                                         accum_out=c_s2))
            E("vector", lambda e: e.tensor_scalar(out=c_mean, in0=c_s1, scalar1=1.0 / D,
                                                  scalar2=None, op0=ALU.mult), fence=True)
            E("vector", lambda e: e.scalar_tensor_tensor(out=c_var, in0=c_s1, scalar=c_mean,
                                                         in1=c_s2, op0=ALU.mult,
                                                         op1=ALU.subtract), fence=True)
            v_var = E("vector", lambda e: e.tensor_scalar(out=c_var, in0=c_var,
                                                          scalar1=-1.0 / D, scalar2=None,
                                                          op0=ALU.mult), inc="lnv", fence=True)
            E("scalar", lambda e: e.activation(out=c_rstd, in_=c_var, func=AF.Ln,
                                               bias=epsc[:], scale=1.0),
              waits=[("lnv", v_var), ("prep", prep_all)])
            a_r = E("scalar", lambda e: e.activation(out=c_rstd, in_=c_rstd, func=AF.Exp,
                                                     bias=0.0, scale=-0.5), inc="lnr",
                    fence=True)
            v_n = E("vector", lambda e: e.tensor_scalar(out=c_nmr, in0=c_mean, scalar1=c_rstd,
                                                        scalar2=-1.0, op0=ALU.mult, op1=ALU.mult),
                    waits=[("lnr", a_r)], inc="lnn")
            a_x = E("scalar", lambda e: e.activation(out=xn_t[:], in_=xa_t[:], func=AF.Identity,
                                                     bias=c_nmr, scale=c_rstd),
                    waits=[("lnn", v_n)], inc="lnx")
            return a_x

        tev_bank = {0: 0, 1: 0}
        lnx_tok = {}
        for t8 in range(8):
            wx = [("lnx", lnx_tok[t8 % 2])] if t8 >= 2 else []
            xl = E("sync", lambda e, t=t8: e.dma_start(out=xa[t % 2][:],
                                                       in_=xs[128 * t:128 * (t + 1), :]),
                   waits=wx, inc="xld", amt=16)
            a_x = ln_tile(xa[t8 % 2], xn[t8 % 2], t8 % 2,
                          [("xld", xl), ("cst", cst_all)])
            lnx_tok[t8 % 2] = a_x
            for g in range(4):
                bank = (t8 * 4 + g) % 2
                w = [("lnx", a_x)]
                if tev_bank[bank]:
                    w.append(("tev", tev_bank[bank]))
                last = None
                for j2 in range(8):
                    j = 8 * g + j2
                    last = E("tensor", lambda e, b=bank, j2=j2, j=j, t=t8: e.transpose(
                        out=PBb(b)[:, 128 * j2:128 * (j2 + 1)],
                        in_=xn[t % 2][:, 128 * j:128 * (j + 1)], identity=s_id[:]),
                        waits=(w if j2 == 0 else []), inc=("ttr" if j2 == 7 else None))
                dst = big[:].rearrange("p (j c) -> p j c", c=NT)[:, 8 * g:8 * (g + 1),
                                                                128 * t8:128 * (t8 + 1)]
                src = PBb(bank)[:].rearrange("p (j c) -> p j c", c=128)
                tev_bank[bank] = E("vector", lambda e, d=dst, s=src: e.tensor_copy(out=d, in_=s),
                                   waits=[("ttr", last)], inc="tev")
        tev_all = T("tev")
        brl_all = E("gpsimd", lambda e: e.dma_start(out=bvb, in_=bvr),
                    waits=[("lnv", T("lnv"))], inc="brl", amt=16)

        slotg = {}

        def proj_phase(wT, pref, bias_t, scale, send, entry_guard):
            strip_free = {0: None, 1: None}
            for hg in range(8):
                ev_before = T(pref + "ev")
                wtk = {}
                for jh in range(2):
                    buf = jh
                    src = wT[2048 * jh:2048 * (jh + 1), 512 * hg:512 * (hg + 1)].rearrange(
                        "(j p) c -> p j c", p=128)
                    dstw = wb[buf][:].rearrange("p (j c) -> p j c", c=512)
                    w = [("tev", tev_all)]
                    if strip_free[buf] is not None:
                        w.append((pref + "mm", strip_free[buf]))
                    if pref == "q":
                        w.append(("kmm", T("kmm")))
                    wtk[jh] = E("gpsimd", lambda e, s=src, d=dstw: e.dma_start(out=d, in_=s),
                                waits=w, inc="wld", amt=16)
                for jh in range(2):
                    for hh in range(4):
                        for j16 in range(16):
                            for half in range(2):
                                j = 16 * jh + j16
                                st = (jh == 0 and j16 == 0)
                                sp = (jh == 1 and j16 == 15)
                                w2 = []
                                if hh == 0 and j16 == 0 and half == 0:
                                    w2.append(("wld", wtk[jh]))
                                if jh == 0 and j16 == 0 and half == 0:
                                    bt = S.bank_ev.get((pref, hh))
                                    if bt:
                                        w2.append((pref + "ev", bt))
                                    if hg == 0:
                                        w2.extend(entry_guard)
                                E("tensor",
                                  lambda e, b=jh, hh=hh, j16=j16, j=j, half=half, st=st, sp=sp:
                                  e.matmul(out=pall[:, 1024 * hh + 512 * half:
                                                    1024 * hh + 512 * half + 512],
                                           lhsT=wb[b][:, 512 * j16 + 128 * hh:
                                                      512 * j16 + 128 * (hh + 1)],
                                           rhs=big[:, NT * j + 512 * half:NT * j + 512 * half + 512],
                                           start=st, stop=sp),
                                  waits=w2,
                                  inc=(pref + "mm") if (j16 == 15 and half == 1)
                                  else None)
                            if jh == 1 and j16 == 15:
                                mmt = T(pref + "mm")
                                h = 4 * hg + hh
                                evl = None
                                for half2 in range(2):
                                    slot = (hg * 8 + hh * 2 + half2) % 4
                                    wv = [(pref + "mm", mmt), ("prep", prep_all)] +                                         slotg.get(slot, [])
                                    ev = E("scalar", lambda e, hh=hh, half=half2, h=h, s=slot:
                                           e.activation(out=kst[s][:],
                                                        in_=pall[:, 1024 * hh + 512 * half:
                                                                 1024 * hh + 512 * half + 512],
                                                        func=AF.Identity,
                                                        bias=bias_t[:, h:h + 1],
                                                        scale=scale),
                                           waits=wv, inc=pref + "ev")
                                    evl = ev
                                    r0 = 512 * (h // 4) + 128 * (h % 4)
                                    stk = E("scalar", lambda e, s=slot, r0=r0, half=half2:
                                            e.dma_start(out=send.ap()[r0:r0 + 128,
                                                                      512 * half:
                                                                      512 * (half + 1)],
                                                        in_=kst[s][:]),
                                            waits=[(pref + "ev", ev)],
                                            inc=pref + "st", amt=16)
                                    slotg[slot] = [(pref + "st", stk)]
                                S.bank_ev[(pref, hh)] = evl
                    strip_free[jh] = T(pref + "mm")

        S.bank_ev = {}
        proj_phase(wkT, "k", s_bk, 1.0, k_send, [])

        proj_phase(wqT, "q", s_bqs, SCLQ, q_send, [("kev", T("kev"))])

        strip_free = {0: None, 1: None}
        vbank_rd = {}
        for g8 in range(8):
            wtk = {}
            for jh in range(2):
                src = wvT[2048 * jh:2048 * (jh + 1), 512 * g8:512 * (g8 + 1)].rearrange(
                    "(j p) c -> p j c", p=128)
                dstw = wb[jh][:].rearrange("p (j c) -> p j c", c=512)
                w = [("qmm", T("qmm"))]
                if strip_free[jh] is not None:
                    w.append(("vjh", strip_free[jh]))
                wtk[jh] = E("gpsimd", lambda e, s=src, d=dstw: e.dma_start(out=d, in_=s),
                            waits=w, inc="wld", amt=16)
            for jh in range(2):
                for t8 in range(8):
                    for j16 in range(16):
                        j = 16 * jh + j16
                        st = (jh == 0 and j16 == 0)
                        w2 = []
                        if t8 == 0 and j16 == 0:
                            w2.append(("wld", wtk[jh]))
                            if st:
                                w2.append(("qev", T("qev")))
                        if st and j16 == 0:
                            bt = vbank_rd.get(t8)
                            if bt:
                                w2.append(("vev", bt))
                        E("tensor", lambda e, b=jh, t8=t8, j16=j16, j=j, st=st:
                          e.matmul(out=PB(t8),
                                   lhsT=big[:, NT * j + 128 * t8:NT * j + 128 * (t8 + 1)],
                                   rhs=wb[b][:, 512 * j16:512 * (j16 + 1)],
                                   start=st, stop=False),
                          waits=w2,
                          inc="vjh" if (j16 == 15) else None)
                    if jh == 1:
                        bm = E("tensor", lambda e, t8=t8, g8=g8:
                               e.matmul(out=PB(t8), lhsT=s_or[:],
                                        rhs=bvb[0:1, 512 * g8:512 * (g8 + 1)],
                                        start=False, stop=True),
                               waits=[("cst", cst_all), ("brl", brl_all)], inc="vmm")
                        slot = (g8 * 8 + t8) % 4
                        wv = [("vmm", bm)] + slotg.get(slot, [])
                        ev = E("vector", lambda e, t8=t8, s=slot: e.tensor_copy(
                            out=kst[s][:], in_=PB(t8)),
                            waits=wv, inc="vev")
                        vbank_rd[t8] = ev
                        stk = E("scalar", lambda e, s=slot, g8=g8, t8=t8:
                                e.dma_start(out=v_send.ap()[1024 * g8 + 128 * t8:
                                                            1024 * g8 + 128 * (t8 + 1), :],
                                            in_=kst[s][:]),
                                waits=[("vev", ev)], inc="vst", amt=16)
                        slotg[slot] = [("vst", stk)]
                strip_free[jh] = T("vjh")
        brl_bo = E("gpsimd", lambda e: e.dma_start(out=bob, in_=bor),
                   waits=[("vmm", T("vmm"))], inc="brl", amt=16)

        cck = E("gpsimd", lambda e: e.collective_compute(
            "AllToAll", ALU.bypass, replica_groups=[list(range(8))],
            ins=[k_send.ap().opt()], outs=[kgD.ap().opt()]),
            waits=[("kst", T("kst")), ("qst", T("qst")), ("vst", T("vst")),
                   ("qmm", T("qmm")), ("vmm", T("vmm")), ("xld", T("xld")),
                   ("cst", cst_all), ("brl", T("brl")), ("wld", T("wld"))], inc="cck")
        ccq = E("gpsimd", lambda e: e.collective_compute(
            "AllToAll", ALU.bypass, replica_groups=[list(range(8))],
            ins=[q_send.ap().opt()], outs=[qgD.ap().opt()]),
            waits=[("cck", cck)], inc="ccq")
        ccv = E("gpsimd", lambda e: e.collective_compute(
            "AllToAll", ALU.bypass, replica_groups=[list(range(8))],
            ins=[v_send.ap().opt()], outs=[vgD.ap().opt()]),
            waits=[("ccq", ccq)], inc="ccv")

        def kres(par):
            return big[:, 12288 * par:12288 * par + 4096]

        def vres(par):
            return big[:, 12288 * par + 4096:12288 * par + 8192]

        def qres(par):
            return big[:, 12288 * par + 8192:12288 * par + 12288]

        mqc = [mqct[:, 0:4], mqct[:, 4:8]]
        nmq = [nmqt[0:1, 0:512], nmqt[0:1, 512:1024]]
        mxps = [pall[:, 2560:2688], pall[:, 3072:3200]]
        pden = pall[0:1, 2048:2560]
        pbc = PB(4)
        sc_tok = {0: None, 1: None, 7: None}
        av_tok = {2: None, 3: None}
        attT_tok = [None] * 4
        rbc_tok = [None, None]
        aost_tok = [None] * 3
        mx_tok = [None, None]
        den_free = None
        ld_prev = {}
        sci = 0
        ui = 0
        for p in range(NP):
            b, hh = p // 4, p % 4
            par = p % 2
            w0 = [("cck", cck), ("ccq", ccq), ("ccv", ccv)]
            if ld_prev.get(par) is not None:
                w0.append(("dnm", ld_prev[par]))
            lds = 0
            for B in range(8):
                s = 4 * b + _rho(B)
                co = _co(B)
                E("sync", lambda e, s=s, co=co, hh=hh, B=B, par=par:
                  e.dma_start(out=kres(par)[:, 512 * B:512 * (B + 1)],
                              in_=kgD.ap()[512 * s + 128 * hh:512 * s + 128 * (hh + 1),
                                           co:co + 512]),
                  waits=(w0 if B == 0 else []), inc="p2ld", amt=16)
                E("sync", lambda e, s=s, co=co, hh=hh, B=B, par=par:
                  e.dma_start(out=vres(par)[:, 512 * B:512 * (B + 1)].rearrange(
                      "p (u c) -> p u c", c=128),
                      in_=vgD.ap()[1024 * s + co:1024 * s + co + 512,
                                   128 * hh:128 * (hh + 1)].rearrange("(u p) c -> p u c", p=128)),
                  inc="p2ld", amt=16)
                lds = E("sync", lambda e, s=s, co=co, hh=hh, B=B, par=par:
                        e.dma_start(out=qres(par)[:, 512 * B:512 * (B + 1)],
                                    in_=qgD.ap()[512 * s + 128 * hh:512 * s + 128 * (hh + 1),
                                                 co:co + 512]),
                        inc="p2ld", amt=16)

            def mx_minis(P, qt):
                for q4 in range(4):
                    for B in range(P + 1):
                        yield (P, q4, B)
                yield None

            def emit_mx_mini(item, qt):
                if item is None:
                    emit_mx_tail(qt)
                    return
                Pm, q4, B = item
                if True:
                    if True:
                        mi = (S.cnt.get("_mxi", 0)) % 2
                        S.cnt["_mxi"] = S.cnt.get("_mxi", 0) + 1
                        wmx = [("p2ld", lds)]
                        if mx_tok[mi] is not None:
                            wmx.append(("mxr", mx_tok[mi]))
                        mm = E("tensor", lambda e, mi=mi, P=Pm, q4=q4, B=B, par=par:
                               e.matmul(out=mxps[mi],
                                        lhsT=qres(par)[:, 512 * P + 128 * q4:
                                                       512 * P + 128 * (q4 + 1)],
                                        rhs=kres(par)[:, 512 * B:512 * B + 128],
                                        start=True, stop=True),
                               waits=wmx, inc="mxm")
                        rdt = E("vector", lambda e, mi=mi: e.reduce_max(
                            out=mqt[:, 0:1], in_=mxps[mi], axis=AXX),
                            waits=[("mxm", mm)], inc="mxr", fence=True)
                        if B == 0:
                            E("vector", lambda e, qt=qt, q4=q4: e.tensor_copy(
                                out=mqc[qt][:, q4:q4 + 1], in_=mqt[:, 0:1]),
                              inc="mqv", fence=True)
                        else:
                            E("vector", lambda e, qt=qt, q4=q4: e.tensor_max(
                                out=mqc[qt][:, q4:q4 + 1], in0=mqc[qt][:, q4:q4 + 1],
                                in1=mqt[:, 0:1]), inc="mqv", fence=True)
                        mx_tok[mi] = rdt

            def emit_mx_tail(qt):
                for q4 in range(4):
                    scb = [0, 1, 7][(S.cnt.get("_sci", 0)) % 3]
                    S.cnt["_sci"] = S.cnt.get("_sci", 0) + 1
                    wtp = [("mqv", T("mqv"))]
                    if sc_tok[scb] is not None:
                        wtp.append(sc_tok[scb])
                    tp = E("tensor", lambda e, scb=scb, qt=qt, q4=q4: e.transpose(
                        out=PBb(scb)[0:1, 0:128], in_=mqc[qt][:, q4:q4 + 1],
                        identity=s_id[:]),
                        waits=wtp, inc="mtp")
                    ng = E("vector", lambda e, scb=scb, qt=qt, q4=q4: e.tensor_scalar(
                        out=nmq[qt][0:1, 128 * q4:128 * (q4 + 1)], in0=PBb(scb)[0:1, 0:128],
                        scalar1=-1.0, scalar2=None, op0=ALU.mult),
                        waits=[("mtp", tp)], inc="nmg")
                    sc_tok[scb] = ("nmg", ng)

            def units(P, qt, nmg_tok, minis, mqt_next):
                nonlocal ui, den_free
                avb = 2 + qt
                pend = None
                for B in range(P + 1):
                    for u in range(4):
                        if minis is not None:
                            item = next(minis, "done")
                            if item != "done":
                                emit_mx_mini(item, mqt_next)
                        first = (B == 0 and u == 0)
                        lastu = (B == P and u == 3)
                        scb = [0, 1, 7][(S.cnt.get("_sci", 0)) % 3]
                        S.cnt["_sci"] = S.cnt.get("_sci", 0) + 1
                        ats = ui % 4
                        wsc = [("p2ld", lds), ("nmg", nmg_tok)]
                        if sc_tok[scb] is not None:
                            wsc.append(sc_tok[scb])
                        smm = E("tensor", lambda e, scb=scb, B=B, u=u, P=P, par=par:
                                e.matmul(out=PB(scb),
                                         lhsT=kres(par)[:, 512 * B + 128 * u:
                                                        512 * B + 128 * (u + 1)],
                                         rhs=qres(par)[:, 512 * P:512 * (P + 1)],
                                         start=True, stop=False),
                                waits=wsc, inc="smm")
                        smm = E("tensor", lambda e, scb=scb, qt=qt:
                                e.matmul(out=PB(scb), lhsT=s_or[:],
                                         rhs=nmq[qt][0:1, :], start=False, stop=True),
                                inc="smm")
                        wxp = [("smm", smm)]
                        if attT_tok[ats] is not None:
                            wxp.append(("dnm", attT_tok[ats]))
                        sxp = E("scalar", lambda e, scb=scb, a=ats:
                                e.activation(out=attT[a][:], in_=PB(scb), func=AF.Exp,
                                             bias=0.0, scale=1.0),
                                waits=wxp, inc="sxp")
                        sc_tok[scb] = ("sxp", sxp)
                        gate = ("sxp", sxp)
                        if B == P:
                            mku = E("vector", lambda e, a=ats, u=u:
                                    e.tensor_mul(out=attT[a][:], in0=attT[a][:],
                                                 in1=s_msk[:, 512 * u:512 * (u + 1)]),
                                    waits=[("sxp", sxp)], inc="msk")
                            gate = ("msk", mku)
                        def mk_avden(gate=gate, avb=avb, B=B, u=u, par=par, a=ats,
                                     first=first, lastu=lastu):
                            wav = [gate]
                            if first and av_tok[avb] is not None:
                                wav.append(("aev", av_tok[avb]))
                            E("tensor", lambda e:
                              e.matmul(out=PB(avb),
                                       lhsT=vres(par)[:, 512 * B + 128 * u:
                                                      512 * B + 128 * (u + 1)],
                                       rhs=attT[a][:], start=first, stop=lastu),
                              waits=wav, inc="avm")
                            wdn = []
                            if first and den_free is not None:
                                wdn.append(den_free)
                            dnm = E("tensor", lambda e:
                                    e.matmul(out=pden, lhsT=s_oc[:], rhs=attT[a][:],
                                             start=first, stop=lastu),
                                    waits=wdn, inc="dnm")
                            attT_tok[a] = dnm
                        if pend is not None:
                            pend()
                        pend = mk_avden
                        ui += 1
                if pend is not None:
                    pend()
                if minis is not None:
                    for item in minis:
                        emit_mx_mini(item, mqt_next)
                dl1 = E("scalar", lambda e: e.activation(
                    out=r_t[:], in_=pden, func=AF.Ln, bias=0.0, scale=1.0),
                    waits=[("dnm", T("dnm"))], inc="dnl")
                rd = E("scalar", lambda e, qt=qt: e.activation(out=r_rd[qt][:], in_=r_t[:],
                                                               func=AF.Exp, bias=0.0,
                                                               scale=-1.0),
                       inc="rdn", fence=True)
                bcm = E("tensor", lambda e: e.matmul(out=pbc, lhsT=s_orf[:],
                                                     rhs=r_rd[qt][:],
                                                     start=True, stop=True),
                        waits=[("rdn", rd), ("dnl", dl1)], inc="bcm")
                wcc = [("bcm", bcm)]
                if rbc_tok[qt] is not None:
                    wcc.append(("aev", rbc_tok[qt]))
                bcc = E("vector", lambda e, qt=qt: e.tensor_copy(out=rbc[qt][:], in_=pbc),
                        waits=wcc, inc="bcc")
                den_free = ("bcc", bcc)
                aslot = (p * 8 + P) % 3
                wev = [("bcc", bcc)]
                if aost_tok[aslot] is not None:
                    wev.append(("aos", aost_tok[aslot]))
                aev = E("vector", lambda e, avb=avb, qt=qt, s=aslot:
                        e.tensor_tensor(out=aost[s][:], in0=PB(avb), in1=rbc[qt][:],
                                        op=ALU.mult),
                        waits=wev, inc="aev")
                av_tok[avb] = aev
                rbc_tok[qt] = aev
                d = 4 * b + _rho(P)
                co = _co(P)
                aost_tok[aslot] = E("scalar", lambda e, s=aslot, d=d, co=co, hh=hh:
                                    e.dma_start(out=ao_send.ap()[
                                        512 * d + 128 * hh:512 * d + 128 * (hh + 1),
                                        co:co + 512], in_=aost[s][:]),
                                    waits=[("aev", aev)], inc="aos", amt=16)

            for item in mx_minis(0, 0):
                emit_mx_mini(item, 0)
            nmg_tok = T("nmg")
            for P in range(8):
                minis = mx_minis(P + 1, (P + 1) % 2) if P < 7 else None
                units(P, P % 2, nmg_tok, minis, (P + 1) % 2)
                nmg_tok = T("nmg")
            ld_prev[par] = T("dnm")

        cca = E("gpsimd", lambda e: e.collective_compute(
            "AllToAll", ALU.bypass, replica_groups=[list(range(8))],
            ins=[ao_send.ap().opt()], outs=[aogD.ap().opt()]),
            waits=[("aos", T("aos")), ("p2ld", T("p2ld")), ("kst", T("kst")),
                   ("qst", T("qst")), ("vst", T("vst"))], inc="cca")

        agl = E("sync", lambda e: e.dma_start(
            out=big[:].rearrange("p (j c) -> p j c", c=NT),
            in_=aogD.ap().rearrange("(j p) c -> p j c", p=128)),
            waits=[("cca", cca), ("dnm", T("dnm")), ("smm", T("smm"))], inc="agl", amt=16)
        strip_free = {0: None, 1: None}
        wo_bank_rd = {}
        xres_free = {}
        x1_free = {}
        for g8 in range(8):
            wtk = {}
            for jh in range(2):
                src = woT[2048 * jh:2048 * (jh + 1), 512 * g8:512 * (g8 + 1)].rearrange(
                    "(j p) c -> p j c", p=128)
                dstw = wb[jh][:].rearrange("p (j c) -> p j c", c=512)
                w = [("smm", T("smm")), ("cca", cca)]
                if strip_free[jh] is not None:
                    w.append(("ojh", strip_free[jh]))
                wtk[jh] = E("gpsimd", lambda e, s=src, d=dstw: e.dma_start(out=d, in_=s),
                            waits=w, inc="wld", amt=16)
            for jh in range(2):
                for t8 in range(8):
                    for j16 in range(16):
                        j = 16 * jh + j16
                        st = (jh == 0 and j16 == 0)
                        w2 = []
                        if t8 == 0 and j16 == 0:
                            w2.append(("wld", wtk[jh]))
                            w2.append(("agl", agl))
                            if st and g8 == 0:
                                w2.append(("aev", T("aev")))
                                w2.append(("bcc", T("bcc")))
                                w2.append(("dnl", T("dnl")))
                        if st and j16 == 0:
                            bt = wo_bank_rd.get(t8)
                            if bt:
                                w2.append(("x1e", bt))
                        E("tensor", lambda e, b=jh, t8=t8, j16=j16, j=j, st=st:
                          e.matmul(out=PB(t8),
                                   lhsT=big[:, NT * j + 128 * t8:NT * j + 128 * (t8 + 1)],
                                   rhs=wb[b][:, 512 * j16:512 * (j16 + 1)],
                                   start=st, stop=False),
                          waits=w2,
                          inc="ojh" if (j16 == 15) else None)
                    if jh == 1:
                        bm = E("tensor", lambda e, t8=t8, g8=g8:
                               e.matmul(out=PB(t8), lhsT=s_or[:],
                                        rhs=bob[0:1, 512 * g8:512 * (g8 + 1)],
                                        start=False, stop=True),
                               waits=[("brl", brl_bo)], inc="omm")
                        slot = (g8 * 8 + t8) % 2
                        wx = []
                        if xres_free.get(slot) is not None:
                            wx.append(("x1e", xres_free[slot]))
                        xl = E("sync", lambda e, s=slot, t8=t8, g8=g8:
                               e.dma_start(out=xres[s][:],
                                           in_=xs[128 * t8:128 * (t8 + 1),
                                                  512 * g8:512 * (g8 + 1)]),
                               waits=wx, inc="xrl", amt=16)
                        w1 = [("omm", bm), ("xrl", xl)]
                        if x1_free.get(slot) is not None:
                            w1.append(("x1s", x1_free[slot]))
                        x1e = E("vector", lambda e, t8=t8, s=slot:
                                e.tensor_add(out=x1st[s][:], in0=PB(t8), in1=xres[s][:]),
                                waits=w1, inc="x1e")
                        xres_free[slot] = x1e
                        wo_bank_rd[t8] = x1e
                        x1_free[slot] = E("scalar", lambda e, s=slot, t8=t8, g8=g8:
                                          e.dma_start(out=x1d.ap()[128 * t8:128 * (t8 + 1),
                                                                   512 * g8:512 * (g8 + 1)],
                                                      in_=x1st[s][:]),
                                          waits=[("x1e", x1e)], inc="x1s", amt=16)
                strip_free[jh] = T("ojh")

        btl = E("gpsimd", lambda e: e.dma_start(out=wb[0][0:64, 0:4096], in_=bTm),
                waits=[("ojh", T("ojh")), ("omm", T("omm"))], inc="btl", amt=16)
        bTb = wb[0][0:64, 0:4096]
        xn2f = wb[1][:, 0:4096]
        tr_bank = {5: None, 6: None}
        pml_rd = None
        pmlt_rd = None
        pmgT_rd = None
        pd_rd = {3: None, 4: None}
        ost_free = [None] * 2
        xa_free = {0: None, 1: None}
        lnx2_tok = {}
        for t8 in range(8):
            par4 = t8 % 2
            mltv = mlt[:, 68 * par4:68 * (par4 + 1)]
            mgv = mg[:, 64 * par4:64 * (par4 + 1)]
            mgTv = mgT[:, 128 * par4:128 * (par4 + 1)]
            glv = gl[:, 28 * par4:28 * (par4 + 1)]
            L = lambda ee, m=mltv: m[:, 64 + ee:65 + ee]
            wxl = [("x1s", T("x1s"))]
            if lnx2_tok.get(t8 % 2) is not None:
                wxl.append(("lnx", lnx2_tok[t8 % 2]))
            if xa_free[t8 % 2] is not None:
                wxl.append(("ooe", xa_free[t8 % 2]))
            xl = E("sync", lambda e, t=t8: e.dma_start(out=xa[t % 2][:],
                                                       in_=x1d.ap()[128 * t:128 * (t + 1), :]),
                   waits=wxl, inc="xld", amt=16)
            a_x = ln_tile(xa[t8 % 2], xn[t8 % 2], t8 % 2, [("xld", xl)])
            lnx2_tok[t8 % 2] = a_x
            for g in range(4):
                bank = 5 + ((t8 * 4 + g) % 2)
                w = [("lnx", a_x), ("ojh", T("ojh"))]
                if t8 == 0 and g < 2:
                    w.append(("x1e", T("x1e")))
                if tr_bank[bank] is not None:
                    w.append(("tev", tr_bank[bank]))
                last = None
                for j2 in range(8):
                    j = 8 * g + j2
                    last = E("tensor", lambda e, b=bank, j2=j2, j=j, t=t8: e.transpose(
                        out=PBb(b)[:, 128 * j2:128 * (j2 + 1)],
                        in_=xn[t % 2][:, 128 * j:128 * (j + 1)], identity=s_id[:]),
                        waits=(w if j2 == 0 else []), inc=("ttr" if j2 == 7 else None))
                tr_bank[bank] = E("vector", lambda e, b=bank, g=g: e.tensor_copy(
                    out=xn2f[:, 1024 * g:1024 * (g + 1)], in_=PBb(b)[:]),
                    waits=[("ttr", last), ("ojh", T("ojh"))], inc="tev")
            tevt = T("tev")
            for j in range(32):
                w2 = []
                if j == 0:
                    w2 = [("tev", tevt), ("cst", cst_all)]
                    if pml_rd is not None:
                        w2.append(("mle", pml_rd))
                    if t8 == 0:
                        w2.append(("x1e", T("x1e")))
                E("tensor", lambda e, j=j: e.matmul(out=pall[0:68, 0:128],
                                                    lhsT=rhswb[:, 68 * j:68 * (j + 1)],
                                                    rhs=xn2f[:, 128 * j:128 * (j + 1)],
                                                    start=(j == 0), stop=(j == 31)),
                  waits=w2, inc=("mlm" if j == 31 else None))
            pml_rd = E("vector", lambda e: e.tensor_copy(out=sml[0:68, :], in_=pall[0:68, 0:128]),
                       waits=[("mlm", T("mlm"))], inc="mle")
            w3 = [("mle", pml_rd)]
            if pmlt_rd is not None:
                w3.append(("mlte", pmlt_rd))
            if t8 == 0:
                w3.append(("x1e", T("x1e")))
            trm = E("tensor", lambda e: e.transpose(out=PBb(1)[:, 0:68], in_=sml[0:68, :],
                                                    identity=s_id[0:68, 0:68]),
                    waits=w3, inc="mltm")
            pmlt_rd = E("vector", lambda e: e.tensor_copy(out=mltv, in_=PBb(1)[:, 0:68]),
                        waits=[("mltm", trm)], inc="mlte")
            fw = [("mlte", pmlt_rd)]
            for e_i in range(4):
                terms = [(a_i, ALU.is_ge if a_i < e_i else ALU.is_gt)
                         for a_i in range(4) if a_i != e_i]
                for k_i, (a_i, op) in enumerate(terms):
                    if k_i == 0:
                        E("vector", lambda e, a=a_i, ei=e_i, op=op: e.tensor_tensor(
                            out=glv[:, ei:ei + 1], in0=L(a), in1=L(ei), op=op), waits=fw,
                          fence=True)
                        fw = []
                    else:
                        E("vector", lambda e, a=a_i, ei=e_i, op=op: e.tensor_tensor(
                            out=glv[:, 4:5], in0=L(a), in1=L(ei), op=op), fence=True)
                        E("vector", lambda e, ei=e_i: e.tensor_add(
                            out=glv[:, ei:ei + 1], in0=glv[:, ei:ei + 1], in1=glv[:, 4:5]),
                          fence=True)
            E("vector", lambda e: e.tensor_max(out=glv[:, 5:6], in0=L(0), in1=L(1)), fence=True)
            E("vector", lambda e: e.tensor_max(out=glv[:, 5:6], in0=glv[:, 5:6], in1=L(2)),
              fence=True)
            E("vector", lambda e: e.tensor_max(out=glv[:, 5:6], in0=glv[:, 5:6], in1=L(3)),
              fence=True)
            vnm = E("vector", lambda e: e.tensor_scalar(out=glv[:, 6:7], in0=glv[:, 5:6],
                                                        scalar1=-1.0, scalar2=None,
                                                        op0=ALU.mult), inc="gnm", fence=True)
            for e_i in range(4):
                E("scalar", lambda e, ei=e_i: e.activation(out=glv[:, 7 + ei:8 + ei], in_=L(ei),
                                                           func=AF.Exp, bias=glv[:, 6:7],
                                                           scale=1.0),
                  waits=([("gnm", vnm)] if e_i == 0 else []),
                  inc=("gee" if e_i == 3 else None))
            gee = T("gee")
            for e_i in range(4):
                E("vector", lambda e, ei=e_i: e.tensor_single_scalar(
                    out=glv[:, 11 + ei:12 + ei], in_=glv[:, ei:ei + 1], scalar=1.5, op=ALU.is_le),
                  waits=([("gee", gee)] if e_i == 0 else []), fence=True)
            for e_i in range(4):
                E("vector", lambda e, ei=e_i: e.tensor_mul(out=glv[:, 15 + ei:16 + ei],
                                                           in0=glv[:, 7 + ei:8 + ei],
                                                           in1=glv[:, 11 + ei:12 + ei]),
                  fence=True)
            E("vector", lambda e: e.tensor_add(out=glv[:, 19:20], in0=glv[:, 15:16],
                                               in1=glv[:, 16:17]), fence=True)
            E("vector", lambda e: e.tensor_add(out=glv[:, 19:20], in0=glv[:, 19:20],
                                               in1=glv[:, 17:18]), fence=True)
            vdn = E("vector", lambda e: e.tensor_add(out=glv[:, 19:20], in0=glv[:, 19:20],
                                                     in1=glv[:, 18:19]), inc="gdn", fence=True)
            E("scalar", lambda e: e.activation(out=glv[:, 20:21], in_=glv[:, 19:20],
                                               func=AF.Ln, bias=0.0, scale=1.0),
              waits=[("gdn", vdn)])
            grg = E("scalar", lambda e: e.activation(out=glv[:, 20:21], in_=glv[:, 20:21],
                                                     func=AF.Exp, bias=0.0, scale=-1.0),
                    inc="grg", fence=True)
            for e_i in range(4):
                E("vector", lambda e, ei=e_i: e.tensor_scalar(
                    out=glv[:, 21 + ei:22 + ei], in0=glv[:, 15 + ei:16 + ei],
                    scalar1=glv[:, 20:21], scalar2=4.0, op0=ALU.mult, op1=ALU.mult),
                  waits=([("grg", grg)] if e_i == 0 else []), fence=True)
            gms = None
            for e_i in range(4):
                gms = E("vector", lambda e, ei=e_i: e.tensor_scalar(
                    out=mgv[:, 16 * ei:16 * (ei + 1)], in0=mltv[:, 16 * ei:16 * (ei + 1)],
                    scalar1=glv[:, 21 + ei:22 + ei], scalar2=None, op0=ALU.mult),
                    inc=("gms" if e_i == 3 else None), fence=True)
            gms = T("gms")
            w4 = [("gms", gms)]
            if pmgT_rd is not None:
                w4.append(("mge", pmgT_rd))
            if t8 == 0:
                w4.append(("x1e", T("x1e")))
            E("tensor", lambda e: e.transpose(out=PBb(2)[0:64, 0:128], in_=mgv,
                                              identity=s_id[:]),
              waits=w4, inc="mgm")
            pmgT_rd = E("vector", lambda e: e.tensor_copy(out=mgTv, in_=PBb(2)[0:64, 0:128]),
                        waits=[("mgm", T("mgm"))], inc="mge")
            mge = pmgT_rd
            last_ooe = None
            for g8 in range(8):
                bank = 3 + (g8 % 2)
                wd = [("mge", mge), ("btl", btl)]
                if pd_rd[bank] is not None:
                    wd.append(("ooe", pd_rd[bank]))
                if t8 == 0 and g8 < 2:
                    wd.append(("x1e", T("x1e")))
                dm = E("tensor", lambda e, bank=bank, g8=g8: e.matmul(
                    out=PB(bank), lhsT=mgTv[0:64, :], rhs=bTb[0:64, 512 * g8:512 * (g8 + 1)],
                    start=True, stop=True), waits=wd, inc="dm")
                oslot = (t8 * 8 + g8) % 2
                wo_ = [("dm", dm)]
                if ost_free[oslot] is not None:
                    wo_.append(("outs", ost_free[oslot]))
                ooe = E("vector", lambda e, bank=bank, s=oslot, t=t8, g8=g8: e.tensor_add(
                    out=ost[s][:], in0=PB(bank), in1=xa[t % 2][:, 512 * g8:512 * (g8 + 1)]),
                    waits=wo_, inc="ooe")
                pd_rd[bank] = ooe
                last_ooe = ooe
                ost_free[oslot] = E("scalar", lambda e, s=oslot, t=t8, g8=g8: e.dma_start(
                    out=out[128 * t:128 * (t + 1), 512 * g8:512 * (g8 + 1)], in_=ost[s][:]),
                    waits=[("ooe", ooe)], inc="outs", amt=16)
            xa_free[t8 % 2] = last_ooe

        if taps:
            E("gpsimd", lambda e: e.dma_start(out=tap["ksend"].ap(), in_=k_send.ap()),
              waits=[("kst", T("kst"))], inc="tapd", amt=16)
            E("gpsimd", lambda e: e.dma_start(out=tap["qsend"].ap(), in_=q_send.ap()),
              waits=[("qst", T("qst")), ("mxst", T("mxst"))], inc="tapd", amt=16)
            E("gpsimd", lambda e: e.dma_start(out=tap["vsend"].ap(), in_=v_send.ap()),
              waits=[("vst", T("vst"))], inc="tapd", amt=16)
            E("gpsimd", lambda e: e.dma_start(out=tap["aosend"].ap(), in_=ao_send.ap()),
              waits=[("aos", T("aos"))], inc="tapd", amt=16)
            E("gpsimd", lambda e: e.dma_start(out=tap["aog"].ap(), in_=aogD.ap()),
              waits=[("cca", cca)], inc="tapd", amt=16)
            E("gpsimd", lambda e: e.dma_start(out=tap["x1"].ap(), in_=x1d.ap()),
              waits=[("x1s", T("x1s"))], inc="tapd", amt=16)
            E("gpsimd", lambda e: e.dma_start(out=tap["kg"].ap(), in_=kgD.ap()),
              waits=[("cck", cck)], inc="tapd", amt=16)
            E("gpsimd", lambda e: e.dma_start(out=tap["qg"].ap(), in_=qgD.ap()),
              waits=[("ccq", ccq)], inc="tapd", amt=16)
            E("gpsimd", lambda e: e.dma_start(out=tap["vg"].ap(), in_=vgD.ap()),
              waits=[("ccv", ccv)], inc="tapd", amt=16)
            S.wait_only("gpsimd", [("tapd", T("tapd"))])

        S.wait_only("scalar", [("outs", T("outs"))])

        for chname in S.cnt:
            sems[chname] = ctx.enter_context(nc.semaphore(chname))
        with allow_ici_all_to_all():
            with nc.Block() as block:
                S.replay(block, sems)
    return nc


def _prep_inputs(L0_attn_x, L0_attn_Wq, L0_attn_bq, L0_attn_Wk, L0_attn_bk,
                 L0_attn_Wv, L0_attn_bv, L0_attn_Wo, L0_attn_bo,
                 L0_ffn_A, L0_ffn_B, L0_ffn_Wr):
    x = np.asarray(L0_attn_x, dtype=np.float32)
    C = lambda a: np.ascontiguousarray(np.asarray(a, dtype=np.float32))
    A = np.asarray(L0_ffn_A, dtype=np.float32)
    B = np.asarray(L0_ffn_B, dtype=np.float32)
    Wr = np.asarray(L0_ffn_Wr, dtype=np.float32)
    aT = A.transpose(2, 0, 1).reshape(D, 64)
    wrT = np.asarray(Wr).T
    rhsw = np.concatenate([aT.reshape(32, 128, 64),
                           np.ascontiguousarray(wrT).reshape(32, 128, 4)], axis=2)
    shared = dict(
        wqT=C(np.asarray(L0_attn_Wq).T), wkT=C(np.asarray(L0_attn_Wk).T),
        wvT=C(np.asarray(L0_attn_Wv).T), woT=C(np.asarray(L0_attn_Wo).T),
        bqc=C(np.asarray(L0_attn_bq).reshape(32, 128).T),
        bkc=C(np.asarray(L0_attn_bk).reshape(32, 128).T),
        bvr=C(np.asarray(L0_attn_bv).reshape(1, D)),
        bor=C(np.asarray(L0_attn_bo).reshape(1, D)),
        rhsw=C(rhsw.transpose(1, 0, 2).reshape(128, 32 * 68)),
        bTm=C(B.transpose(0, 2, 1).reshape(64, D)))
    in_maps = []
    for c in range(8):
        g, qc = c // 4, c % 4
        xsl = np.concatenate([x[g, 512 * qc:512 * (qc + 1)],
                              x[g, 512 * (7 - qc):512 * (8 - qc)]], axis=0)
        m = dict(shared)
        m["xs"] = np.ascontiguousarray(xsl)
        in_maps.append(m)
    return in_maps


def kernel(**inputs):
    in_maps = _prep_inputs(**inputs)
    with allow_ici_all_to_all():
        nc = build_nc(taps=_TAPS)
        res = run_bass_kernel_spmd(nc, in_maps, core_ids=list(range(8)))
    outf = np.zeros((2, D, D), dtype=np.float32)
    for c in range(8):
        g, qc = c // 4, c % 4
        o = res.results[c]["out"]
        outf[g, 512 * qc:512 * (qc + 1)] = o[0:512]
        outf[g, 512 * (7 - qc):512 * (8 - qc)] = o[512:1024]
    if _TAPS:
        kernel._tap_results = res.results
    return outf


# version 07
# speedup vs baseline: 1.3699x; 1.0003x over previous
import math
from contextlib import ExitStack

import numpy as np
import ml_dtypes

import concourse.bass as bass
import concourse.mybir as mybir
from concourse.bass_utils import run_bass_kernel_spmd
from concourse.aot_env import allow_ici_all_to_all

F32 = mybir.dt.float32
BF16 = mybir.dt.bfloat16
AF = mybir.ActivationFunctionType
ALU = mybir.AluOpType
AXX = mybir.AxisListType.X

D = 4096
NT = 1024
NP = 8
SCLQ = 1.0 / math.sqrt(128.0)
EPS = 1e-5
_TAPS = False

ENGS = ("sync", "gpsimd", "scalar", "vector", "tensor")


def _rho(B):
    return B if B < 4 else 7 - B


def _co(B):
    return 0 if B < 4 else 512


class Sched:
    def __init__(self):
        self.ops = {e: [] for e in ENGS}
        self.cnt = {}
        self._pend = {e: [] for e in ENGS}

    def tok(self, ch):
        return self.cnt.get(ch, 0)

    def emit(self, eng, fn, waits=(), inc=None, amt=1, fence=False):
        if fence:
            self.fence(eng)
        waits = list(waits) + self._pend[eng]
        self._pend[eng] = []
        if inc is not None:
            self.cnt[inc] = self.cnt.get(inc, 0) + amt
        self.ops[eng].append((tuple((c, v) for c, v in waits if v), fn, inc, amt))
        return self.cnt.get(inc, 0)

    def fence(self, eng):
        if not self.ops[eng]:
            return
        w, fn, inc, amt = self.ops[eng][-1]
        if fn is None:
            return
        if inc is None:
            ch = "sf_" + eng
            self.cnt[ch] = self.cnt.get(ch, 0) + 1
            self.ops[eng][-1] = (w, fn, ch, 1)
            self._pend[eng].append((ch, self.cnt[ch]))
        else:
            self._pend[eng].append((inc, self.cnt[inc]))

    def wait_only(self, eng, waits):
        self.ops[eng].append((tuple(waits), None, None, 0))

    def replay(self, block, sems):
        def mk(ename):
            def body(eng):
                seen = {}
                for waits, fn, inc, amt in self.ops[ename]:
                    for ch, val in waits:
                        if val > seen.get(ch, 0):
                            eng.wait_ge(sems[ch], val)
                            seen[ch] = val
                    if fn is not None:
                        ins = fn(eng)
                        if inc is not None:
                            ins.then_inc(sems[inc], amt)
            return body

        block.sync(mk("sync"))
        block.gpsimd(mk("gpsimd"))
        block.scalar(mk("scalar"))
        block.vector(mk("vector"))
        block.tensor(mk("tensor"))


def build_nc(taps=False):
    nc = bass.Bass("TRN2", target_bir_lowering=False)

    xs = nc.dram_tensor("xs", [NT, D], F32, kind="ExternalInput").ap()
    wqT = nc.dram_tensor("wqT", [D, D], F32, kind="ExternalInput").ap()
    wkT = nc.dram_tensor("wkT", [D, D], F32, kind="ExternalInput").ap()
    wvT = nc.dram_tensor("wvT", [D, D], F32, kind="ExternalInput").ap()
    woT = nc.dram_tensor("woT", [D, D], F32, kind="ExternalInput").ap()
    bqc = nc.dram_tensor("bqc", [128, 32], F32, kind="ExternalInput").ap()
    bkc = nc.dram_tensor("bkc", [128, 32], F32, kind="ExternalInput").ap()
    bvr = nc.dram_tensor("bvr", [1, D], F32, kind="ExternalInput").ap()
    bor = nc.dram_tensor("bor", [1, D], F32, kind="ExternalInput").ap()
    rhsw = nc.dram_tensor("rhsw", [128, 32 * 68], F32, kind="ExternalInput").ap()
    bTm = nc.dram_tensor("bTm", [64, D], F32, kind="ExternalInput").ap()
    out = nc.dram_tensor("out", [NT, D], F32, kind="ExternalOutput").ap()

    ident = nc.inline_tensor(np.eye(128, dtype=ml_dtypes.bfloat16), "ident")
    ones_col = nc.inline_tensor(np.ones((128, 1), dtype=ml_dtypes.bfloat16), "ones_col")
    ones_row = nc.inline_tensor(np.ones((1, 128), dtype=ml_dtypes.bfloat16), "ones_row")
    ones_row_f = nc.inline_tensor(np.ones((1, 128), dtype=np.float32), "ones_row_f")
    mknp = np.zeros((128, 4 * 512), dtype=ml_dtypes.bfloat16)
    pp = np.arange(128)[:, None]
    jj_ = np.arange(512)[None, :]
    for u in range(4):
        mknp[:, 512 * u:512 * (u + 1)] = ((128 * u + pp) <= jj_).astype(ml_dtypes.bfloat16)
    maskt = nc.inline_tensor(mknp, "maskt")

    k_send = nc.dram_tensor("k_send", [8 * 512, NT], BF16, kind="Internal")
    q_send = nc.dram_tensor("q_send", [8 * 512, NT], BF16, kind="Internal")
    v_send = nc.dram_tensor("v_send", [8 * 1024, 512], BF16, kind="Internal")
    kgD = nc.dram_tensor("kgD", [8 * 512, NT], BF16, kind="Internal")
    qgD = nc.dram_tensor("qgD", [8 * 512, NT], BF16, kind="Internal")
    vgD = nc.dram_tensor("vgD", [8 * 1024, 512], BF16, kind="Internal")
    ao_send = nc.dram_tensor("ao_send", [8 * 512, NT], BF16, kind="Internal")
    aogD = nc.dram_tensor("aogD", [8 * 512, NT], BF16, kind="Internal")
    x1d = nc.dram_tensor("x1d", [NT, D], F32, kind="Internal")
    tap = {}
    if taps:
        for n, sh, dt in [("ksend", (8 * 512, NT), BF16), ("qsend", (8 * 512, NT), BF16),
                          ("vsend", (8 * 1024, 512), BF16), ("aosend", (8 * 512, NT), BF16),
                          ("aog", (8 * 512, NT), BF16), ("x1", (NT, D), F32),
                          ("kg", (8 * 512, NT), BF16), ("qg", (8 * 512, NT), BF16),
                          ("vg", (8 * 1024, 512), BF16)]:
            tap[n] = nc.dram_tensor("tap_" + n, list(sh), dt, kind="ExternalOutput")

    S = Sched()
    E = S.emit
    T = S.tok

    with ExitStack() as ctx:
        SB = lambda n, sh, dt: ctx.enter_context(nc.sbuf_tensor(n, sh, dt))
        big = SB("big", [128, 32 * NT], BF16)
        wb = [SB(f"wb{i}", [128, 8192], BF16) for i in range(2)]
        xa = [SB(f"xa{i}", [128, D], F32) for i in range(2)]
        xn = [SB(f"xn{i}", [128, D], BF16) for i in range(2)]
        dump = SB("dump", [128, D], BF16)
        kst = [SB(f"kst{i}", [128, 512], BF16) for i in range(4)]
        k2t = [SB(f"k2t{i}", [128, 512], BF16) for i in range(2)]
        attT = [SB(f"attT{i}", [128, 512], BF16) for i in range(4)]
        rbc = [SB(f"rbc{i}", [128, 512], F32) for i in range(2)]
        aost = [SB(f"aost{i}", [128, 512], BF16) for i in range(3)]
        xres = [SB(f"xres{i}", [128, 512], F32) for i in range(2)]
        x1st = [SB(f"x1st{i}", [128, 512], F32) for i in range(2)]
        ost = [SB(f"ost{i}", [128, 512], F32) for i in range(2)]
        sml = SB("sml", [128, 128], BF16)
        mlt = SB("mlt", [128, 136], BF16)
        mg = SB("mg", [128, 128], BF16)
        mgT = SB("mgT", [64, 256], BF16)
        gl = SB("gl", [128, 56], F32)
        s_id = SB("s_id", [128, 128], BF16)
        s_msk = SB("s_msk", [128, 4 * 512], BF16)
        s_oc = SB("s_oc", [128, 1], BF16)
        s_or = SB("s_or", [1, 128], BF16)
        s_orf = SB("s_orf", [1, 128], F32)
        s_bq = SB("s_bq", [128, 32], F32)
        s_bqs = SB("s_bqs", [128, 32], F32)
        s_bk = SB("s_bk", [128, 32], F32)
        rhswb = SB("rhswb", [128, 32 * 68], BF16)
        urow = SB("urow", [128, 2048], F32)
        mqct = SB("mqct", [128, 8], BF16)
        mqt = SB("mqt", [128, 1], BF16)
        nmqt = SB("nmqt", [1, 1024], BF16)
        stat = SB("stat", [128, 16], F32)
        epsc = SB("epsc", [128, 1], F32)
        pall = ctx.enter_context(nc.psum_tensor("pall", [128, 4096], F32))
        sems = {}

        def PB(b):
            return pall[:, 512 * b:512 * (b + 1)]

        def PBb(b):
            return pall[:, 512 * b:512 * (b + 1)].bitcast(BF16)

        bvb = dump[0:1, :]
        bob = dump[0:1, :]
        r_t = urow[0:1, 0:512]
        r_rd = [urow[0:1, 512:1024], urow[0:1, 1024:1536]]
        r_ksq = urow[0:1, 1536:1568]
        r_qsq = urow[0:1, 1568:1600]
        r_m2 = urow[0:1, 1600:1608]
        r_mro = urow[0:1, 1608:1616]
        r_nM = urow[0:1, 1616:1624]
        r_mx1 = urow[0:1, 1624:1632]

        for dst, src in [(s_id[:], ident.ap()), (s_msk[:], maskt.ap()),
                         (s_oc[:], ones_col.ap()), (s_or[:], ones_row.ap()),
                         (s_orf[:], ones_row_f.ap()), (s_bq[:], bqc), (s_bk[:], bkc)]:
            E("sync", lambda e, d=dst, s=src: e.dma_start(out=d, in_=s), inc="cst", amt=16)
        for dst, src in [(rhswb[:], rhsw)]:
            E("gpsimd", lambda e, d=dst, s=src: e.dma_start(out=d, in_=s), inc="cstg", amt=16)
        cst_all = T("cst")
        cstg_all = T("cstg")
        E("vector", lambda e: e.tensor_scalar(out=s_bqs[:], in0=s_bq[:], scalar1=SCLQ,
                                              scalar2=None, op0=ALU.mult),
          waits=[("cst", cst_all)], inc="prep")
        E("vector", lambda e: e.memset(urow[0:1, 1536:1600], -3.0e38), inc="prep")
        prep_all = E("vector", lambda e: e.memset(epsc[:], EPS), inc="prep")

        def ln_tile(xa_t, xn_t, par, wfirst):
            o = 8 * par
            c_s1, c_s2, c_mean, c_var, c_ms, c_rstd, c_nmr, c_m2 = [
                stat[:, o + i:o + i + 1] for i in range(8)]
            E("vector", lambda e: e.tensor_scalar(out=dump[:], in0=xa_t[:], scalar1=1.0,
                                                  scalar2=0.0, op0=ALU.mult, op1=ALU.add,
                                                  accum_out=c_s1),
              waits=wfirst)
            E("vector", lambda e: e.scalar_tensor_tensor(out=dump[:], in0=xa_t[:], scalar=1.0,
                                                         in1=xa_t[:], op0=ALU.mult, op1=ALU.mult,
                                                         accum_out=c_s2))
            E("vector", lambda e: e.tensor_scalar(out=c_mean, in0=c_s1, scalar1=1.0 / D,
                                                  scalar2=None, op0=ALU.mult), fence=True)
            E("vector", lambda e: e.scalar_tensor_tensor(out=c_var, in0=c_s1, scalar=c_mean,
                                                         in1=c_s2, op0=ALU.mult,
                                                         op1=ALU.subtract), fence=True)
            v_var = E("vector", lambda e: e.tensor_scalar(out=c_var, in0=c_var,
                                                          scalar1=-1.0 / D, scalar2=None,
                                                          op0=ALU.mult), inc="lnv", fence=True)
            E("scalar", lambda e: e.activation(out=c_rstd, in_=c_var, func=AF.Ln,
                                               bias=epsc[:], scale=1.0),
              waits=[("lnv", v_var), ("prep", prep_all)])
            a_r = E("scalar", lambda e: e.activation(out=c_rstd, in_=c_rstd, func=AF.Exp,
                                                     bias=0.0, scale=-0.5), inc="lnr",
                    fence=True)
            v_n = E("vector", lambda e: e.tensor_scalar(out=c_nmr, in0=c_mean, scalar1=c_rstd,
                                                        scalar2=-1.0, op0=ALU.mult, op1=ALU.mult),
                    waits=[("lnr", a_r)], inc="lnn")
            a_x = E("scalar", lambda e: e.activation(out=xn_t[:], in_=xa_t[:], func=AF.Identity,
                                                     bias=c_nmr, scale=c_rstd),
                    waits=[("lnn", v_n)], inc="lnx")
            return a_x

        tev_bank = {0: 0, 1: 0}
        lnx_tok = {}
        for t8 in range(8):
            wx = [("lnx", lnx_tok[t8 % 2])] if t8 >= 2 else []
            xl = E("sync", lambda e, t=t8: e.dma_start(out=xa[t % 2][:],
                                                       in_=xs[128 * t:128 * (t + 1), :]),
                   waits=wx, inc="xld", amt=16)
            a_x = ln_tile(xa[t8 % 2], xn[t8 % 2], t8 % 2,
                          [("xld", xl), ("cst", cst_all)])
            lnx_tok[t8 % 2] = a_x
            for g in range(4):
                bank = (t8 * 4 + g) % 2
                w = [("lnx", a_x)]
                if tev_bank[bank]:
                    w.append(("tev", tev_bank[bank]))
                last = None
                for j2 in range(8):
                    j = 8 * g + j2
                    last = E("tensor", lambda e, b=bank, j2=j2, j=j, t=t8: e.transpose(
                        out=PBb(b)[:, 128 * j2:128 * (j2 + 1)],
                        in_=xn[t % 2][:, 128 * j:128 * (j + 1)], identity=s_id[:]),
                        waits=(w if j2 == 0 else []), inc=("ttr" if j2 == 7 else None))
                dst = big[:].rearrange("p (j c) -> p j c", c=NT)[:, 8 * g:8 * (g + 1),
                                                                128 * t8:128 * (t8 + 1)]
                src = PBb(bank)[:].rearrange("p (j c) -> p j c", c=128)
                tev_bank[bank] = E("vector", lambda e, d=dst, s=src: e.tensor_copy(out=d, in_=s),
                                   waits=[("ttr", last)], inc="tev")
        tev_all = T("tev")
        brl_all = E("gpsimd", lambda e: e.dma_start(out=bvb, in_=bvr),
                    waits=[("lnv", T("lnv"))], inc="brl", amt=16)

        slotg = {}

        def proj_phase(wT, pref, bias_t, scale, send, entry_guard):
            strip_free = {0: None, 1: None}
            for hg in range(8):
                ev_before = T(pref + "ev")
                wtk = {}
                for jh in range(2):
                    buf = jh
                    src = wT[2048 * jh:2048 * (jh + 1), 512 * hg:512 * (hg + 1)].rearrange(
                        "(j p) c -> p j c", p=128)
                    dstw = wb[buf][:].rearrange("p (j c) -> p j c", c=512)
                    w = [("tev", tev_all)]
                    if strip_free[buf] is not None:
                        w.append((pref + "mm", strip_free[buf]))
                    if pref == "q":
                        w.append(("kmm", T("kmm")))
                    wtk[jh] = E("gpsimd", lambda e, s=src, d=dstw: e.dma_start(out=d, in_=s),
                                waits=w, inc="wld", amt=16)
                for jh in range(2):
                    for hh in range(4):
                        for j16 in range(16):
                            for half in range(2):
                                j = 16 * jh + j16
                                st = (jh == 0 and j16 == 0)
                                sp = (jh == 1 and j16 == 15)
                                w2 = []
                                if hh == 0 and j16 == 0 and half == 0:
                                    w2.append(("wld", wtk[jh]))
                                if jh == 0 and j16 == 0 and half == 0:
                                    bt = S.bank_ev.get((pref, hh))
                                    if bt:
                                        w2.append((pref + "ev", bt))
                                    if hg == 0:
                                        w2.extend(entry_guard)
                                E("tensor",
                                  lambda e, b=jh, hh=hh, j16=j16, j=j, half=half, st=st, sp=sp:
                                  e.matmul(out=pall[:, 1024 * hh + 512 * half:
                                                    1024 * hh + 512 * half + 512],
                                           lhsT=wb[b][:, 512 * j16 + 128 * hh:
                                                      512 * j16 + 128 * (hh + 1)],
                                           rhs=big[:, NT * j + 512 * half:NT * j + 512 * half + 512],
                                           start=st, stop=sp),
                                  waits=w2,
                                  inc=(pref + "mm") if (j16 == 15 and half == 1)
                                  else None)
                            if jh == 1 and j16 == 15:
                                mmt = T(pref + "mm")
                                h = 4 * hg + hh
                                evl = None
                                for half2 in range(2):
                                    slot = (hg * 8 + hh * 2 + half2) % 4
                                    wv = [(pref + "mm", mmt), ("prep", prep_all)] +                                         slotg.get(slot, [])
                                    ev = E("scalar", lambda e, hh=hh, half=half2, h=h, s=slot:
                                           e.activation(out=kst[s][:],
                                                        in_=pall[:, 1024 * hh + 512 * half:
                                                                 1024 * hh + 512 * half + 512],
                                                        func=AF.Identity,
                                                        bias=bias_t[:, h:h + 1],
                                                        scale=scale),
                                           waits=wv, inc=pref + "ev")
                                    evl = ev
                                    r0 = 512 * (h // 4) + 128 * (h % 4)
                                    stk = E("scalar", lambda e, s=slot, r0=r0, half=half2:
                                            e.dma_start(out=send.ap()[r0:r0 + 128,
                                                                      512 * half:
                                                                      512 * (half + 1)],
                                                        in_=kst[s][:]),
                                            waits=[(pref + "ev", ev)],
                                            inc=pref + "st", amt=16)
                                    slotg[slot] = [(pref + "st", stk)]
                                S.bank_ev[(pref, hh)] = evl
                    strip_free[jh] = T(pref + "mm")

        S.bank_ev = {}
        proj_phase(wkT, "k", s_bk, 1.0, k_send, [])

        proj_phase(wqT, "q", s_bqs, SCLQ, q_send, [("kev", T("kev"))])

        strip_free = {0: None, 1: None}
        vbank_rd = {}
        for g8 in range(8):
            wtk = {}
            for jh in range(2):
                src = wvT[2048 * jh:2048 * (jh + 1), 512 * g8:512 * (g8 + 1)].rearrange(
                    "(j p) c -> p j c", p=128)
                dstw = wb[jh][:].rearrange("p (j c) -> p j c", c=512)
                w = [("qmm", T("qmm"))]
                if strip_free[jh] is not None:
                    w.append(("vjh", strip_free[jh]))
                wtk[jh] = E("gpsimd", lambda e, s=src, d=dstw: e.dma_start(out=d, in_=s),
                            waits=w, inc="wld", amt=16)
            for jh in range(2):
                for t8 in range(8):
                    for j16 in range(16):
                        j = 16 * jh + j16
                        st = (jh == 0 and j16 == 0)
                        w2 = []
                        if t8 == 0 and j16 == 0:
                            w2.append(("wld", wtk[jh]))
                            if st:
                                w2.append(("qev", T("qev")))
                        if st and j16 == 0:
                            bt = vbank_rd.get(t8)
                            if bt:
                                w2.append(("vev", bt))
                        E("tensor", lambda e, b=jh, t8=t8, j16=j16, j=j, st=st:
                          e.matmul(out=PB(t8),
                                   lhsT=big[:, NT * j + 128 * t8:NT * j + 128 * (t8 + 1)],
                                   rhs=wb[b][:, 512 * j16:512 * (j16 + 1)],
                                   start=st, stop=False),
                          waits=w2,
                          inc="vjh" if (j16 == 15) else None)
                    if jh == 1:
                        bm = E("tensor", lambda e, t8=t8, g8=g8:
                               e.matmul(out=PB(t8), lhsT=s_or[:],
                                        rhs=bvb[0:1, 512 * g8:512 * (g8 + 1)],
                                        start=False, stop=True),
                               waits=[("cst", cst_all), ("brl", brl_all)], inc="vmm")
                        slot = (g8 * 8 + t8) % 4
                        wv = [("vmm", bm)] + slotg.get(slot, [])
                        ev = E("vector", lambda e, t8=t8, s=slot: e.tensor_copy(
                            out=kst[s][:], in_=PB(t8)),
                            waits=wv, inc="vev")
                        vbank_rd[t8] = ev
                        stk = E("scalar", lambda e, s=slot, g8=g8, t8=t8:
                                e.dma_start(out=v_send.ap()[1024 * g8 + 128 * t8:
                                                            1024 * g8 + 128 * (t8 + 1), :],
                                            in_=kst[s][:]),
                                waits=[("vev", ev)], inc="vst", amt=16)
                        slotg[slot] = [("vst", stk)]
                strip_free[jh] = T("vjh")
        brl_bo = E("gpsimd", lambda e: e.dma_start(out=bob, in_=bor),
                   waits=[("vmm", T("vmm"))], inc="brl", amt=16)

        cck = E("gpsimd", lambda e: e.collective_compute(
            "AllToAll", ALU.bypass, replica_groups=[list(range(8))],
            ins=[k_send.ap().opt()], outs=[kgD.ap().opt()]),
            waits=[("kst", T("kst")), ("qst", T("qst")), ("vst", T("vst")),
                   ("qmm", T("qmm")), ("vmm", T("vmm")), ("xld", T("xld")),
                   ("cst", cst_all), ("cstg", cstg_all), ("brl", T("brl")),
                   ("wld", T("wld"))], inc="cck")
        ccq = E("gpsimd", lambda e: e.collective_compute(
            "AllToAll", ALU.bypass, replica_groups=[list(range(8))],
            ins=[q_send.ap().opt()], outs=[qgD.ap().opt()]),
            waits=[("cck", cck)], inc="ccq")
        ccv = E("gpsimd", lambda e: e.collective_compute(
            "AllToAll", ALU.bypass, replica_groups=[list(range(8))],
            ins=[v_send.ap().opt()], outs=[vgD.ap().opt()]),
            waits=[("ccq", ccq)], inc="ccv")

        def kres(par):
            return big[:, 12288 * par:12288 * par + 4096]

        def vres(par):
            return big[:, 12288 * par + 4096:12288 * par + 8192]

        def qres(par):
            return big[:, 12288 * par + 8192:12288 * par + 12288]

        mqc = [mqct[:, 0:4], mqct[:, 4:8]]
        nmq = [nmqt[0:1, 0:512], nmqt[0:1, 512:1024]]
        mxps = [pall[:, 2560:2688], pall[:, 3072:3200]]
        pden = pall[0:1, 2048:2560]
        pbc = PB(4)
        sc_tok = {0: None, 1: None, 7: None}
        av_tok = {2: None, 3: None}
        attT_tok = [None] * 4
        rbc_tok = [None, None]
        aost_tok = [None] * 3
        mx_tok = [None, None]
        den_free = None
        ld_prev = {}
        sci = 0
        ui = 0
        for p in range(NP):
            b, hh = p // 4, p % 4
            par = p % 2
            w0 = [("cck", cck), ("ccq", ccq), ("ccv", ccv)]
            if ld_prev.get(par) is not None:
                w0.append(("dnm", ld_prev[par]))
            lds = 0
            for B in range(8):
                s = 4 * b + _rho(B)
                co = _co(B)
                E("sync", lambda e, s=s, co=co, hh=hh, B=B, par=par:
                  e.dma_start(out=kres(par)[:, 512 * B:512 * (B + 1)],
                              in_=kgD.ap()[512 * s + 128 * hh:512 * s + 128 * (hh + 1),
                                           co:co + 512]),
                  waits=(w0 if B == 0 else []), inc="p2ld", amt=16)
                E("sync", lambda e, s=s, co=co, hh=hh, B=B, par=par:
                  e.dma_start(out=vres(par)[:, 512 * B:512 * (B + 1)].rearrange(
                      "p (u c) -> p u c", c=128),
                      in_=vgD.ap()[1024 * s + co:1024 * s + co + 512,
                                   128 * hh:128 * (hh + 1)].rearrange("(u p) c -> p u c", p=128)),
                  inc="p2ld", amt=16)
                lds = E("sync", lambda e, s=s, co=co, hh=hh, B=B, par=par:
                        e.dma_start(out=qres(par)[:, 512 * B:512 * (B + 1)],
                                    in_=qgD.ap()[512 * s + 128 * hh:512 * s + 128 * (hh + 1),
                                                 co:co + 512]),
                        inc="p2ld", amt=16)

            def mx_minis(P, qt):
                for q4 in range(4):
                    for B in range(P + 1):
                        yield (P, q4, B)
                yield None

            def emit_mx_mini(item, qt):
                if item is None:
                    emit_mx_tail(qt)
                    return
                Pm, q4, B = item
                if True:
                    if True:
                        mi = (S.cnt.get("_mxi", 0)) % 2
                        S.cnt["_mxi"] = S.cnt.get("_mxi", 0) + 1
                        wmx = [("p2ld", lds)]
                        if mx_tok[mi] is not None:
                            wmx.append(("mxr", mx_tok[mi]))
                        mm = E("tensor", lambda e, mi=mi, P=Pm, q4=q4, B=B, par=par:
                               e.matmul(out=mxps[mi],
                                        lhsT=qres(par)[:, 512 * P + 128 * q4:
                                                       512 * P + 128 * (q4 + 1)],
                                        rhs=kres(par)[:, 512 * B:512 * B + 128],
                                        start=True, stop=True),
                               waits=wmx, inc="mxm")
                        rdt = E("vector", lambda e, mi=mi: e.reduce_max(
                            out=mqt[:, 0:1], in_=mxps[mi], axis=AXX),
                            waits=[("mxm", mm)], inc="mxr", fence=True)
                        if B == 0:
                            E("vector", lambda e, qt=qt, q4=q4: e.tensor_copy(
                                out=mqc[qt][:, q4:q4 + 1], in_=mqt[:, 0:1]),
                              inc="mqv", fence=True)
                        else:
                            E("vector", lambda e, qt=qt, q4=q4: e.tensor_max(
                                out=mqc[qt][:, q4:q4 + 1], in0=mqc[qt][:, q4:q4 + 1],
                                in1=mqt[:, 0:1]), inc="mqv", fence=True)
                        mx_tok[mi] = rdt

            def emit_mx_tail(qt):
                for q4 in range(4):
                    scb = [0, 1, 7][(S.cnt.get("_sci", 0)) % 3]
                    S.cnt["_sci"] = S.cnt.get("_sci", 0) + 1
                    wtp = [("mqv", T("mqv"))]
                    if sc_tok[scb] is not None:
                        wtp.append(sc_tok[scb])
                    tp = E("tensor", lambda e, scb=scb, qt=qt, q4=q4: e.transpose(
                        out=PBb(scb)[0:1, 0:128], in_=mqc[qt][:, q4:q4 + 1],
                        identity=s_id[:]),
                        waits=wtp, inc="mtp")
                    ng = E("vector", lambda e, scb=scb, qt=qt, q4=q4: e.tensor_scalar(
                        out=nmq[qt][0:1, 128 * q4:128 * (q4 + 1)], in0=PBb(scb)[0:1, 0:128],
                        scalar1=-1.0, scalar2=None, op0=ALU.mult),
                        waits=[("mtp", tp)], inc="nmg")
                    sc_tok[scb] = ("nmg", ng)

            def units(P, qt, nmg_tok, minis, mqt_next):
                nonlocal ui, den_free
                avb = 2 + qt
                pend = None
                for B in range(P + 1):
                    for u in range(4):
                        if minis is not None:
                            item = next(minis, "done")
                            if item != "done":
                                emit_mx_mini(item, mqt_next)
                        first = (B == 0 and u == 0)
                        lastu = (B == P and u == 3)
                        scb = [0, 1, 7][(S.cnt.get("_sci", 0)) % 3]
                        S.cnt["_sci"] = S.cnt.get("_sci", 0) + 1
                        ats = ui % 4
                        wsc = [("p2ld", lds), ("nmg", nmg_tok)]
                        if sc_tok[scb] is not None:
                            wsc.append(sc_tok[scb])
                        smm = E("tensor", lambda e, scb=scb, B=B, u=u, P=P, par=par:
                                e.matmul(out=PB(scb),
                                         lhsT=kres(par)[:, 512 * B + 128 * u:
                                                        512 * B + 128 * (u + 1)],
                                         rhs=qres(par)[:, 512 * P:512 * (P + 1)],
                                         start=True, stop=False),
                                waits=wsc, inc="smm")
                        smm = E("tensor", lambda e, scb=scb, qt=qt:
                                e.matmul(out=PB(scb), lhsT=s_or[:],
                                         rhs=nmq[qt][0:1, :], start=False, stop=True),
                                inc="smm")
                        wxp = [("smm", smm)]
                        if attT_tok[ats] is not None:
                            wxp.append(("dnm", attT_tok[ats]))
                        sxp = E("scalar", lambda e, scb=scb, a=ats:
                                e.activation(out=attT[a][:], in_=PB(scb), func=AF.Exp,
                                             bias=0.0, scale=1.0),
                                waits=wxp, inc="sxp")
                        sc_tok[scb] = ("sxp", sxp)
                        gate = ("sxp", sxp)
                        if B == P:
                            mku = E("vector", lambda e, a=ats, u=u:
                                    e.tensor_mul(out=attT[a][:], in0=attT[a][:],
                                                 in1=s_msk[:, 512 * u:512 * (u + 1)]),
                                    waits=[("sxp", sxp)], inc="msk")
                            gate = ("msk", mku)
                        def mk_avden(gate=gate, avb=avb, B=B, u=u, par=par, a=ats,
                                     first=first, lastu=lastu):
                            wav = [gate]
                            if first and av_tok[avb] is not None:
                                wav.append(("aev", av_tok[avb]))
                            E("tensor", lambda e:
                              e.matmul(out=PB(avb),
                                       lhsT=vres(par)[:, 512 * B + 128 * u:
                                                      512 * B + 128 * (u + 1)],
                                       rhs=attT[a][:], start=first, stop=lastu),
                              waits=wav, inc="avm")
                            wdn = []
                            if first and den_free is not None:
                                wdn.append(den_free)
                            dnm = E("tensor", lambda e:
                                    e.matmul(out=pden, lhsT=s_oc[:], rhs=attT[a][:],
                                             start=first, stop=lastu),
                                    waits=wdn, inc="dnm")
                            attT_tok[a] = dnm
                        if pend is not None:
                            pend()
                        pend = mk_avden
                        ui += 1
                if pend is not None:
                    pend()
                if minis is not None:
                    for item in minis:
                        emit_mx_mini(item, mqt_next)
                dl1 = E("scalar", lambda e: e.activation(
                    out=r_t[:], in_=pden, func=AF.Ln, bias=0.0, scale=1.0),
                    waits=[("dnm", T("dnm"))], inc="dnl")
                rd = E("scalar", lambda e, qt=qt: e.activation(out=r_rd[qt][:], in_=r_t[:],
                                                               func=AF.Exp, bias=0.0,
                                                               scale=-1.0),
                       inc="rdn", fence=True)
                bcm = E("tensor", lambda e: e.matmul(out=pbc, lhsT=s_orf[:],
                                                     rhs=r_rd[qt][:],
                                                     start=True, stop=True),
                        waits=[("rdn", rd), ("dnl", dl1)], inc="bcm")
                wcc = [("bcm", bcm)]
                if rbc_tok[qt] is not None:
                    wcc.append(("aev", rbc_tok[qt]))
                bcc = E("vector", lambda e, qt=qt: e.tensor_copy(out=rbc[qt][:], in_=pbc),
                        waits=wcc, inc="bcc")
                den_free = ("bcc", bcc)
                aslot = (p * 8 + P) % 3
                wev = [("bcc", bcc)]
                if aost_tok[aslot] is not None:
                    wev.append(("aos", aost_tok[aslot]))
                aev = E("vector", lambda e, avb=avb, qt=qt, s=aslot:
                        e.tensor_tensor(out=aost[s][:], in0=PB(avb), in1=rbc[qt][:],
                                        op=ALU.mult),
                        waits=wev, inc="aev")
                av_tok[avb] = aev
                rbc_tok[qt] = aev
                d = 4 * b + _rho(P)
                co = _co(P)
                aost_tok[aslot] = E("scalar", lambda e, s=aslot, d=d, co=co, hh=hh:
                                    e.dma_start(out=ao_send.ap()[
                                        512 * d + 128 * hh:512 * d + 128 * (hh + 1),
                                        co:co + 512], in_=aost[s][:]),
                                    waits=[("aev", aev)], inc="aos", amt=16)

            for item in mx_minis(0, 0):
                emit_mx_mini(item, 0)
            nmg_tok = T("nmg")
            for P in range(8):
                minis = mx_minis(P + 1, (P + 1) % 2) if P < 7 else None
                units(P, P % 2, nmg_tok, minis, (P + 1) % 2)
                nmg_tok = T("nmg")
            ld_prev[par] = T("dnm")

        cca = E("gpsimd", lambda e: e.collective_compute(
            "AllToAll", ALU.bypass, replica_groups=[list(range(8))],
            ins=[ao_send.ap().opt()], outs=[aogD.ap().opt()]),
            waits=[("aos", T("aos")), ("p2ld", T("p2ld")), ("kst", T("kst")),
                   ("qst", T("qst")), ("vst", T("vst"))], inc="cca")

        agl = E("sync", lambda e: e.dma_start(
            out=big[:].rearrange("p (j c) -> p j c", c=NT),
            in_=aogD.ap().rearrange("(j p) c -> p j c", p=128)),
            waits=[("cca", cca), ("dnm", T("dnm")), ("smm", T("smm"))], inc="agl", amt=16)
        strip_free = {0: None, 1: None}
        wo_bank_rd = {}
        xres_free = {}
        x1_free = {}
        for g8 in range(8):
            wtk = {}
            for jh in range(2):
                src = woT[2048 * jh:2048 * (jh + 1), 512 * g8:512 * (g8 + 1)].rearrange(
                    "(j p) c -> p j c", p=128)
                dstw = wb[jh][:].rearrange("p (j c) -> p j c", c=512)
                w = [("smm", T("smm")), ("cca", cca)]
                if strip_free[jh] is not None:
                    w.append(("ojh", strip_free[jh]))
                wtk[jh] = E("gpsimd", lambda e, s=src, d=dstw: e.dma_start(out=d, in_=s),
                            waits=w, inc="wld", amt=16)
            for jh in range(2):
                for t8 in range(8):
                    for j16 in range(16):
                        j = 16 * jh + j16
                        st = (jh == 0 and j16 == 0)
                        w2 = []
                        if t8 == 0 and j16 == 0:
                            w2.append(("wld", wtk[jh]))
                            w2.append(("agl", agl))
                            if st and g8 == 0:
                                w2.append(("aev", T("aev")))
                                w2.append(("bcc", T("bcc")))
                                w2.append(("dnl", T("dnl")))
                        if st and j16 == 0:
                            bt = wo_bank_rd.get(t8)
                            if bt:
                                w2.append(("x1e", bt))
                        E("tensor", lambda e, b=jh, t8=t8, j16=j16, j=j, st=st:
                          e.matmul(out=PB(t8),
                                   lhsT=big[:, NT * j + 128 * t8:NT * j + 128 * (t8 + 1)],
                                   rhs=wb[b][:, 512 * j16:512 * (j16 + 1)],
                                   start=st, stop=False),
                          waits=w2,
                          inc="ojh" if (j16 == 15) else None)
                    if jh == 1:
                        bm = E("tensor", lambda e, t8=t8, g8=g8:
                               e.matmul(out=PB(t8), lhsT=s_or[:],
                                        rhs=bob[0:1, 512 * g8:512 * (g8 + 1)],
                                        start=False, stop=True),
                               waits=[("brl", brl_bo)], inc="omm")
                        slot = (g8 * 8 + t8) % 2
                        wx = []
                        if xres_free.get(slot) is not None:
                            wx.append(("x1e", xres_free[slot]))
                        xl = E("sync", lambda e, s=slot, t8=t8, g8=g8:
                               e.dma_start(out=xres[s][:],
                                           in_=xs[128 * t8:128 * (t8 + 1),
                                                  512 * g8:512 * (g8 + 1)]),
                               waits=wx, inc="xrl", amt=16)
                        w1 = [("omm", bm), ("xrl", xl)]
                        if x1_free.get(slot) is not None:
                            w1.append(("x1s", x1_free[slot]))
                        x1e = E("vector", lambda e, t8=t8, s=slot:
                                e.tensor_add(out=x1st[s][:], in0=PB(t8), in1=xres[s][:]),
                                waits=w1, inc="x1e")
                        xres_free[slot] = x1e
                        wo_bank_rd[t8] = x1e
                        x1_free[slot] = E("scalar", lambda e, s=slot, t8=t8, g8=g8:
                                          e.dma_start(out=x1d.ap()[128 * t8:128 * (t8 + 1),
                                                                   512 * g8:512 * (g8 + 1)],
                                                      in_=x1st[s][:]),
                                          waits=[("x1e", x1e)], inc="x1s", amt=16)
                strip_free[jh] = T("ojh")

        btl = E("gpsimd", lambda e: e.dma_start(out=wb[0][0:64, 0:4096], in_=bTm),
                waits=[("ojh", T("ojh")), ("omm", T("omm"))], inc="btl", amt=16)
        bTb = wb[0][0:64, 0:4096]
        xn2f = wb[1][:, 0:4096]
        tr_bank = {5: None, 6: None}
        pml_rd = None
        pmlt_rd = None
        pmgT_rd = None
        pd_rd = {3: None, 4: None}
        ost_free = [None] * 2
        xa_free = {0: None, 1: None}
        lnx2_tok = {}
        for t8 in range(8):
            par4 = t8 % 2
            mltv = mlt[:, 68 * par4:68 * (par4 + 1)]
            mgv = mg[:, 64 * par4:64 * (par4 + 1)]
            mgTv = mgT[:, 128 * par4:128 * (par4 + 1)]
            glv = gl[:, 28 * par4:28 * (par4 + 1)]
            L = lambda ee, m=mltv: m[:, 64 + ee:65 + ee]
            wxl = [("x1s", T("x1s"))]
            if lnx2_tok.get(t8 % 2) is not None:
                wxl.append(("lnx", lnx2_tok[t8 % 2]))
            if xa_free[t8 % 2] is not None:
                wxl.append(("ooe", xa_free[t8 % 2]))
            xl = E("sync", lambda e, t=t8: e.dma_start(out=xa[t % 2][:],
                                                       in_=x1d.ap()[128 * t:128 * (t + 1), :]),
                   waits=wxl, inc="xld", amt=16)
            a_x = ln_tile(xa[t8 % 2], xn[t8 % 2], t8 % 2, [("xld", xl)])
            lnx2_tok[t8 % 2] = a_x
            for g in range(4):
                bank = 5 + ((t8 * 4 + g) % 2)
                w = [("lnx", a_x), ("ojh", T("ojh"))]
                if t8 == 0 and g < 2:
                    w.append(("x1e", T("x1e")))
                if tr_bank[bank] is not None:
                    w.append(("tev", tr_bank[bank]))
                last = None
                for j2 in range(8):
                    j = 8 * g + j2
                    last = E("tensor", lambda e, b=bank, j2=j2, j=j, t=t8: e.transpose(
                        out=PBb(b)[:, 128 * j2:128 * (j2 + 1)],
                        in_=xn[t % 2][:, 128 * j:128 * (j + 1)], identity=s_id[:]),
                        waits=(w if j2 == 0 else []), inc=("ttr" if j2 == 7 else None))
                tr_bank[bank] = E("vector", lambda e, b=bank, g=g: e.tensor_copy(
                    out=xn2f[:, 1024 * g:1024 * (g + 1)], in_=PBb(b)[:]),
                    waits=[("ttr", last), ("ojh", T("ojh"))], inc="tev")
            tevt = T("tev")
            for j in range(32):
                w2 = []
                if j == 0:
                    w2 = [("tev", tevt), ("cst", cst_all), ("cstg", cstg_all)]
                    if pml_rd is not None:
                        w2.append(("mle", pml_rd))
                    if t8 == 0:
                        w2.append(("x1e", T("x1e")))
                E("tensor", lambda e, j=j: e.matmul(out=pall[0:68, 0:128],
                                                    lhsT=rhswb[:, 68 * j:68 * (j + 1)],
                                                    rhs=xn2f[:, 128 * j:128 * (j + 1)],
                                                    start=(j == 0), stop=(j == 31)),
                  waits=w2, inc=("mlm" if j == 31 else None))
            pml_rd = E("vector", lambda e: e.tensor_copy(out=sml[0:68, :], in_=pall[0:68, 0:128]),
                       waits=[("mlm", T("mlm"))], inc="mle")
            w3 = [("mle", pml_rd)]
            if pmlt_rd is not None:
                w3.append(("mlte", pmlt_rd))
            if t8 == 0:
                w3.append(("x1e", T("x1e")))
            trm = E("tensor", lambda e: e.transpose(out=PBb(1)[:, 0:68], in_=sml[0:68, :],
                                                    identity=s_id[0:68, 0:68]),
                    waits=w3, inc="mltm")
            pmlt_rd = E("vector", lambda e: e.tensor_copy(out=mltv, in_=PBb(1)[:, 0:68]),
                        waits=[("mltm", trm)], inc="mlte")
            fw = [("mlte", pmlt_rd)]
            for e_i in range(4):
                terms = [(a_i, ALU.is_ge if a_i < e_i else ALU.is_gt)
                         for a_i in range(4) if a_i != e_i]
                for k_i, (a_i, op) in enumerate(terms):
                    if k_i == 0:
                        E("vector", lambda e, a=a_i, ei=e_i, op=op: e.tensor_tensor(
                            out=glv[:, ei:ei + 1], in0=L(a), in1=L(ei), op=op), waits=fw,
                          fence=True)
                        fw = []
                    else:
                        E("vector", lambda e, a=a_i, ei=e_i, op=op: e.tensor_tensor(
                            out=glv[:, 4:5], in0=L(a), in1=L(ei), op=op), fence=True)
                        E("vector", lambda e, ei=e_i: e.tensor_add(
                            out=glv[:, ei:ei + 1], in0=glv[:, ei:ei + 1], in1=glv[:, 4:5]),
                          fence=True)
            E("vector", lambda e: e.tensor_max(out=glv[:, 5:6], in0=L(0), in1=L(1)), fence=True)
            E("vector", lambda e: e.tensor_max(out=glv[:, 5:6], in0=glv[:, 5:6], in1=L(2)),
              fence=True)
            E("vector", lambda e: e.tensor_max(out=glv[:, 5:6], in0=glv[:, 5:6], in1=L(3)),
              fence=True)
            vnm = E("vector", lambda e: e.tensor_scalar(out=glv[:, 6:7], in0=glv[:, 5:6],
                                                        scalar1=-1.0, scalar2=None,
                                                        op0=ALU.mult), inc="gnm", fence=True)
            for e_i in range(4):
                E("scalar", lambda e, ei=e_i: e.activation(out=glv[:, 7 + ei:8 + ei], in_=L(ei),
                                                           func=AF.Exp, bias=glv[:, 6:7],
                                                           scale=1.0),
                  waits=([("gnm", vnm)] if e_i == 0 else []),
                  inc=("gee" if e_i == 3 else None))
            gee = T("gee")
            for e_i in range(4):
                E("vector", lambda e, ei=e_i: e.tensor_single_scalar(
                    out=glv[:, 11 + ei:12 + ei], in_=glv[:, ei:ei + 1], scalar=1.5, op=ALU.is_le),
                  waits=([("gee", gee)] if e_i == 0 else []), fence=True)
            for e_i in range(4):
                E("vector", lambda e, ei=e_i: e.tensor_mul(out=glv[:, 15 + ei:16 + ei],
                                                           in0=glv[:, 7 + ei:8 + ei],
                                                           in1=glv[:, 11 + ei:12 + ei]),
                  fence=True)
            E("vector", lambda e: e.tensor_add(out=glv[:, 19:20], in0=glv[:, 15:16],
                                               in1=glv[:, 16:17]), fence=True)
            E("vector", lambda e: e.tensor_add(out=glv[:, 19:20], in0=glv[:, 19:20],
                                               in1=glv[:, 17:18]), fence=True)
            vdn = E("vector", lambda e: e.tensor_add(out=glv[:, 19:20], in0=glv[:, 19:20],
                                                     in1=glv[:, 18:19]), inc="gdn", fence=True)
            E("scalar", lambda e: e.activation(out=glv[:, 20:21], in_=glv[:, 19:20],
                                               func=AF.Ln, bias=0.0, scale=1.0),
              waits=[("gdn", vdn)])
            grg = E("scalar", lambda e: e.activation(out=glv[:, 20:21], in_=glv[:, 20:21],
                                                     func=AF.Exp, bias=0.0, scale=-1.0),
                    inc="grg", fence=True)
            for e_i in range(4):
                E("vector", lambda e, ei=e_i: e.tensor_scalar(
                    out=glv[:, 21 + ei:22 + ei], in0=glv[:, 15 + ei:16 + ei],
                    scalar1=glv[:, 20:21], scalar2=4.0, op0=ALU.mult, op1=ALU.mult),
                  waits=([("grg", grg)] if e_i == 0 else []), fence=True)
            gms = None
            for e_i in range(4):
                gms = E("vector", lambda e, ei=e_i: e.tensor_scalar(
                    out=mgv[:, 16 * ei:16 * (ei + 1)], in0=mltv[:, 16 * ei:16 * (ei + 1)],
                    scalar1=glv[:, 21 + ei:22 + ei], scalar2=None, op0=ALU.mult),
                    inc=("gms" if e_i == 3 else None), fence=True)
            gms = T("gms")
            w4 = [("gms", gms)]
            if pmgT_rd is not None:
                w4.append(("mge", pmgT_rd))
            if t8 == 0:
                w4.append(("x1e", T("x1e")))
            E("tensor", lambda e: e.transpose(out=PBb(2)[0:64, 0:128], in_=mgv,
                                              identity=s_id[:]),
              waits=w4, inc="mgm")
            pmgT_rd = E("vector", lambda e: e.tensor_copy(out=mgTv, in_=PBb(2)[0:64, 0:128]),
                        waits=[("mgm", T("mgm"))], inc="mge")
            mge = pmgT_rd
            last_ooe = None
            for g8 in range(8):
                bank = 3 + (g8 % 2)
                wd = [("mge", mge), ("btl", btl)]
                if pd_rd[bank] is not None:
                    wd.append(("ooe", pd_rd[bank]))
                if t8 == 0 and g8 < 2:
                    wd.append(("x1e", T("x1e")))
                dm = E("tensor", lambda e, bank=bank, g8=g8: e.matmul(
                    out=PB(bank), lhsT=mgTv[0:64, :], rhs=bTb[0:64, 512 * g8:512 * (g8 + 1)],
                    start=True, stop=True), waits=wd, inc="dm")
                oslot = (t8 * 8 + g8) % 2
                wo_ = [("dm", dm)]
                if ost_free[oslot] is not None:
                    wo_.append(("outs", ost_free[oslot]))
                ooe = E("vector", lambda e, bank=bank, s=oslot, t=t8, g8=g8: e.tensor_add(
                    out=ost[s][:], in0=PB(bank), in1=xa[t % 2][:, 512 * g8:512 * (g8 + 1)]),
                    waits=wo_, inc="ooe")
                pd_rd[bank] = ooe
                last_ooe = ooe
                ost_free[oslot] = E("scalar", lambda e, s=oslot, t=t8, g8=g8: e.dma_start(
                    out=out[128 * t:128 * (t + 1), 512 * g8:512 * (g8 + 1)], in_=ost[s][:]),
                    waits=[("ooe", ooe)], inc="outs", amt=16)
            xa_free[t8 % 2] = last_ooe

        if taps:
            E("gpsimd", lambda e: e.dma_start(out=tap["ksend"].ap(), in_=k_send.ap()),
              waits=[("kst", T("kst"))], inc="tapd", amt=16)
            E("gpsimd", lambda e: e.dma_start(out=tap["qsend"].ap(), in_=q_send.ap()),
              waits=[("qst", T("qst")), ("mxst", T("mxst"))], inc="tapd", amt=16)
            E("gpsimd", lambda e: e.dma_start(out=tap["vsend"].ap(), in_=v_send.ap()),
              waits=[("vst", T("vst"))], inc="tapd", amt=16)
            E("gpsimd", lambda e: e.dma_start(out=tap["aosend"].ap(), in_=ao_send.ap()),
              waits=[("aos", T("aos"))], inc="tapd", amt=16)
            E("gpsimd", lambda e: e.dma_start(out=tap["aog"].ap(), in_=aogD.ap()),
              waits=[("cca", cca)], inc="tapd", amt=16)
            E("gpsimd", lambda e: e.dma_start(out=tap["x1"].ap(), in_=x1d.ap()),
              waits=[("x1s", T("x1s"))], inc="tapd", amt=16)
            E("gpsimd", lambda e: e.dma_start(out=tap["kg"].ap(), in_=kgD.ap()),
              waits=[("cck", cck)], inc="tapd", amt=16)
            E("gpsimd", lambda e: e.dma_start(out=tap["qg"].ap(), in_=qgD.ap()),
              waits=[("ccq", ccq)], inc="tapd", amt=16)
            E("gpsimd", lambda e: e.dma_start(out=tap["vg"].ap(), in_=vgD.ap()),
              waits=[("ccv", ccv)], inc="tapd", amt=16)
            S.wait_only("gpsimd", [("tapd", T("tapd"))])

        S.wait_only("scalar", [("outs", T("outs"))])

        for chname in S.cnt:
            sems[chname] = ctx.enter_context(nc.semaphore(chname))
        with allow_ici_all_to_all():
            with nc.Block() as block:
                S.replay(block, sems)
    return nc


def _prep_inputs(L0_attn_x, L0_attn_Wq, L0_attn_bq, L0_attn_Wk, L0_attn_bk,
                 L0_attn_Wv, L0_attn_bv, L0_attn_Wo, L0_attn_bo,
                 L0_ffn_A, L0_ffn_B, L0_ffn_Wr):
    x = np.asarray(L0_attn_x, dtype=np.float32)
    C = lambda a: np.ascontiguousarray(np.asarray(a, dtype=np.float32))
    A = np.asarray(L0_ffn_A, dtype=np.float32)
    B = np.asarray(L0_ffn_B, dtype=np.float32)
    Wr = np.asarray(L0_ffn_Wr, dtype=np.float32)
    aT = A.transpose(2, 0, 1).reshape(D, 64)
    wrT = np.asarray(Wr).T
    rhsw = np.concatenate([aT.reshape(32, 128, 64),
                           np.ascontiguousarray(wrT).reshape(32, 128, 4)], axis=2)
    shared = dict(
        wqT=C(np.asarray(L0_attn_Wq).T), wkT=C(np.asarray(L0_attn_Wk).T),
        wvT=C(np.asarray(L0_attn_Wv).T), woT=C(np.asarray(L0_attn_Wo).T),
        bqc=C(np.asarray(L0_attn_bq).reshape(32, 128).T),
        bkc=C(np.asarray(L0_attn_bk).reshape(32, 128).T),
        bvr=C(np.asarray(L0_attn_bv).reshape(1, D)),
        bor=C(np.asarray(L0_attn_bo).reshape(1, D)),
        rhsw=C(rhsw.transpose(1, 0, 2).reshape(128, 32 * 68)),
        bTm=C(B.transpose(0, 2, 1).reshape(64, D)))
    in_maps = []
    for c in range(8):
        g, qc = c // 4, c % 4
        xsl = np.concatenate([x[g, 512 * qc:512 * (qc + 1)],
                              x[g, 512 * (7 - qc):512 * (8 - qc)]], axis=0)
        m = dict(shared)
        m["xs"] = np.ascontiguousarray(xsl)
        in_maps.append(m)
    return in_maps


def kernel(**inputs):
    in_maps = _prep_inputs(**inputs)
    with allow_ici_all_to_all():
        nc = build_nc(taps=_TAPS)
        res = run_bass_kernel_spmd(nc, in_maps, core_ids=list(range(8)))
    outf = np.zeros((2, D, D), dtype=np.float32)
    for c in range(8):
        g, qc = c // 4, c % 4
        o = res.results[c]["out"]
        outf[g, 512 * qc:512 * (qc + 1)] = o[0:512]
        outf[g, 512 * (7 - qc):512 * (8 - qc)] = o[512:1024]
    if _TAPS:
        kernel._tap_results = res.results
    return outf
